# version 02
# baseline (speedup 1.0000x reference)
import contextlib
import numpy as np
import ml_dtypes
import concourse.bass as bass
import concourse.mybir as mybir
from concourse.bass_utils import run_bass_kernel_spmd

F32 = mybir.dt.float32
BF16 = mybir.dt.bfloat16
AF = mybir.ActivationFunctionType
ALU = mybir.AluOpType
bf16 = ml_dtypes.bfloat16

D_MODEL = 2048
D_INNER = 4096
D_STATE = 128
NHEADS = 64
HEADDIM = 64
CONV_DIM = D_INNER + 2 * D_STATE
D_IN_PROJ = 2 * D_INNER + 2 * D_STATE + NHEADS
INTER = 8192
TOK = 2048
HALO = 3
W_A = TOK + HALO
CHUNK = 128
NC = TOK // CHUNK
NJ = D_MODEL // 128
NI = INTER // 128
NZ = D_INNER // 128
NDC = CONV_DIM // 128
NO_IN = 67
MASK_NEG = -30000.0
PUBN = 128 * D_INNER + NHEADS

TSUBS_A = [(0, 512), (512, 512), (1024, 512), (1536, 256), (1792, 259)]
TSUBS_F = [(0, 512), (512, 512), (1024, 512), (1536, 512)]

DBG = []
STOP_AFTER = None


class Sem:
    def __init__(self, nc, es, name):
        self.h = es.enter_context(nc.semaphore(name))
        self.v = 0


def inc(instr, sem, n=1):
    instr.then_inc(sem.h, n)
    sem.v += n
    return instr


class SPair:

    def __init__(self, nc, es, name):
        self.s = [Sem(nc, es, name + "a"), Sem(nc, es, name + "b")]

    def base(self):
        return (self.s[0].v, self.s[1].v)


def sp_inc(instr, sp, g, n=16):
    inc(instr, sp.s[g % 2], n)


def sp_wait(eng, sp, b, g, n=16):
    eng.wait_ge(sp.s[g % 2].h, b[g % 2] + n * (g // 2 + 1))


class Ctx:
    pass


def build_program():
    nc = bass.Bass("TRN2", target_bir_lowering=False, debug=False)
    ctx = Ctx()
    ctx.nc = nc
    es = contextlib.ExitStack()
    ctx.es = es
    ap = lambda *a, **k: nc.dram_tensor(*a, **k).ap()

    ctx.u = ap("u", [D_MODEL, W_A], F32, kind="ExternalInput")
    ctx.wgu = ap("wgu", [D_MODEL, 2 * INTER], F32, kind="ExternalInput")
    ctx.wdn = ap("wdn", [INTER, D_MODEL], F32, kind="ExternalInput")
    ctx.win = ap("win", [D_MODEL, D_IN_PROJ], F32, kind="ExternalInput")
    ctx.wout = ap("wout", [D_INNER, D_MODEL], F32, kind="ExternalInput")
    ctx.wconv = ap("wconv", [128, NDC * 4], F32, kind="ExternalInput")
    ctx.bconv = ap("bconv", [128, NDC], F32, kind="ExternalInput")
    ctx.dtb = ap("dtb", [NHEADS, 1], F32, kind="ExternalInput")
    ctx.alog = ap("alog", [NHEADS, 1], F32, kind="ExternalInput")
    ctx.dprow = ap("dprow", [1, NHEADS], F32, kind="ExternalInput")
    ctx.nwin = ap("nwin", [128, NZ], F32, kind="ExternalInput")
    ctx.btwT = ap("btwT", [8, 8], F32, kind="ExternalInput")
    ctx.predc = ap("predc", [8, 1], F32, kind="ExternalInput")
    ctx.o = ap("o", [D_MODEL, TOK], F32, kind="ExternalOutput")
    ctx.h_scr = ap("h_scr", [INTER, W_A], BF16)
    ctx.x1_scr = ap("x1_scr", [D_MODEL, W_A], F32)
    ctx.z_scr = ap("z_scr", [D_INNER, TOK], BF16)
    ctx.xbcp_scr = ap("xbcp_scr", [CONV_DIM, W_A], BF16)
    ctx.xbc_scr = ap("xbc_scr", [CONV_DIM, TOK], BF16)
    ctx.sel_scr = ap("sel_scr", [NC * 128, D_INNER], F32)
    ctx.y_scr = ap("y_scr", [D_INNER, TOK], BF16)
    ctx.x2_scr = ap("x2_scr", [D_MODEL, TOK], F32)
    ctx.cums_scr = ap("cums_scr", [NHEADS, TOK], F32)
    ctx.rows_scr = ap("rows_scr", [64, NHEADS], F32)
    ctx.pub_scr = ap("pub_scr", [1, PUBN], F32)
    ctx.gat_scr = ap("gat_scr", [8, PUBN], F32)

    maskb_np = np.where(np.arange(CHUNK)[None, :] >= np.arange(CHUNK)[:, None],
                        0.0, MASK_NEG).astype(np.float32)
    ctx.maskb_c = nc.inline_tensor(maskb_np, name="c_maskb").ap()
    ctx.onesrow_c = nc.inline_tensor(np.ones((1, 128), np.float32), name="c_onesrow").ap()
    ctx.onescolbf_c = nc.inline_tensor(np.ones((128, 1), bf16), name="c_onescolbf").ap()
    ctx.idf32_c = nc.inline_tensor(np.eye(128, dtype=np.float32), name="c_idf32").ap()
    segm = np.ones((NHEADS, TOK), np.float32)
    segm[:, ::CHUNK] = 0.0
    ctx.segmask_c = nc.inline_tensor(segm, name="c_segmask").ap()
    ctx.zcol_c = nc.inline_tensor(np.zeros((NHEADS, 1), np.float32), name="c_zcol").ap()
    halves = np.zeros((2, 128), np.float32)
    halves[0, 0:64] = 1.0
    halves[1, 64:128] = 1.0
    ctx.halves_c = nc.inline_tensor(halves, name="c_halves").ap()

    sb = lambda name, shape, dt: es.enter_context(nc.sbuf_tensor(name, shape, dt))
    ctx.dtraw = sb("g_dtraw", [NHEADS, W_A], F32)
    ctx.Ab = sb("g_Ab", [NHEADS, TOK], F32)
    ctx.cumA = sb("g_cumA", [NHEADS, TOK], F32)
    ctx.tots = sb("g_tots", [NHEADS, NC], F32)
    ctx.ptot = sb("g_ptot", [NHEADS, NC + 1], F32)
    ctx.dp_pair = sb("g_dp_pair", [128, NHEADS // 2], F32)
    ctx.maskb = sb("g_maskb", [CHUNK, CHUNK], F32)
    ctx.onesrow = sb("g_onesrow", [1, 128], F32)
    ctx.onescolbf = sb("g_onescolbf", [128, 1], BF16)
    ctx.idf32 = sb("g_idf32", [128, 128], F32)
    ctx.segmask = sb("g_segmask", [NHEADS, TOK], F32)
    ctx.cols = sb("g_cols", [NHEADS, 4], F32)
    ctx.nw = sb("g_nw", [128, NZ], F32)
    ctx.dprow_sb = sb("g_dprow", [1, NHEADS], F32)
    ctx.dppk_sb = sb("g_dppk", [2, NHEADS // 2], F32)
    ctx.halves_sb = sb("g_halves", [2, 128], F32)

    ctx.ps = [es.enter_context(nc.psum_tensor(f"ps{i}", [128, 512], F32)) for i in range(8)]

    for s in ("dC", "dH", "PE", "VE", "SE", "PO", "CC", "BAR"):
        setattr(ctx, s, Sem(nc, es, "s_" + s))
    ctx.P_LD = SPair(nc, es, "p_ld")
    ctx.P_W = SPair(nc, es, "p_w")
    ctx.P_ST = SPair(nc, es, "p_st")
    ctx.P_X = SPair(nc, es, "p_x")
    ctx.P_PS = SPair(nc, es, "p_ps")
    ctx.bar_n = 0

    phases = [
        ("init", lambda: phase_init(ctx)),
        ("ln1", lambda: phase_ln(ctx, src=ctx.u, width=W_A, name="ln1")),
        ("gu1", lambda: phase_gateup(ctx, width=W_A, name="gu1")),
        ("dn1", lambda: phase_down(ctx, res_src=ctx.u, dst=ctx.x1_scr, width=W_A,
                                   name="dn1")),
        ("ln2", lambda: phase_ln(ctx, src=ctx.x1_scr, width=W_A, name="ln2")),
        ("inproj", lambda: phase_inproj(ctx)),
        ("dt", lambda: phase_dt(ctx)),
        ("conv", lambda: phase_conv(ctx)),
        ("d1", lambda: phase_d1(ctx)),
        ("exchange", lambda: phase_exchange(ctx)),
        ("d2", lambda: phase_d2(ctx)),
        ("outproj", lambda: phase_outproj(ctx)),
        ("ln3", lambda: phase_ln(ctx, src=ctx.x2_scr, width=TOK, name="ln3")),
        ("gu2", lambda: phase_gateup(ctx, width=TOK, name="gu2")),
        ("dn2", lambda: phase_down(ctx, res_src=ctx.x2_scr, dst=ctx.o, width=TOK,
                                   name="dn2")),
    ]
    for pname, fn in phases:
        fn()
        if STOP_AFTER == pname:
            break
    if "xln" in DBG and ctx.xln is not None:
        w = ctx.xln_w
        xs = nc.dram_tensor("dbg_xln", [128, NJ * w], BF16, kind="ExternalOutput").ap()
        b_dL = ctx.dC.v
        with nc.Block("dumpx") as blkx:
            @blkx.sync
            def _(s):
                wb(s, ctx)
                inc(s.dma_start(out=xs, in_=ctx.xln[:]), ctx.dC, 16)
                s.wait_ge(ctx.dC.h, b_dL + 16)
    if [d for d in DBG if d != "xln"]:
        phase_dbg(ctx)
    if getattr(ctx, "xln", None) is not None:
        ctx.xln_scope.close()
    if getattr(ctx, "S0", None) is not None and not getattr(ctx, "_s0_closed", False):
        try:
            ctx.S0_scope.close()
        except Exception:
            pass
    es.close()
    return nc


def endbar(ctx, blk, waits=None):
    sems = [ctx.dC, ctx.dH, ctx.PE, ctx.VE, ctx.SE, ctx.PO, ctx.CC]
    for p in (ctx.P_LD, ctx.P_W, ctx.P_ST, ctx.P_X, ctx.P_PS):
        sems += p.s
    targets = [(s.h, s.v) for s in sems]
    ctx.bar_n += 5
    for name in ("sync", "gpsimd", "tensor", "scalar", "vector"):
        def body(eng):
            for h, v in targets:
                if v:
                    eng.wait_ge(h, v)
            eng.sem_inc(ctx.BAR.h, 1)
        getattr(blk, name)(body)
    ctx.BAR.v += 5


def wb(eng, ctx):
    if ctx.bar_n:
        eng.wait_ge(ctx.BAR.h, ctx.bar_n)


def phase_init(ctx):
    nc = ctx.nc
    dC, PE, VE, SE = ctx.dC, ctx.PE, ctx.VE, ctx.SE
    b_dC, b_PE, b_VE, b_SE = dC.v, PE.v, VE.v, SE.v
    with nc.Block("init") as blk:
        @blk.sync
        def _(s):
            inc(s.dma_start(out=ctx.maskb[:], in_=ctx.maskb_c), dC, 16)
            inc(s.dma_start(out=ctx.onesrow[:], in_=ctx.onesrow_c), dC, 16)
            inc(s.dma_start(out=ctx.onescolbf[:], in_=ctx.onescolbf_c), dC, 16)
            inc(s.dma_start(out=ctx.idf32[:], in_=ctx.idf32_c), dC, 16)
            inc(s.dma_start(out=ctx.segmask[:], in_=ctx.segmask_c), dC, 16)
            inc(s.dma_start(out=ctx.cols[:, 0:1], in_=ctx.dtb), dC, 16)
            inc(s.dma_start(out=ctx.cols[:, 2:3], in_=ctx.alog), dC, 16)
            inc(s.dma_start(out=ctx.cols[:, 3:4], in_=ctx.zcol_c), dC, 16)
            inc(s.dma_start(out=ctx.nw[:], in_=ctx.nwin), dC, 16)
            inc(s.dma_start(out=ctx.dprow_sb[:], in_=ctx.dprow), dC, 16)
            inc(s.dma_start(out=ctx.halves_sb[:], in_=ctx.halves_c), dC, 16)
            inc(s.dma_start(out=ctx.dppk_sb[:],
                            in_=ctx.dprow.rearrange("u (m par) -> (u par) m", par=2),
                            allow_slow_non_contiguous=True), dC, 16)
        NL = 12

        @blk.scalar
        def _(sc):
            sc.wait_ge(dC.h, b_dC + 16 * NL)
            inc(sc.activation(out=ctx.cols[:, 1:2], in_=ctx.cols[:, 2:3], func=AF.Exp),
                SE, 1)

        @blk.tensor
        def _(t):
            t.wait_ge(dC.h, b_dC + 16 * NL)
            inc(t.matmul(ctx.ps[0][:, 0:NHEADS // 2], ctx.halves_sb[:], ctx.dppk_sb[:],
                         start=True, stop=True), PE, 1)

        @blk.vector
        def _(v):
            v.wait_ge(SE.h, b_SE + 1)
            v.tensor_scalar_mul(out=ctx.cols[:, 1:2], in0=ctx.cols[:, 1:2], scalar1=-1.0)
            v.wait_ge(PE.h, b_PE + 1)
            inc(v.tensor_copy(out=ctx.dp_pair[:], in_=ctx.ps[0][:, 0:NHEADS // 2]),
                VE, 1)

        endbar(ctx, blk)


def phase_ln(ctx, src, width, name):
    nc = ctx.nc
    tsubs = TSUBS_A if width == W_A else TSUBS_F
    NT = len(tsubs)
    P_LD, PE, VE, SE = ctx.P_LD, ctx.PE, ctx.VE, ctx.SE
    b_LD, b_PE, b_VE, b_SE = P_LD.base(), PE.v, VE.v, SE.v

    ctx.xln_scope = contextlib.ExitStack()
    ctx.xln = ctx.xln_scope.enter_context(
        nc.sbuf_tensor(f"{name}_xln", [128, NJ * width], BF16))
    ctx.xln_w = width

    with contextlib.ExitStack() as psx:
        ubuf = psx.enter_context(nc.sbuf_tensor(f"{name}_ubuf", [128, NJ * 512], F32))
        xb = [psx.enter_context(nc.sbuf_tensor(f"{name}_xb{i}", [128, 512], BF16)) for i in range(2)]
        sq = [psx.enter_context(nc.sbuf_tensor(f"{name}_sq{i}", [128, 512], BF16)) for i in range(2)]
        rows = psx.enter_context(nc.sbuf_tensor(f"{name}_rows", [1, 1536], F32))
        blk = psx.enter_context(nc.Block(name))

        @blk.sync
        def _(s):
            wb(s, ctx)
            for ti, (t0, tw) in enumerate(tsubs):
                if ti >= 1:
                    s.wait_ge(VE.h, b_VE + ti * 5)
                sp_inc(s.dma_start(
                    out=ubuf[:].rearrange("p (j t) -> p j t", j=NJ)[:, :, 0:tw],
                    in_=src[:, t0:t0 + tw].rearrange("(j p) t -> p j t", p=128)),
                    P_LD, ti)

        @blk.scalar
        def _(sc):
            wb(sc, ctx)
            for ti, (t0, tw) in enumerate(tsubs):
                sp_wait(sc, P_LD, b_LD, ti)
                for j in range(NJ):
                    bb = j % 2
                    gi = ti * NJ + j
                    if gi >= 2:
                        sc.wait_ge(PE.h, b_PE + (gi - 2) // NJ * 17 + (gi - 2) % NJ + 1)
                    uj = ubuf[:, j * 512:j * 512 + tw]
                    inc(sc.activation(out=xb[bb][:, 0:tw], in_=uj, func=AF.Copy), SE, 1)
                    inc(sc.activation(out=sq[bb][:, 0:tw], in_=uj, func=AF.Square), SE, 1)
                sc.wait_ge(VE.h, b_VE + ti * 5 + 3)
                inc(sc.activation(out=rows[0:1, 1024:1024 + tw],
                                  in_=rows[0:1, 1024:1024 + tw], func=AF.Ln), SE, 1)
                sc.wait_ge(SE.h, b_SE + ti * 35 + 33)
                inc(sc.activation(out=rows[0:1, 1024:1024 + tw],
                                  in_=rows[0:1, 1024:1024 + tw], func=AF.Exp,
                                  scale=-0.5), SE, 2)

        @blk.tensor
        def _(t):
            wb(t, ctx)
            for ti, (t0, tw) in enumerate(tsubs):
                if ti >= 1:
                    t.wait_ge(VE.h, b_VE + (ti - 1) * 5 + 2)
                for j in range(NJ):
                    bb = j % 2
                    t.wait_ge(SE.h, b_SE + ti * 35 + 2 * (j + 1))
                    t.matmul(ctx.ps[0][0:1, 0:tw], ctx.onescolbf[:], xb[bb][:, 0:tw],
                             start=(j == 0), stop=(j == NJ - 1))
                    inc(t.matmul(ctx.ps[1][0:1, 0:tw], ctx.onescolbf[:], sq[bb][:, 0:tw],
                                 start=(j == 0), stop=(j == NJ - 1)), PE, 1)
                t.wait_ge(SE.h, b_SE + (ti + 1) * 35)
                if ti >= 1:
                    t.wait_ge(VE.h, b_VE + ti * 5)
                t.matmul(ctx.ps[2][:, 0:tw], ctx.onesrow[:], rows[0:1, 0:tw],
                         start=True, stop=True)
                inc(t.matmul(ctx.ps[3][:, 0:tw], ctx.onesrow[:], rows[0:1, 1024:1024 + tw],
                             start=True, stop=True), PE, 1)

        @blk.vector
        def _(v):
            wb(v, ctx)
            for ti, (t0, tw) in enumerate(tsubs):
                v.wait_ge(PE.h, b_PE + ti * 17 + 16)
                v.tensor_scalar_mul(out=rows[0:1, 0:tw], in0=ctx.ps[0][0:1, 0:tw],
                                    scalar1=1.0 / D_MODEL)
                inc(v.tensor_scalar_mul(out=rows[0:1, 512:512 + tw],
                                        in0=ctx.ps[1][0:1, 0:tw],
                                        scalar1=1.0 / D_MODEL), VE, 1)
                v.wait_ge(VE.h, b_VE + ti * 5 + 1)
                inc(v.tensor_mul(out=rows[0:1, 1024:1024 + tw], in0=rows[0:1, 0:tw],
                                 in1=rows[0:1, 0:tw]), VE, 1)
                v.wait_ge(VE.h, b_VE + ti * 5 + 2)
                inc(v.scalar_tensor_tensor(out=rows[0:1, 1024:1024 + tw],
                                           in0=rows[0:1, 512:512 + tw], scalar=1e-5,
                                           in1=rows[0:1, 1024:1024 + tw],
                                           op0=ALU.add, op1=ALU.subtract), VE, 1)
                v.wait_ge(PE.h, b_PE + (ti + 1) * 17)
                last = None
                for j in range(NJ):
                    uj = ubuf[:, j * 512:j * 512 + tw]
                    last = v.tensor_sub(out=uj, in0=uj, in1=ctx.ps[2][:, 0:tw])
                inc(last, VE, 1)
                v.wait_ge(VE.h, b_VE + ti * 5 + 4)
                last = None
                for j in range(NJ):
                    uj = ubuf[:, j * 512:j * 512 + tw]
                    xo = ctx.xln[:, j * width + t0:j * width + t0 + tw]
                    last = v.tensor_mul(out=xo, in0=uj, in1=ctx.ps[3][:, 0:tw])
                inc(last, VE, 1)

        endbar(ctx, blk)


def phase_gateup(ctx, width, name):
    nc = ctx.nc
    tsubs = TSUBS_A if width == W_A else TSUBS_F
    NT = len(tsubs)
    NP = NI
    P_W, P_ST, PE, VE, SE = ctx.P_W, ctx.P_ST, ctx.PE, ctx.VE, ctx.SE
    b_W, b_ST, b_PE, b_VE, b_SE = P_W.base(), P_ST.base(), PE.v, VE.v, SE.v

    with contextlib.ExitStack() as psx:
        wg = [psx.enter_context(nc.sbuf_tensor(f"{name}_wg{i}", [128, NJ * 128], BF16)) for i in range(2)]
        wu = [psx.enter_context(nc.sbuf_tensor(f"{name}_wu{i}", [128, NJ * 128], BF16)) for i in range(2)]
        sg = [psx.enter_context(nc.sbuf_tensor(f"{name}_sg{i}", [128, 512], BF16)) for i in range(2)]
        hb = [psx.enter_context(nc.sbuf_tensor(f"{name}_hb{i}", [128, width], BF16)) for i in range(2)]
        blk = psx.enter_context(nc.Block(name))

        @blk.gpsimd
        def _(g):
            wb(g, ctx)
            for k in range(NP):
                bb = k % 2
                if k >= 2:
                    g.wait_ge(PE.h, b_PE + (k - 1) * NT)
                sp_inc(g.dma_start(out=wg[bb][:].rearrange("p (j o) -> p j o", j=NJ),
                                   in_=ctx.wgu[:, k * 128:(k + 1) * 128]
                                   .rearrange("(j p) o -> p j o", p=128)), P_W, k)
                sp_inc(g.dma_start(out=wu[bb][:].rearrange("p (j o) -> p j o", j=NJ),
                                   in_=ctx.wgu[:, INTER + k * 128:INTER + (k + 1) * 128]
                                   .rearrange("(j p) o -> p j o", p=128)), P_W, k)

        @blk.tensor
        def _(t):
            wb(t, ctx)
            for k in range(NP):
                bb = k % 2
                sp_wait(t, P_W, b_W, k, 32)
                for ti, (t0, tw) in enumerate(tsubs):
                    it = k * NT + ti
                    pg = ctx.ps[4 + 2 * (it % 2)]
                    pu = ctx.ps[5 + 2 * (it % 2)]
                    if it >= 2:
                        t.wait_ge(VE.h, b_VE + it - 1)
                    mm = None
                    for j in range(NJ):
                        t.matmul(pg[:, 0:tw], wg[bb][:, j * 128:(j + 1) * 128],
                                 ctx.xln[:, j * width + t0:j * width + t0 + tw],
                                 start=(j == 0), stop=(j == NJ - 1))
                        mm = t.matmul(pu[:, 0:tw], wu[bb][:, j * 128:(j + 1) * 128],
                                      ctx.xln[:, j * width + t0:j * width + t0 + tw],
                                      start=(j == 0), stop=(j == NJ - 1))
                    inc(mm, PE, 1)

        @blk.scalar
        def _(sc):
            wb(sc, ctx)
            for k in range(NP):
                for ti, (t0, tw) in enumerate(tsubs):
                    it = k * NT + ti
                    pg = ctx.ps[4 + 2 * (it % 2)]
                    sc.wait_ge(PE.h, b_PE + it + 1)
                    if it >= 2:
                        sc.wait_ge(VE.h, b_VE + it - 1)
                    inc(sc.activation(out=sg[it % 2][:, 0:tw], in_=pg[:, 0:tw],
                                      func=AF.Silu), SE, 1)

        @blk.vector
        def _(v):
            wb(v, ctx)
            for k in range(NP):
                hbuf = hb[k % 2]
                if k >= 2:
                    sp_wait(v, P_ST, b_ST, k - 2)
                for ti, (t0, tw) in enumerate(tsubs):
                    it = k * NT + ti
                    pu = ctx.ps[5 + 2 * (it % 2)]
                    v.wait_ge(SE.h, b_SE + it + 1)
                    inc(v.tensor_mul(out=hbuf[:, t0:t0 + tw], in0=sg[it % 2][:, 0:tw],
                                     in1=pu[:, 0:tw]), VE, 1)

        @blk.sync
        def _(s):
            wb(s, ctx)
            for k in range(NP):
                s.wait_ge(VE.h, b_VE + (k + 1) * NT)
                sp_inc(s.dma_start(out=ctx.h_scr[k * 128:(k + 1) * 128, 0:width],
                                   in_=hb[k % 2][:]), P_ST, k)

        endbar(ctx, blk)
    ctx.xln_scope.close()
    ctx.xln = None


def phase_down(ctx, res_src, dst, width, name):
    nc = ctx.nc
    tsubs = TSUBS_A if width == W_A else TSUBS_F
    NT = len(tsubs)
    dH, P_LD, P_W, P_ST, PE, VE = ctx.dH, ctx.P_LD, ctx.P_W, ctx.P_ST, ctx.PE, ctx.VE
    b_dH, b_LD, b_W, b_ST, b_PE, b_VE = dH.v, P_LD.base(), P_W.base(), P_ST.base(), PE.v, VE.v

    with contextlib.ExitStack() as psx:
        hseg = psx.enter_context(nc.sbuf_tensor(f"{name}_hseg", [128, NI * 512], BF16))
        wd = [psx.enter_context(nc.sbuf_tensor(f"{name}_wd{i}", [128, NI * 128], BF16)) for i in range(2)]
        ub = [psx.enter_context(nc.sbuf_tensor(f"{name}_ub{i}", [128, 512], F32)) for i in range(2)]
        xo = [psx.enter_context(nc.sbuf_tensor(f"{name}_xo{i}", [128, 512], F32)) for i in range(2)]
        blk = psx.enter_context(nc.Block(name))

        @blk.sync
        def _(s):
            wb(s, ctx)
            for ti, (t0, tw) in enumerate(tsubs):
                if ti >= 1:
                    s.wait_ge(PE.h, b_PE + ti * NJ)
                inc(s.dma_start(
                    out=hseg[:].rearrange("p (i t) -> p i t", i=NI)[:, :, 0:tw],
                    in_=ctx.h_scr[:, t0:t0 + tw].rearrange("(i p) t -> p i t", p=128)),
                    dH, 16)
                for d in range(NJ):
                    it = ti * NJ + d
                    if it >= 2:
                        s.wait_ge(VE.h, b_VE + it - 1)
                    sp_inc(s.dma_start(out=ub[d % 2][:, 0:tw],
                                       in_=res_src[d * 128:(d + 1) * 128, t0:t0 + tw]),
                           P_LD, it)

        @blk.gpsimd
        def _(g):
            wb(g, ctx)
            for ti in range(NT):
                for d in range(NJ):
                    it = ti * NJ + d
                    if it >= 2:
                        g.wait_ge(PE.h, b_PE + it - 1)
                    sp_inc(g.dma_start(out=wd[d % 2][:].rearrange("p (i d) -> p i d", i=NI),
                                       in_=ctx.wdn[:, d * 128:(d + 1) * 128]
                                       .rearrange("(i p) d -> p i d", p=128)), P_W, it)

        @blk.tensor
        def _(t):
            wb(t, ctx)
            for ti, (t0, tw) in enumerate(tsubs):
                t.wait_ge(dH.h, b_dH + 16 * (ti + 1))
                for d in range(NJ):
                    it = ti * NJ + d
                    pd = ctx.ps[4 + it % 2]
                    sp_wait(t, P_W, b_W, it)
                    if it >= 2:
                        t.wait_ge(VE.h, b_VE + it - 1)
                    mm = None
                    for i in range(NI):
                        mm = t.matmul(pd[:, 0:tw], wd[d % 2][:, i * 128:(i + 1) * 128],
                                      hseg[:, i * 512:i * 512 + tw],
                                      start=(i == 0), stop=(i == NI - 1))
                    inc(mm, PE, 1)

        @blk.vector
        def _(v):
            wb(v, ctx)
            for ti, (t0, tw) in enumerate(tsubs):
                for d in range(NJ):
                    it = ti * NJ + d
                    pd = ctx.ps[4 + it % 2]
                    v.wait_ge(PE.h, b_PE + it + 1)
                    sp_wait(v, P_LD, b_LD, it)
                    if it >= 2:
                        sp_wait(v, P_ST, b_ST, it - 2)
                    inc(v.scalar_tensor_tensor(out=xo[it % 2][:, 0:tw], in0=pd[:, 0:tw],
                                               scalar=0.5, in1=ub[d % 2][:, 0:tw],
                                               op0=ALU.mult, op1=ALU.add), VE, 1)

        @blk.scalar
        def _(sc):
            wb(sc, ctx)
            for ti, (t0, tw) in enumerate(tsubs):
                for d in range(NJ):
                    it = ti * NJ + d
                    sc.wait_ge(VE.h, b_VE + it + 1)
                    sp_inc(sc.dma_start(out=dst[d * 128:(d + 1) * 128, t0:t0 + tw],
                                        in_=xo[it % 2][:, 0:tw]), P_ST, it)

        endbar(ctx, blk)


def phase_inproj(ctx):
    nc = ctx.nc
    tsubs = TSUBS_A
    NT = len(tsubs)
    P_W, P_ST, PE, SE = ctx.P_W, ctx.P_ST, ctx.PE, ctx.SE
    b_W, b_ST, b_PE, b_SE = P_W.base(), P_ST.base(), PE.v, SE.v

    with contextlib.ExitStack() as psx:
        wbuf = [psx.enter_context(nc.sbuf_tensor(f"ip_wb{i}", [128, NJ * 128], BF16)) for i in range(2)]
        ob = [psx.enter_context(nc.sbuf_tensor(f"ip_ob{i}", [128, W_A], BF16)) for i in range(2)]
        blk = psx.enter_context(nc.Block("inproj"))

        @blk.gpsimd
        def _(g):
            wb(g, ctx)
            for k in range(NO_IN):
                ow = 128 if k < 66 else 64
                if k >= 2:
                    g.wait_ge(PE.h, b_PE + (k - 1) * NT)
                sp_inc(g.dma_start(out=wbuf[k % 2][:].rearrange("p (j o) -> p j o", j=NJ)
                                   [:, :, 0:ow],
                                   in_=ctx.win[:, k * 128:k * 128 + ow]
                                   .rearrange("(j p) o -> p j o", p=128)), P_W, k)

        @blk.tensor
        def _(t):
            wb(t, ctx)
            for k in range(NO_IN):
                ow = 128 if k < 66 else 64
                sp_wait(t, P_W, b_W, k)
                for ti, (t0, tw) in enumerate(tsubs):
                    it = k * NT + ti
                    pp = ctx.ps[4 + it % 2]
                    if it >= 2:
                        t.wait_ge(SE.h, b_SE + it - 1)
                    mm = None
                    for j in range(NJ):
                        mm = t.matmul(pp[0:ow, 0:tw], wbuf[k % 2][:, j * 128:j * 128 + ow],
                                      ctx.xln[:, j * W_A + t0:j * W_A + t0 + tw],
                                      start=(j == 0), stop=(j == NJ - 1))
                    inc(mm, PE, 1)

        @blk.scalar
        def _(sc):
            wb(sc, ctx)
            for k in range(NO_IN):
                ow = 128 if k < 66 else 64
                for ti, (t0, tw) in enumerate(tsubs):
                    it = k * NT + ti
                    pp = ctx.ps[4 + it % 2]
                    sc.wait_ge(PE.h, b_PE + it + 1)
                    if k >= 2 and ti == 0 and k < 66:
                        sp_wait(sc, P_ST, b_ST, k - 2)
                    if k == 66:
                        inc(sc.activation(out=ctx.dtraw[:, t0:t0 + tw],
                                          in_=pp[0:64, 0:tw], func=AF.Copy), SE, 1)
                    else:
                        inc(sc.activation(out=ob[k % 2][:, t0:t0 + tw], in_=pp[:, 0:tw],
                                          func=AF.Copy), SE, 1)

        @blk.sync
        def _(s):
            wb(s, ctx)
            for k in range(66):
                s.wait_ge(SE.h, b_SE + (k + 1) * NT)
                if k < 32:
                    sp_inc(s.dma_start(out=ctx.z_scr[k * 128:(k + 1) * 128, :],
                                       in_=ob[k % 2][:, HALO:W_A]), P_ST, k)
                else:
                    kk = k - 32
                    sp_inc(s.dma_start(out=ctx.xbcp_scr[kk * 128:(kk + 1) * 128, :],
                                       in_=ob[k % 2][:]), P_ST, k)

        endbar(ctx, blk)
    ctx.xln_scope.close()
    ctx.xln = None


def phase_dt(ctx):
    nc = ctx.nc
    dC, PE, VE, SE = ctx.dC, ctx.PE, ctx.VE, ctx.SE
    b_dC, b_PE, b_VE, b_SE = dC.v, PE.v, VE.v, SE.v

    with contextlib.ExitStack() as psx:
        tmp = psx.enter_context(nc.sbuf_tensor("dt_tmp", [NHEADS, TOK], F32))
        tmp2 = psx.enter_context(nc.sbuf_tensor("dt_tmp2", [NHEADS, TOK], F32))
        trow = psx.enter_context(nc.sbuf_tensor("dt_trow", [16, NHEADS], F32))
        trow2 = psx.enter_context(nc.sbuf_tensor("dt_trow2", [16, NHEADS], F32))
        trow1 = psx.enter_context(nc.sbuf_tensor("dt_trow1", [1, NHEADS], F32))
        blk = psx.enter_context(nc.Block("dt"))

        @blk.vector
        def _(v):
            wb(v, ctx)
            inc(v.tensor_scalar_add(out=tmp[:], in0=ctx.dtraw[:, HALO:W_A],
                                    scalar1=ctx.cols[:, 0:1]), VE, 1)
            v.wait_ge(SE.h, b_SE + 3)
            inc(v.scalar_tensor_tensor(out=tmp[:], in0=tmp[:], scalar=0.0,
                                       in1=tmp2[:], op0=ALU.max, op1=ALU.add), VE, 1)
            v.wait_ge(VE.h, b_VE + 2)
            inc(v.tensor_scalar_mul(out=ctx.Ab[:], in0=tmp[:],
                                    scalar1=ctx.cols[:, 1:2]), VE, 1)
            v.wait_ge(VE.h, b_VE + 3)
            inc(v.tensor_tensor_scan(out=ctx.cumA[:], data0=ctx.segmask[:],
                                     data1=ctx.Ab[:], initial=0.0,
                                     op0=ALU.mult, op1=ALU.add), VE, 1)
            v.wait_ge(VE.h, b_VE + 4)
            inc(v.tensor_copy(out=ctx.tots[:].rearrange("h (c u) -> h c u", u=1),
                              in_=ctx.cumA[:].rearrange("h (c l) -> h c l", l=CHUNK)
                              [:, :, CHUNK - 1:CHUNK]), VE, 1)
            v.wait_ge(VE.h, b_VE + 5)
            v.tensor_tensor_scan(out=ctx.ptot[:, 1:NC + 1], data0=ctx.segmask[:, 0:NC],
                                 data1=ctx.tots[:], initial=0.0,
                                 op0=ALU.mult, op1=ALU.add)
            inc(v.tensor_copy(out=ctx.ptot[:, 0:1], in_=ctx.cols[:, 3:4]), VE, 1)
            v.wait_ge(PE.h, b_PE + 1)
            v.tensor_copy(out=trow[:], in_=ctx.ps[0][0:16, 0:NHEADS])
            v.tensor_copy(out=trow2[:], in_=ctx.ps[1][0:16, 0:NHEADS])
            inc(v.tensor_copy(out=trow1[:], in_=ctx.ps[2][0:1, 0:NHEADS]), VE, 1)

        @blk.scalar
        def _(sc):
            wb(sc, ctx)
            sc.wait_ge(VE.h, b_VE + 1)
            inc(sc.activation(out=tmp2[:], in_=tmp[:], func=AF.Abs), SE, 1)
            sc.wait_ge(SE.h, b_SE + 1)
            inc(sc.activation(out=tmp2[:], in_=tmp2[:], func=AF.Exp, scale=-1.0), SE, 1)
            sc.wait_ge(SE.h, b_SE + 2)
            inc(sc.activation(out=tmp2[:], in_=tmp2[:], func=AF.Ln, bias=1.0), SE, 1)

        @blk.tensor
        def _(t):
            wb(t, ctx)
            t.wait_ge(VE.h, b_VE + 6)
            t.transpose(ctx.ps[0][0:16, 0:NHEADS], ctx.tots[:], ctx.idf32[0:NHEADS, 0:NHEADS])
            t.transpose(ctx.ps[1][0:16, 0:NHEADS], ctx.ptot[:, 0:NC],
                        ctx.idf32[0:NHEADS, 0:NHEADS])
            inc(t.transpose(ctx.ps[2][0:1, 0:NHEADS], ctx.ptot[:, NC:NC + 1],
                            ctx.idf32[0:NHEADS, 0:NHEADS]), PE, 1)

        @blk.sync
        def _(s):
            wb(s, ctx)
            s.wait_ge(VE.h, b_VE + 7)
            inc(s.dma_start(out=ctx.cums_scr, in_=ctx.cumA[:]), dC, 16)
            inc(s.dma_start(out=ctx.rows_scr[0:16, :], in_=trow[:]), dC, 16)
            inc(s.dma_start(out=ctx.rows_scr[16:32, :], in_=trow2[:]), dC, 16)
            inc(s.dma_start(out=ctx.pub_scr[0:1, 128 * D_INNER:PUBN], in_=trow1[:]),
                dC, 16)

        endbar(ctx, blk)


def phase_conv(ctx):
    nc = ctx.nc
    dC, P_LD, P_PS, VE, SE = ctx.dC, ctx.P_LD, ctx.P_PS, ctx.VE, ctx.SE
    b_dC, b_LD, b_PS, b_VE, b_SE = dC.v, P_LD.base(), P_PS.base(), VE.v, SE.v

    with contextlib.ExitStack() as psx:
        xin = [psx.enter_context(nc.sbuf_tensor(f"cv_xin{i}", [128, W_A], BF16)) for i in range(2)]
        acc = [psx.enter_context(nc.sbuf_tensor(f"cv_acc{i}", [128, TOK], F32)) for i in range(2)]
        outb = [psx.enter_context(nc.sbuf_tensor(f"cv_out{i}", [128, TOK], BF16)) for i in range(2)]
        wcv = psx.enter_context(nc.sbuf_tensor("cv_w", [128, NDC * 4], F32))
        bcv = psx.enter_context(nc.sbuf_tensor("cv_b", [128, NDC], F32))
        blk = psx.enter_context(nc.Block("conv"))

        @blk.sync
        def _(s):
            wb(s, ctx)
            inc(s.dma_start(out=wcv[:], in_=ctx.wconv), dC, 16)
            inc(s.dma_start(out=bcv[:], in_=ctx.bconv), dC, 16)
            for dc in range(NDC):
                if dc >= 2:
                    s.wait_ge(VE.h, b_VE + 4 * (dc - 1))
                sp_inc(s.dma_start(out=xin[dc % 2][:],
                                   in_=ctx.xbcp_scr[dc * 128:(dc + 1) * 128, :]), P_LD, dc)

        @blk.vector
        def _(v):
            wb(v, ctx)
            v.wait_ge(dC.h, b_dC + 32)
            for dc in range(NDC):
                bb = dc % 2
                sp_wait(v, P_LD, b_LD, dc)
                if dc >= 2:
                    v.wait_ge(SE.h, b_SE + dc - 1)
                inc(v.tensor_scalar_mul(out=acc[bb][:], in0=xin[bb][:, 0:TOK],
                                        scalar1=wcv[:, 4 * dc:4 * dc + 1]), VE, 1)
                for kk in range(1, 4):
                    v.wait_ge(VE.h, b_VE + 4 * dc + kk)
                    inc(v.scalar_tensor_tensor(out=acc[bb][:], in0=xin[bb][:, kk:kk + TOK],
                                               scalar=wcv[:, 4 * dc + kk:4 * dc + kk + 1],
                                               in1=acc[bb][:], op0=ALU.mult, op1=ALU.add),
                        VE, 1)

        @blk.scalar
        def _(sc):
            wb(sc, ctx)
            for dc in range(NDC):
                bb = dc % 2
                sc.wait_ge(VE.h, b_VE + 4 * (dc + 1))
                if dc >= 2:
                    sp_wait(sc, P_PS, b_PS, dc - 2)
                inc(sc.activation(out=outb[bb][:], in_=acc[bb][:], func=AF.Silu,
                                  bias=bcv[:, dc:dc + 1]), SE, 1)

        @blk.gpsimd
        def _(g):
            wb(g, ctx)
            for dc in range(NDC):
                g.wait_ge(SE.h, b_SE + dc + 1)
                sp_inc(g.dma_start(out=ctx.xbc_scr[dc * 128:(dc + 1) * 128, :],
                                   in_=outb[dc % 2][:]), P_PS, dc)

        endbar(ctx, blk)


def phase_d1(ctx):
    nc = ctx.nc
    P_LD, P_W, P_ST, PE, VE, SE = ctx.P_LD, ctx.P_W, ctx.P_ST, ctx.PE, ctx.VE, ctx.SE
    b_LD, b_W, b_ST, b_PE, b_VE, b_SE = P_LD.base(), P_W.base(), P_ST.base(), PE.v, VE.v, SE.v
    PE_PER = 2 + NHEADS
    VE_PER = 1 + 2 * NHEADS

    def ve_nrev(c):
        return b_VE + c * VE_PER + 1

    def ve_xsc(c, h):
        if h <= 3:
            return b_VE + c * VE_PER + 1 + (h + 1)
        return b_VE + c * VE_PER + 2 * h - 1

    def ve_scan(c, h):
        if h <= 59:
            return b_VE + c * VE_PER + 2 * h + 6
        return b_VE + c * VE_PER + h + 66

    def pe_tr(c):
        return b_PE + c * PE_PER + 1

    def pe_repl(c):
        return b_PE + c * PE_PER + 2

    def pe_mmc(c, h):
        return b_PE + c * PE_PER + 2 + (h + 1)

    with contextlib.ExitStack() as psx:
        xtm = [psx.enter_context(nc.sbuf_tensor(f"d1_xtm{i}", [128, D_INNER], BF16)) for i in range(2)]
        btm = [psx.enter_context(nc.sbuf_tensor(f"d1_btm{i}", [128, 128], BF16)) for i in range(2)]
        xsc = [psx.enter_context(nc.sbuf_tensor(f"d1_xsc{i}", [128, 64], BF16)) for i in range(4)]
        sel = psx.enter_context(nc.sbuf_tensor("d1_sel", [128, D_INNER], F32))
        dect = [psx.enter_context(nc.sbuf_tensor(f"d1_dect{i}", [128, NHEADS], F32)) for i in range(2)]
        decc = [psx.enter_context(nc.sbuf_tensor(f"d1_decc{i}", [128, NHEADS], F32)) for i in range(2)]
        nrev = [psx.enter_context(nc.sbuf_tensor(f"d1_nrev{i}", [NHEADS, CHUNK], F32)) for i in range(2)]
        rowb = [psx.enter_context(nc.sbuf_tensor(f"d1_rowb{i}", [1, NHEADS], F32)) for i in range(2)]
        blk = psx.enter_context(nc.Block("d1"))

        @blk.sync
        def _(s):
            wb(s, ctx)
            for c in range(NC):
                if c >= 2:
                    s.wait_ge(PE.h, pe_mmc(c - 1, NHEADS - 1))
                sp_inc(s.dma_start_transpose(out=xtm[c % 2][:],
                                             in_=ctx.xbc_scr[0:D_INNER,
                                                             c * CHUNK:(c + 1) * CHUNK]),
                       P_LD, c)
                sp_inc(s.dma_start_transpose(out=btm[c % 2][:],
                                             in_=ctx.xbc_scr[D_INNER:D_INNER + 128,
                                                             c * CHUNK:(c + 1) * CHUNK]),
                       P_LD, c)

        @blk.gpsimd
        def _(g):
            wb(g, ctx)
            for c in range(NC):
                if c >= 2:
                    g.wait_ge(PE.h, pe_repl(c - 1))
                sp_inc(g.dma_start(out=rowb[c % 2][:], in_=ctx.rows_scr[c:c + 1, :]),
                       P_W, c)

        @blk.vector
        def _(v):
            wb(v, ctx)

            def emit_xsc(v, c, h):
                if h >= 4:
                    v.wait_ge(PE.h, pe_mmc(c, h - 4))
                elif c >= 1:
                    v.wait_ge(PE.h, pe_mmc(c - 1, NHEADS - 4 + h))
                inc(v.tensor_scalar_mul(out=xsc[h % 4][:],
                                        in0=xtm[c % 2][:, h * 64:(h + 1) * 64],
                                        scalar1=dect[c % 2][:, h:h + 1]), VE, 1)

            def emit_scan(v, c, h):
                pb = ctx.ps[4 + h % 2]
                v.wait_ge(PE.h, pe_mmc(c, h))
                if c == 0:
                    inc(v.tensor_copy(out=sel[:, h * 64:(h + 1) * 64],
                                      in_=pb[:, 0:64]), VE, 1)
                else:
                    inc(v.scalar_tensor_tensor(out=sel[:, h * 64:(h + 1) * 64],
                                               in0=sel[:, h * 64:(h + 1) * 64],
                                               scalar=decc[c % 2][:, h:h + 1],
                                               in1=pb[:, 0:64],
                                               op0=ALU.mult, op1=ALU.add), VE, 1)

            for c in range(NC):
                cs = ctx.cumA[:, c * CHUNK:(c + 1) * CHUNK]
                if c >= 2:
                    v.wait_ge(PE.h, pe_tr(c - 1))
                inc(v.tensor_scalar(out=nrev[c % 2][:], in0=cs,
                                    scalar1=ctx.cumA[:, (c + 1) * CHUNK - 1:(c + 1) * CHUNK],
                                    scalar2=None, op0=ALU.subtract), VE, 1)
                v.wait_ge(SE.h, b_SE + 2 * c + 1)
                sp_wait(v, P_LD, b_LD, c, 32)
                for h in range(4):
                    emit_xsc(v, c, h)
                v.wait_ge(SE.h, b_SE + 2 * c + 2)
                if c >= 1:
                    sp_wait(v, P_ST, b_ST, c - 1)
                for k in range(NHEADS - 4):
                    emit_scan(v, c, k)
                    emit_xsc(v, c, k + 4)
                for k in range(NHEADS - 4, NHEADS):
                    emit_scan(v, c, k)

        @blk.scalar
        def _(sc):
            wb(sc, ctx)
            for c in range(NC):
                sc.wait_ge(PE.h, pe_tr(c))
                if c >= 2:
                    sc.wait_ge(VE.h, ve_xsc(c - 2, NHEADS - 1))
                inc(sc.activation(out=dect[c % 2][:], in_=ctx.ps[2][:, 0:NHEADS],
                                  func=AF.Exp, scale=-1.0), SE, 1)
                sc.wait_ge(PE.h, pe_repl(c))
                if c >= 2:
                    sc.wait_ge(VE.h, ve_scan(c - 2, NHEADS - 1))
                inc(sc.activation(out=decc[c % 2][:], in_=ctx.ps[3][:, 0:NHEADS],
                                  func=AF.Exp), SE, 1)
                sc.wait_ge(VE.h, ve_scan(c, NHEADS - 1))
                if c < NC - 1:
                    sp_inc(sc.dma_start(out=ctx.sel_scr[(c + 1) * 128:(c + 2) * 128, :],
                                        in_=sel[:]), P_ST, c)
                else:
                    sp_inc(sc.dma_start(out=ctx.pub_scr[0:1, 0:128 * D_INNER]
                                        .rearrange("u (p f) -> (u p) f", p=128),
                                        in_=sel[:]), P_ST, c)

        @blk.tensor
        def _(t):
            wb(t, ctx)
            for c in range(NC):
                t.wait_ge(VE.h, ve_nrev(c))
                if c >= 1:
                    t.wait_ge(SE.h, b_SE + 2 * (c - 1) + 1)
                inc(t.transpose(ctx.ps[2][:, 0:NHEADS], nrev[c % 2][:],
                                ctx.idf32[0:NHEADS, 0:NHEADS]), PE, 1)
                sp_wait(t, P_W, b_W, c)
                if c >= 1:
                    t.wait_ge(SE.h, b_SE + 2 * (c - 1) + 2)
                inc(t.matmul(ctx.ps[3][:, 0:NHEADS], ctx.onesrow[:], rowb[c % 2][:],
                             start=True, stop=True), PE, 1)
                for h in range(NHEADS):
                    pb = ctx.ps[4 + h % 2]
                    t.wait_ge(VE.h, ve_xsc(c, h))
                    if h >= 2:
                        t.wait_ge(VE.h, ve_scan(c, h - 2))
                    elif c >= 1:
                        t.wait_ge(VE.h, ve_scan(c - 1, NHEADS - 2 + h))
                    inc(t.matmul(pb[:, 0:64], btm[c % 2][:], xsc[h % 4][:],
                                 start=True, stop=True), PE, 1)

        endbar(ctx, blk)


def phase_exchange(ctx):
    nc = ctx.nc
    P_W, P_X, P_ST, PE, VE, SE, CC = (ctx.P_W, ctx.P_X, ctx.P_ST,
                                      ctx.PE, ctx.VE, ctx.SE, ctx.CC)
    b_W, b_X, b_ST = P_W.base(), P_X.base(), P_ST.base()
    b_PE, b_VE, b_SE, b_CC = PE.v, VE.v, SE.v, CC.v

    ctx.S0_scope = contextlib.ExitStack()
    ctx.S0 = ctx.S0_scope.enter_context(nc.sbuf_tensor("S0", [128, D_INNER], F32))

    with contextlib.ExitStack() as psx:
        logd = psx.enter_context(nc.sbuf_tensor("ex_logd", [8, NHEADS], F32))
        btw = psx.enter_context(nc.sbuf_tensor("ex_btw", [8, 8], F32))
        prd = psx.enter_context(nc.sbuf_tensor("ex_prd", [8, 1], F32))
        coef = psx.enter_context(nc.sbuf_tensor("ex_coef", [8, NHEADS], F32))
        crow = [psx.enter_context(nc.sbuf_tensor(f"ex_crow{i}", [1, NHEADS], F32)) for i in range(2)]
        lbuf = [psx.enter_context(nc.sbuf_tensor(f"ex_lb{i}", [128, D_INNER], F32)) for i in range(2)]
        tmp = psx.enter_context(nc.sbuf_tensor("ex_tmp", [128, D_INNER], F32))
        blk = psx.enter_context(nc.Block("exchange"))

        @blk.gpsimd
        def _(g):
            wb(g, ctx)
            cc_i = g.collective_compute("AllGather", ALU.bypass,
                                        replica_groups=[list(range(8))],
                                        ins=[ctx.pub_scr.opt()],
                                        outs=[ctx.gat_scr.opt()])
            inc(cc_i, CC, 1)
            g.wait_ge(CC.h, b_CC + 1)
            inc(g.dma_start(out=logd[:],
                            in_=ctx.gat_scr[:, 128 * D_INNER:PUBN]), P_W.s[0], 16)
            inc(g.dma_start(out=btw[:], in_=ctx.btwT), P_W.s[0], 16)
            inc(g.dma_start(out=prd[:], in_=ctx.predc), P_W.s[0], 16)
            for j in range(8):
                if j == 0:
                    sp_wait(g, P_ST, b_ST, 0)
                if j >= 1:
                    g.wait_ge(PE.h, b_PE + 1 + j)
                inc(g.dma_start(out=crow[j % 2][:],
                                in_=ctx.rows_scr[32 + j:33 + j, :]), P_X.s[0], 16)
                if j >= 1:
                    g.wait_ge(VE.h, b_VE + 1 + 2 * j)
                inc(g.dma_start(out=lbuf[j % 2][:],
                                in_=ctx.gat_scr[j:j + 1, 0:128 * D_INNER]
                                .rearrange("u (p f) -> (u p) f", p=128)), P_X.s[1], 16)

        @blk.tensor
        def _(t):
            wb(t, ctx)
            t.wait_ge(P_W.s[0].h, b_W[0] + 48)
            inc(t.matmul(ctx.ps[0][0:8, 0:NHEADS], btw[:], logd[:], start=True,
                         stop=True), PE, 1)
            for j in range(8):
                t.wait_ge(P_X.s[0].h, b_X[0] + 16 * (j + 1))
                if j >= 1:
                    t.wait_ge(VE.h, b_VE + 1 + 2 * j)
                inc(t.matmul(ctx.ps[1][:, 0:NHEADS], ctx.onesrow[:], crow[j % 2][:],
                             start=True, stop=True), PE, 1)

        @blk.scalar
        def _(sc):
            wb(sc, ctx)
            sc.wait_ge(PE.h, b_PE + 1)
            inc(sc.activation(out=coef[:], in_=ctx.ps[0][0:8, 0:NHEADS], func=AF.Exp),
                SE, 1)

        @blk.vector
        def _(v):
            wb(v, ctx)
            v.wait_ge(SE.h, b_SE + 1)
            inc(v.tensor_scalar_mul(out=coef[:], in0=coef[:], scalar1=prd[:]), VE, 1)
            for j in range(8):
                v.wait_ge(PE.h, b_PE + 1 + (j + 1))
                v.wait_ge(P_X.s[1].h, b_X[1] + 16 * (j + 1))
                inc(v.tensor_mul(out=tmp[:].rearrange("p (h w) -> p h w", h=NHEADS),
                                 in0=lbuf[j % 2][:].rearrange("p (h w) -> p h w", h=NHEADS),
                                 in1=ctx.ps[1][:, 0:NHEADS]
                                 .rearrange("p (h w) -> p h w", w=1)
                                 .broadcast_to([128, NHEADS, HEADDIM])), VE, 1)
                v.wait_ge(VE.h, b_VE + 2 + 2 * j)
                if j == 0:
                    inc(v.tensor_copy(out=ctx.S0[:], in_=tmp[:]), VE, 1)
                else:
                    inc(v.tensor_add(out=ctx.S0[:], in0=ctx.S0[:], in1=tmp[:]), VE, 1)

        @blk.sync
        def _(s):
            wb(s, ctx)
            s.wait_ge(VE.h, b_VE + 1)
            sp_inc(s.dma_start(out=ctx.rows_scr[32:40, :], in_=coef[:]), P_ST, 0)

        endbar(ctx, blk)


def phase_d2(ctx):
    nc = ctx.nc
    P_LD, P_W, P_ST, PE, VE, SE, PO = (ctx.P_LD, ctx.P_W, ctx.P_ST,
                                       ctx.PE, ctx.VE, ctx.SE, ctx.PO)
    b_LD, b_W, b_ST, b_PS = P_LD.base(), P_W.base(), P_ST.base(), ctx.P_PS.base()
    b_PE, b_VE, b_SE, b_PO = PE.v, VE.v, SE.v, PO.v

    PE_PER = 3 + NHEADS + 2
    SE_PER = 3 + NHEADS + 3
    NPAIRH = NHEADS // 2
    VE_PER = 2 + NHEADS + NPAIRH + 2
    PO_PER = 3 * NHEADS + 2 + NZ

    def pe_cbt(c):
        return b_PE + c * PE_PER + 1

    def pe_prec(c):
        return b_PE + c * PE_PER + 2

    def pe_cam(c):
        return b_PE + c * PE_PER + 3

    def pe_ymm(c, h):
        return b_PE + c * PE_PER + 3 + (h + 1)

    def pe_sq(c):
        return b_PE + c * PE_PER + 3 + NHEADS + 1

    def pe_rr(c):
        return b_PE + c * PE_PER + 3 + NHEADS + 2

    def se_sz(c):
        return b_SE + c * SE_PER + 1

    def se_ebc(c):
        return b_SE + c * SE_PER + 2

    def se_prec(c):
        return b_SE + c * SE_PER + 3

    def se_e0(c, h):
        return b_SE + c * SE_PER + 3 + (h + 1)

    def se_sq(c):
        return b_SE + c * SE_PER + 3 + NHEADS + 1

    def se_rstd(c):
        return b_SE + c * SE_PER + 3 + NHEADS + 3

    def ve_cam(c):
        return b_VE + c * VE_PER + 1

    def ve_cbt(c):
        return b_VE + c * VE_PER + 2

    def ve_seg(c, h):
        if h <= 3:
            return b_VE + c * VE_PER + 2 + h + 1
        m = (h - 4) // 2
        return b_VE + c * VE_PER + 2 + 4 + 3 * m + 2 + (h - 4 - 2 * m)

    def ve_yev(c, m):
        if m <= 29:
            return b_VE + c * VE_PER + 2 + 4 + 3 * m + 1
        return b_VE + c * VE_PER + 2 + 94 + (m - 30) + 1

    def ve_rr(c):
        return b_VE + c * VE_PER + 2 + NHEADS + NPAIRH + 1

    def ve_rrs(c):
        return b_VE + c * VE_PER + 2 + NHEADS + NPAIRH + 2

    def po_e0f(c, h):
        return b_PO + c * PO_PER + 3 * h + 1

    def po_st(c, h):
        return b_PO + c * PO_PER + 3 * h + 2

    def po_sv(c, h):
        return b_PO + c * PO_PER + 3 * h + 3

    def po_ygt(c):
        return b_PO + c * PO_PER + 3 * NHEADS + 1

    def po_app(c, j):
        return b_PO + c * PO_PER + 3 * NHEADS + 1 + (j + 1)

    NLD = 6

    with contextlib.ExitStack() as psx:
        bc = psx.enter_context(nc.sbuf_tensor("d2_bc", [128, NHEADS * CHUNK], F32))
        xtm = psx.enter_context(nc.sbuf_tensor("d2_xtm", [128, D_INNER], BF16))
        xfm = psx.enter_context(nc.sbuf_tensor("d2_xfm", [128, NZ * CHUNK], BF16))
        cfm = psx.enter_context(nc.sbuf_tensor("d2_cfm", [128, CHUNK], BF16))
        bfm = psx.enter_context(nc.sbuf_tensor("d2_bfm", [128, CHUNK], BF16))
        cbt = psx.enter_context(nc.sbuf_tensor("d2_cbt", [128, CHUNK], BF16))
        zt = psx.enter_context(nc.sbuf_tensor("d2_zt", [128, NZ * CHUNK], BF16))
        szb = psx.enter_context(nc.sbuf_tensor("d2_sz", [128, NZ * CHUNK], BF16))
        selb = psx.enter_context(nc.sbuf_tensor("d2_selb", [128, D_INNER], F32))
        seg = psx.enter_context(nc.sbuf_tensor("d2_seg", [128, D_INNER], BF16))
        prec = psx.enter_context(nc.sbuf_tensor("d2_prec", [128, NHEADS], F32))
        camt = psx.enter_context(nc.sbuf_tensor("d2_camt", [128, NHEADS], F32))
        ebc = psx.enter_context(nc.sbuf_tensor("d2_ebc", [128, NHEADS * CHUNK], BF16))
        e0f = [psx.enter_context(nc.sbuf_tensor(f"d2_e0f{i}", [128, CHUNK], F32)) for i in range(2)]
        e0 = [psx.enter_context(nc.sbuf_tensor(f"d2_e0{i}", [128, CHUNK], BF16)) for i in range(2)]
        st = [psx.enter_context(nc.sbuf_tensor(f"d2_st{i}", [128, CHUNK], BF16)) for i in range(2)]
        sv = [psx.enter_context(nc.sbuf_tensor(f"d2_sv{i}", [128, CHUNK], BF16)) for i in range(2)]
        yfm = psx.enter_context(nc.sbuf_tensor("d2_yfm", [128, NZ * CHUNK], F32))
        ynb = psx.enter_context(nc.sbuf_tensor("d2_ynb", [128, NZ * CHUNK], BF16))
        sqb = psx.enter_context(nc.sbuf_tensor("d2_sqb", [128, NZ * CHUNK], BF16))
        rr = psx.enter_context(nc.sbuf_tensor("d2_rr", [1, CHUNK], F32))
        rrs = psx.enter_context(nc.sbuf_tensor("d2_rrs", [128, CHUNK], F32))
        rowp = psx.enter_context(nc.sbuf_tensor("d2_rowp", [1, NHEADS], F32))
        blk = psx.enter_context(nc.Block("d2"))

        @blk.sync
        def _(s):
            wb(s, ctx)
            for c in range(NC):
                if c >= 1:
                    s.wait_ge(PE.h, pe_ymm(c - 1, NHEADS - 1))
                    s.wait_ge(PO.h, po_ygt(c - 1))
                    s.wait_ge(VE.h, ve_yev(c - 1, NPAIRH - 1))
                sp_inc(s.dma_start_transpose(out=xtm[:],
                                             in_=ctx.xbc_scr[0:D_INNER,
                                                             c * CHUNK:(c + 1) * CHUNK]),
                       P_LD, c)
                sp_inc(s.dma_start(out=cfm[:],
                                   in_=ctx.xbc_scr[D_INNER + 128:D_INNER + 256,
                                                   c * CHUNK:(c + 1) * CHUNK]), P_LD, c)
                sp_inc(s.dma_start(out=bfm[:],
                                   in_=ctx.xbc_scr[D_INNER:D_INNER + 128,
                                                   c * CHUNK:(c + 1) * CHUNK]), P_LD, c)
                sp_inc(s.dma_start(out=xfm[:].rearrange("p (i t) -> p i t", i=NZ),
                                   in_=ctx.xbc_scr[0:D_INNER, c * CHUNK:(c + 1) * CHUNK]
                                   .rearrange("(i p) t -> p i t", p=128)), P_LD, c)
                sp_inc(s.dma_start(out=zt[:].rearrange("p (i t) -> p i t", i=NZ),
                                   in_=ctx.z_scr[:, c * CHUNK:(c + 1) * CHUNK]
                                   .rearrange("(i p) t -> p i t", p=128)), P_LD, c)
                cc_ld = max(c, 1)
                sp_inc(s.dma_start(out=selb[:],
                                   in_=ctx.sel_scr[cc_ld * 128:(cc_ld + 1) * 128, :]),
                       P_LD, c)

        @blk.gpsimd
        def _(g):
            wb(g, ctx)
            for c in range(NC):
                if c >= 1:
                    g.wait_ge(SE.h, se_e0(c - 1, NHEADS - 1))
                    g.wait_ge(PO.h, po_sv(c - 1, NHEADS - 1))
                for q in range(4):
                    sp_inc(g.dma_start(out=bc[:, q * 16 * CHUNK:(q + 1) * 16 * CHUNK]
                                       .rearrange("p (h l) -> p h l", h=16),
                                       in_=ctx.cums_scr
                                       .rearrange("h (cc l) -> cc h l", l=CHUNK)
                                       [c:c + 1, q * 16:(q + 1) * 16, :]
                                       .broadcast_to([128, 16, CHUNK])), P_W, c)
                if c >= 1:
                    g.wait_ge(PE.h, pe_prec(c - 1))
                sp_inc(g.dma_start(out=rowp[:], in_=ctx.rows_scr[16 + c:17 + c, :]),
                       P_W, c)
                g.wait_ge(SE.h, se_ebc(c))
                g.wait_ge(VE.h, ve_cbt(c))
                for h in range(NHEADS):
                    bb = h % 2
                    if h >= 2:
                        g.wait_ge(SE.h, se_e0(c, h - 2))
                    inc(g.tensor_tensor(out=e0f[bb][:], in0=bc[:, h * CHUNK:(h + 1) * CHUNK],
                                        in1=ctx.maskb[:], op=ALU.add), PO, 1)
                    g.wait_ge(SE.h, se_e0(c, h))
                    if h >= 2:
                        g.wait_ge(PE.h, pe_ymm(c, h - 2))
                    inc(g.tensor_tensor(out=st[bb][:], in0=e0[bb][:], in1=cbt[:],
                                        op=ALU.mult), PO, 1)
                    inc(g.tensor_tensor(out=sv[bb][:], in0=cfm[:],
                                        in1=ebc[:, h * CHUNK:(h + 1) * CHUNK],
                                        op=ALU.mult), PO, 1)
                g.wait_ge(VE.h, ve_yev(c, NPAIRH - 1))
                g.wait_ge(SE.h, se_sz(c))
                inc(g.tensor_mul(out=yfm[:], in0=yfm[:], in1=szb[:]), PO, 1)
                g.wait_ge(VE.h, ve_rrs(c))
                g.wait_ge(PO.h, po_ygt(c))
                if c >= 1:
                    sp_wait(g, ctx.P_PS, b_PS, c - 1)
                for j in range(NZ):
                    inc(g.tensor_scalar(out=ynb[:, j * CHUNK:(j + 1) * CHUNK],
                                        in0=yfm[:, j * CHUNK:(j + 1) * CHUNK],
                                        scalar1=ctx.nw[:, j:j + 1], scalar2=None,
                                        op0=ALU.mult), PO, 1)
                g.wait_ge(PO.h, po_app(c, NZ - 1))
                inc(g.tensor_tensor(out=ynb[:].rearrange("p (a t) -> p a t", a=NZ),
                                    in0=ynb[:].rearrange("p (a t) -> p a t", a=NZ),
                                    in1=rrs[:].rearrange("p (a t) -> p a t", a=1)
                                    .broadcast_to([128, NZ, CHUNK]), op=ALU.mult), PO, 1)
                g.wait_ge(PO.h, b_PO + (c + 1) * PO_PER)
                sp_inc(g.dma_start(out=ctx.y_scr[:, c * CHUNK:(c + 1) * CHUNK]
                                   .rearrange("(i p) t -> p i t", p=128),
                                   in_=ynb[:].rearrange("p (i t) -> p i t", i=NZ)),
                       ctx.P_PS, c)

        @blk.scalar
        def _(sc):
            wb(sc, ctx)
            for c in range(NC):
                sp_wait(sc, P_LD, b_LD, c, 96)
                if c >= 1:
                    sc.wait_ge(PO.h, po_ygt(c - 1))
                inc(sc.activation(out=szb[:], in_=zt[:], func=AF.Silu), SE, 1)
                sp_wait(sc, P_W, b_W, c, 80)
                if c >= 1:
                    sc.wait_ge(PO.h, po_sv(c - 1, NHEADS - 1))
                inc(sc.activation(out=ebc[:], in_=bc[:], func=AF.Exp), SE, 1)
                sc.wait_ge(PE.h, pe_prec(c))
                if c >= 1:
                    sc.wait_ge(VE.h, ve_seg(c - 1, NHEADS - 1))
                inc(sc.activation(out=prec[:], in_=ctx.ps[2][:, 0:NHEADS], func=AF.Exp),
                    SE, 1)
                for h in range(NHEADS):
                    bb = h % 2
                    sc.wait_ge(PO.h, po_e0f(c, h))
                    sc.wait_ge(VE.h, ve_cam(c))
                    if h >= 2:
                        sc.wait_ge(PO.h, po_st(c, h - 2))
                    inc(sc.activation(out=e0[bb][:], in_=e0f[bb][:], func=AF.Exp,
                                      bias=camt[:, h:h + 1]), SE, 1)
                sc.wait_ge(PO.h, po_ygt(c))
                if c >= 1:
                    sc.wait_ge(PE.h, pe_sq(c - 1))
                inc(sc.activation(out=sqb[:], in_=yfm[:], func=AF.Square), SE, 1)
                sc.wait_ge(VE.h, ve_rr(c))
                inc(sc.activation(out=rr[:], in_=rr[:], func=AF.Ln), SE, 1)
                sc.wait_ge(SE.h, b_SE + c * SE_PER + 3 + NHEADS + 2)
                inc(sc.activation(out=rr[:], in_=rr[:], func=AF.Exp, scale=-0.5), SE, 1)

        @blk.tensor
        def _(t):
            wb(t, ctx)
            for c in range(NC):
                sp_wait(t, P_LD, b_LD, c, 96)
                if c >= 1:
                    t.wait_ge(VE.h, ve_cbt(c - 1))
                inc(t.matmul(ctx.ps[0][:, 0:CHUNK], bfm[:], cfm[:], start=True,
                             stop=True), PE, 1)
                sp_wait(t, P_W, b_W, c, 80)
                if c >= 1:
                    t.wait_ge(SE.h, se_prec(c - 1))
                inc(t.matmul(ctx.ps[2][:, 0:NHEADS], ctx.onesrow[:], rowp[:],
                             start=True, stop=True), PE, 1)
                if c >= 1:
                    t.wait_ge(VE.h, ve_cam(c - 1))
                inc(t.transpose(ctx.ps[3][:, 0:NHEADS],
                                ctx.cumA[:, c * CHUNK:(c + 1) * CHUNK],
                                ctx.idf32[0:NHEADS, 0:NHEADS]), PE, 1)
                for h in range(NHEADS):
                    m = h // 2
                    odd = h % 2
                    pb = ctx.ps[4 + m % 2]
                    po_r = pb[64:128, 0:CHUNK] if odd else pb[0:64, 0:CHUNK]
                    tp = (0, 64) if odd else None
                    t.wait_ge(PO.h, po_sv(c, h))
                    t.wait_ge(VE.h, ve_seg(c, h))
                    if not odd:
                        if m >= 2:
                            t.wait_ge(VE.h, ve_yev(c, m - 2))
                        elif c >= 1:
                            t.wait_ge(VE.h, ve_yev(c - 1, NPAIRH - 2 + m))
                    t.matmul(po_r, xtm[:, h * 64:(h + 1) * 64], st[h % 2][:],
                             start=True, stop=False, tile_position=tp)
                    inc(t.matmul(po_r, seg[:, h * 64:(h + 1) * 64], sv[h % 2][:],
                                 start=False, stop=True, tile_position=tp), PE, 1)
                t.wait_ge(SE.h, se_sq(c))
                if c >= 1:
                    t.wait_ge(VE.h, ve_rr(c - 1))
                mm = None
                for j in range(NZ):
                    mm = t.matmul(ctx.ps[6][0:1, 0:CHUNK], ctx.onescolbf[:],
                                  sqb[:, j * CHUNK:(j + 1) * CHUNK],
                                  start=(j == 0), stop=(j == NZ - 1))
                inc(mm, PE, 1)
                t.wait_ge(SE.h, se_rstd(c))
                if c >= 1:
                    t.wait_ge(VE.h, ve_rrs(c - 1))
                inc(t.matmul(ctx.ps[7][:, 0:CHUNK], ctx.onesrow[:], rr[:],
                             start=True, stop=True), PE, 1)

        @blk.vector
        def _(v):
            wb(v, ctx)
            for c in range(NC):
                v.wait_ge(PE.h, pe_cam(c))
                if c >= 1:
                    v.wait_ge(SE.h, se_e0(c - 1, NHEADS - 1))
                inc(v.tensor_scalar_mul(out=camt[:], in0=ctx.ps[3][:, 0:NHEADS],
                                        scalar1=-1.0), VE, 1)
                if c >= 1:
                    v.wait_ge(PO.h, po_st(c - 1, NHEADS - 1))
                inc(v.tensor_copy(out=cbt[:], in_=ctx.ps[0][:, 0:CHUNK]), VE, 1)
                v.wait_ge(SE.h, se_prec(c))
                sp_wait(v, P_LD, b_LD, c, 96)
                if c >= 1:
                    v.wait_ge(PO.h, b_PO + c * PO_PER)

                def emit_seg(v, c, h):
                    if c == 0:
                        inc(v.tensor_scalar_mul(out=seg[:, h * 64:(h + 1) * 64],
                                                in0=ctx.S0[:, h * 64:(h + 1) * 64],
                                                scalar1=prec[:, h:h + 1]), VE, 1)
                    else:
                        inc(v.scalar_tensor_tensor(out=seg[:, h * 64:(h + 1) * 64],
                                                   in0=ctx.S0[:, h * 64:(h + 1) * 64],
                                                   scalar=prec[:, h:h + 1],
                                                   in1=selb[:, h * 64:(h + 1) * 64],
                                                   op0=ALU.mult, op1=ALU.add), VE, 1)

                def emit_yev(v, c, m):
                    pb = ctx.ps[4 + m % 2]
                    v.wait_ge(PE.h, pe_ymm(c, 2 * m + 1))
                    inc(v.scalar_tensor_tensor(
                        out=yfm[:, m * CHUNK:(m + 1) * CHUNK],
                        in0=xfm[:, m * CHUNK:(m + 1) * CHUNK],
                        scalar=ctx.dp_pair[:, m:m + 1], in1=pb[:, 0:CHUNK],
                        op0=ALU.mult, op1=ALU.add), VE, 1)

                for h in range(4):
                    emit_seg(v, c, h)
                for m in range(NPAIRH - 2):
                    emit_yev(v, c, m)
                    emit_seg(v, c, 2 * m + 4)
                    emit_seg(v, c, 2 * m + 5)
                for m in range(NPAIRH - 2, NPAIRH):
                    emit_yev(v, c, m)
                v.wait_ge(PE.h, pe_sq(c))
                inc(v.tensor_scalar(out=rr[:], in0=ctx.ps[6][0:1, 0:CHUNK],
                                    scalar1=1.0 / D_INNER, scalar2=1e-5,
                                    op0=ALU.mult, op1=ALU.add), VE, 1)
                v.wait_ge(PE.h, pe_rr(c))
                inc(v.tensor_copy(out=rrs[:], in_=ctx.ps[7][:, 0:CHUNK]), VE, 1)

        endbar(ctx, blk)
    ctx.S0_scope.close()


def phase_outproj(ctx):
    nc = ctx.nc
    dH, P_LD, P_W, P_ST, PE, VE = ctx.dH, ctx.P_LD, ctx.P_W, ctx.P_ST, ctx.PE, ctx.VE
    b_dH, b_LD, b_W, b_ST, b_PE, b_VE = (dH.v, P_LD.base(), P_W.base(), P_ST.base(),
                                         PE.v, VE.v)
    NSEG = 2
    SW = 1024

    with contextlib.ExitStack() as psx:
        yseg = psx.enter_context(nc.sbuf_tensor("op_yseg", [128, NZ * SW], BF16))
        wo = [psx.enter_context(nc.sbuf_tensor(f"op_wo{i}", [128, NZ * 128], BF16)) for i in range(2)]
        x1b = [psx.enter_context(nc.sbuf_tensor(f"op_x1{i}", [128, 512], F32)) for i in range(2)]
        xob = [psx.enter_context(nc.sbuf_tensor(f"op_xo{i}", [128, 512], F32)) for i in range(2)]
        blk = psx.enter_context(nc.Block("outproj"))

        @blk.sync
        def _(s):
            wb(s, ctx)
            for sg in range(NSEG):
                if sg >= 1:
                    s.wait_ge(PE.h, b_PE + sg * 32)
                inc(s.dma_start(out=yseg[:].rearrange("p (i t) -> p i t", i=NZ),
                                in_=ctx.y_scr[:, sg * SW:(sg + 1) * SW]
                                .rearrange("(i p) t -> p i t", p=128)), dH, 16)
                for d in range(NJ):
                    for ts2 in range(2):
                        it = sg * 32 + d * 2 + ts2
                        t0 = sg * SW + ts2 * 512
                        if it >= 2:
                            s.wait_ge(VE.h, b_VE + it - 1)
                        sp_inc(s.dma_start(out=x1b[it % 2][:],
                                           in_=ctx.x1_scr[d * 128:(d + 1) * 128,
                                                          HALO + t0:HALO + t0 + 512]),
                               P_LD, it)

        @blk.gpsimd
        def _(g):
            wb(g, ctx)
            for sg in range(NSEG):
                for d in range(NJ):
                    kd = sg * NJ + d
                    if kd >= 2:
                        g.wait_ge(PE.h, b_PE + (kd - 1) * 2)
                    sp_inc(g.dma_start(out=wo[d % 2][:].rearrange("p (i d) -> p i d", i=NZ),
                                       in_=ctx.wout[:, d * 128:(d + 1) * 128]
                                       .rearrange("(i p) d -> p i d", p=128)), P_W, kd)

        @blk.tensor
        def _(t):
            wb(t, ctx)
            for sg in range(NSEG):
                t.wait_ge(dH.h, b_dH + 16 * (sg + 1))
                for d in range(NJ):
                    kd = sg * NJ + d
                    sp_wait(t, P_W, b_W, kd)
                    for ts2 in range(2):
                        it = sg * 32 + d * 2 + ts2
                        pp = ctx.ps[4 + it % 2]
                        if it >= 2:
                            t.wait_ge(VE.h, b_VE + it - 1)
                        mm = None
                        for i in range(NZ):
                            mm = t.matmul(pp[:], wo[d % 2][:, i * 128:(i + 1) * 128],
                                          yseg[:, i * SW + ts2 * 512:i * SW + ts2 * 512 + 512],
                                          start=(i == 0), stop=(i == NZ - 1))
                        inc(mm, PE, 1)

        @blk.vector
        def _(v):
            wb(v, ctx)
            for sg in range(NSEG):
                for d in range(NJ):
                    for ts2 in range(2):
                        it = sg * 32 + d * 2 + ts2
                        pp = ctx.ps[4 + it % 2]
                        v.wait_ge(PE.h, b_PE + it + 1)
                        sp_wait(v, P_LD, b_LD, it)
                        if it >= 2:
                            sp_wait(v, P_ST, b_ST, it - 2)
                        inc(v.tensor_add(out=xob[it % 2][:], in0=pp[:],
                                         in1=x1b[it % 2][:]), VE, 1)

        @blk.scalar
        def _(sc):
            wb(sc, ctx)
            for sg in range(NSEG):
                for d in range(NJ):
                    for ts2 in range(2):
                        it = sg * 32 + d * 2 + ts2
                        t0 = sg * SW + ts2 * 512
                        sc.wait_ge(VE.h, b_VE + it + 1)
                        sp_inc(sc.dma_start(out=ctx.x2_scr[d * 128:(d + 1) * 128,
                                                           t0:t0 + 512],
                                            in_=xob[it % 2][:]), P_ST, it)

        endbar(ctx, blk)


def phase_dbg(ctx):
    nc = ctx.nc
    dL = ctx.dC
    b_dL = dL.v
    outs = []
    for nm in [d for d in DBG if d != "xln"]:
        src = getattr(ctx, nm)
        outs.append((nc.dram_tensor("dbg_" + nm, list(src.shape), src.dtype,
                                    kind="ExternalOutput").ap(), src))
    with nc.Block("dbg") as blk:
        @blk.sync
        def _(s):
            wb(s, ctx)
            n = 0
            for dst, src in outs:
                inc(s.dma_start(out=dst, in_=src), dL, 16)
                n += 1
            s.wait_ge(dL.h, b_dL + 16 * n)


_NC_CACHE = {}


def _get_nc():
    key = (tuple(DBG), STOP_AFTER)
    if key not in _NC_CACHE:
        _NC_CACHE[key] = build_program()
    return _NC_CACHE[key]


def kernel(L0_attn_u, L0_attn_W_in, L0_attn_W_conv, L0_attn_b_conv,
           L0_attn_dt_bias, L0_attn_A_log, L0_attn_Dp, L0_attn_norm_w,
           L0_attn_W_out, L0_ffn_w_gate_up, L0_ffn_w_down):
    f32 = np.float32
    u = np.asarray(L0_attn_u, f32)
    ut = np.ascontiguousarray(u.transpose(0, 2, 1))
    wgu = np.ascontiguousarray(np.asarray(L0_ffn_w_gate_up, f32).T)
    wdn = np.ascontiguousarray(np.asarray(L0_ffn_w_down, f32).T)
    win = np.ascontiguousarray(np.asarray(L0_attn_W_in, f32).T)
    wout = np.ascontiguousarray(np.asarray(L0_attn_W_out, f32).T)
    wconv = np.ascontiguousarray(
        np.asarray(L0_attn_W_conv, f32).reshape(NDC, 128, 4).transpose(1, 0, 2)
        .reshape(128, NDC * 4))
    bconv = np.ascontiguousarray(np.asarray(L0_attn_b_conv, f32).reshape(NDC, 128).T)
    dtb = np.asarray(L0_attn_dt_bias, f32).reshape(NHEADS, 1)
    alog = np.asarray(L0_attn_A_log, f32).reshape(NHEADS, 1)
    dprow = np.asarray(L0_attn_Dp, f32).reshape(1, NHEADS)
    nwin = np.ascontiguousarray(np.asarray(L0_attn_norm_w, f32).reshape(NZ, 128).T)

    in_maps = []
    for core in range(8):
        b, q = divmod(core, 4)
        u_pad = np.zeros((D_MODEL, W_A), f32)
        t0 = q * TOK
        lo = max(0, t0 - HALO)
        u_pad[:, HALO - (t0 - lo):] = ut[b][:, lo:t0 + TOK]
        btwT = np.zeros((8, 8), f32)
        predc = np.zeros((8, 1), f32)
        for j in range(8):
            if j // 4 == b and j % 4 < q:
                predc[j, 0] = 1.0
                for i in range(8):
                    if i // 4 == b and (j % 4) < (i % 4) < q:
                        btwT[i, j] = 1.0
        in_maps.append({
            "u": np.ascontiguousarray(u_pad), "wgu": wgu, "wdn": wdn, "win": win,
            "wout": wout, "wconv": wconv, "bconv": bconv, "dtb": dtb, "alog": alog,
            "dprow": dprow, "nwin": nwin, "btwT": btwT, "predc": predc,
        })

    nc = _get_nc()
    res = run_bass_kernel_spmd(nc, in_maps, core_ids=list(range(8)))
    out = np.empty((2, 8192, 2048), f32)
    for core in range(8):
        b, q = divmod(core, 4)
        out[b, q * TOK:(q + 1) * TOK, :] = np.asarray(res.results[core]["o"]).T
    if DBG:
        kernel._dbg = [{("dbg_" + nm): np.asarray(res.results[c]["dbg_" + nm])
                        for nm in DBG} for c in range(8)]
    return out


# version 03
# speedup vs baseline: 1.1215x; 1.1215x over previous
import contextlib
import numpy as np
import ml_dtypes
import concourse.bass as bass
import concourse.mybir as mybir
from concourse.bass_utils import run_bass_kernel_spmd

F32 = mybir.dt.float32
BF16 = mybir.dt.bfloat16
AF = mybir.ActivationFunctionType
ALU = mybir.AluOpType
bf16 = ml_dtypes.bfloat16

D_MODEL = 2048
D_INNER = 4096
D_STATE = 128
NHEADS = 64
HEADDIM = 64
CONV_DIM = D_INNER + 2 * D_STATE
D_IN_PROJ = 2 * D_INNER + 2 * D_STATE + NHEADS
INTER = 8192
TOK = 2048
HALO = 3
W_A = TOK + HALO
CHUNK = 128
NC = TOK // CHUNK
NJ = D_MODEL // 128
NI = INTER // 128
NZ = D_INNER // 128
NDC = CONV_DIM // 128
NO_IN = 67
MASK_NEG = -30000.0
PUBN = 128 * D_INNER + NHEADS

TSUBS_A = [(0, 512), (512, 512), (1024, 512), (1536, 256), (1792, 259)]
TSUBS_F = [(0, 512), (512, 512), (1024, 512), (1536, 512)]

DBG = []
STOP_AFTER = None


class Sem:
    def __init__(self, nc, es, name):
        self.h = es.enter_context(nc.semaphore(name))
        self.v = 0


def inc(instr, sem, n=1):
    instr.then_inc(sem.h, n)
    sem.v += n
    return instr


class SPair:

    def __init__(self, nc, es, name):
        self.s = [Sem(nc, es, name + "a"), Sem(nc, es, name + "b")]

    def base(self):
        return (self.s[0].v, self.s[1].v)


def sp_inc(instr, sp, g, n=16):
    inc(instr, sp.s[g % 2], n)


def sp_wait(eng, sp, b, g, n=16):
    eng.wait_ge(sp.s[g % 2].h, b[g % 2] + n * (g // 2 + 1))


class Ctx:
    pass


def build_program():
    nc = bass.Bass("TRN2", target_bir_lowering=False, debug=False)
    ctx = Ctx()
    ctx.nc = nc
    es = contextlib.ExitStack()
    ctx.es = es
    ap = lambda *a, **k: nc.dram_tensor(*a, **k).ap()

    ctx.u = ap("u", [D_MODEL, W_A], F32, kind="ExternalInput")
    ctx.wgu = ap("wgu", [D_MODEL, 2 * INTER], F32, kind="ExternalInput")
    ctx.wdn = ap("wdn", [INTER, D_MODEL], F32, kind="ExternalInput")
    ctx.win = ap("win", [D_MODEL, D_IN_PROJ], F32, kind="ExternalInput")
    ctx.wout = ap("wout", [D_INNER, D_MODEL], F32, kind="ExternalInput")
    ctx.wconv = ap("wconv", [128, NDC * 4], F32, kind="ExternalInput")
    ctx.bconv = ap("bconv", [128, NDC], F32, kind="ExternalInput")
    ctx.dtb = ap("dtb", [NHEADS, 1], F32, kind="ExternalInput")
    ctx.alog = ap("alog", [NHEADS, 1], F32, kind="ExternalInput")
    ctx.dprow = ap("dprow", [1, NHEADS], F32, kind="ExternalInput")
    ctx.nwin = ap("nwin", [128, NZ], F32, kind="ExternalInput")
    ctx.btwT = ap("btwT", [8, 8], F32, kind="ExternalInput")
    ctx.predc = ap("predc", [8, 1], F32, kind="ExternalInput")
    ctx.o = ap("o", [D_MODEL, TOK], F32, kind="ExternalOutput")
    ctx.h_scr = ap("h_scr", [INTER, W_A], BF16)
    ctx.x1_scr = ap("x1_scr", [D_MODEL, W_A], F32)
    ctx.z_scr = ap("z_scr", [D_INNER, TOK], BF16)
    ctx.xbcp_scr = ap("xbcp_scr", [CONV_DIM, W_A], BF16)
    ctx.xbc_scr = ap("xbc_scr", [CONV_DIM, TOK], BF16)
    ctx.sel_scr = ap("sel_scr", [NC * 128, D_INNER], F32)
    ctx.y_scr = ap("y_scr", [D_INNER, TOK], BF16)
    ctx.x2_scr = ap("x2_scr", [D_MODEL, TOK], F32)
    ctx.cums_scr = ap("cums_scr", [NHEADS, TOK], F32)
    ctx.rows_scr = ap("rows_scr", [64, NHEADS], F32)
    ctx.pub_scr = ap("pub_scr", [1, PUBN], F32)
    ctx.gat_scr = ap("gat_scr", [8, PUBN], F32)

    maskb_np = np.where(np.arange(CHUNK)[None, :] >= np.arange(CHUNK)[:, None],
                        0.0, MASK_NEG).astype(np.float32)
    ctx.maskb_c = nc.inline_tensor(maskb_np, name="c_maskb").ap()
    ctx.onesrow_c = nc.inline_tensor(np.ones((1, 128), np.float32), name="c_onesrow").ap()
    ctx.onescolbf_c = nc.inline_tensor(np.ones((128, 1), bf16), name="c_onescolbf").ap()
    ctx.idf32_c = nc.inline_tensor(np.eye(128, dtype=np.float32), name="c_idf32").ap()
    segm = np.ones((NHEADS, TOK), np.float32)
    segm[:, ::CHUNK] = 0.0
    ctx.segmask_c = nc.inline_tensor(segm, name="c_segmask").ap()
    ctx.zcol_c = nc.inline_tensor(np.zeros((NHEADS, 1), np.float32), name="c_zcol").ap()
    halves = np.zeros((2, 128), np.float32)
    halves[0, 0:64] = 1.0
    halves[1, 64:128] = 1.0
    ctx.halves_c = nc.inline_tensor(halves, name="c_halves").ap()

    sb = lambda name, shape, dt: es.enter_context(nc.sbuf_tensor(name, shape, dt))
    ctx.dtraw = sb("g_dtraw", [NHEADS, W_A], F32)
    ctx.Ab = sb("g_Ab", [NHEADS, TOK], F32)
    ctx.cumA = sb("g_cumA", [NHEADS, TOK], F32)
    ctx.tots = sb("g_tots", [NHEADS, NC], F32)
    ctx.ptot = sb("g_ptot", [NHEADS, NC + 1], F32)
    ctx.dp_pair = sb("g_dp_pair", [128, NHEADS // 2], F32)
    ctx.maskb = sb("g_maskb", [CHUNK, CHUNK], F32)
    ctx.onesrow = sb("g_onesrow", [1, 128], F32)
    ctx.onescolbf = sb("g_onescolbf", [128, 1], BF16)
    ctx.idf32 = sb("g_idf32", [128, 128], F32)
    ctx.segmask = sb("g_segmask", [NHEADS, TOK], F32)
    ctx.cols = sb("g_cols", [NHEADS, 4], F32)
    ctx.nw = sb("g_nw", [128, NZ], F32)
    ctx.dprow_sb = sb("g_dprow", [1, NHEADS], F32)
    ctx.dppk_sb = sb("g_dppk", [2, NHEADS // 2], F32)
    ctx.halves_sb = sb("g_halves", [2, 128], F32)

    ctx.ps = [es.enter_context(nc.psum_tensor(f"ps{i}", [128, 512], F32)) for i in range(8)]

    for s in ("dC", "dH", "PE", "VE", "SE", "PO", "CC", "BAR"):
        setattr(ctx, s, Sem(nc, es, "s_" + s))
    ctx.P_LD = SPair(nc, es, "p_ld")
    ctx.P_W = SPair(nc, es, "p_w")
    ctx.P_ST = SPair(nc, es, "p_st")
    ctx.P_X = SPair(nc, es, "p_x")
    ctx.P_PS = SPair(nc, es, "p_ps")
    ctx.P_H = SPair(nc, es, "p_h")
    ctx.bar_n = 0

    phases = [
        ("init", lambda: phase_init(ctx)),
        ("ln1", lambda: phase_ln(ctx, src=ctx.u, width=W_A, name="ln1")),
        ("gu1", lambda: phase_gateup(ctx, width=W_A, name="gu1")),
        ("dn1", lambda: phase_down(ctx, res_src=ctx.u, dst=ctx.x1_scr, width=W_A,
                                   name="dn1")),
        ("ln2", lambda: phase_ln(ctx, src=ctx.x1_scr, width=W_A, name="ln2")),
        ("inproj", lambda: phase_inproj(ctx)),
        ("dt", lambda: phase_dt(ctx)),
        ("conv", lambda: phase_conv(ctx)),
        ("d1", lambda: phase_d1(ctx)),
        ("exchange", lambda: phase_exchange(ctx)),
        ("d2", lambda: phase_d2(ctx)),
        ("outproj", lambda: phase_outproj(ctx)),
        ("ln3", lambda: phase_ln(ctx, src=ctx.x2_scr, width=TOK, name="ln3")),
        ("gu2", lambda: phase_gateup(ctx, width=TOK, name="gu2")),
        ("dn2", lambda: phase_down(ctx, res_src=ctx.x2_scr, dst=ctx.o, width=TOK,
                                   name="dn2")),
    ]
    for pname, fn in phases:
        fn()
        if STOP_AFTER == pname:
            break
    if "xln" in DBG and ctx.xln is not None:
        w = ctx.xln_w
        xs = nc.dram_tensor("dbg_xln", [128, NJ * w], BF16, kind="ExternalOutput").ap()
        b_dL = ctx.dC.v
        with nc.Block("dumpx") as blkx:
            @blkx.sync
            def _(s):
                wb(s, ctx)
                inc(s.dma_start(out=xs, in_=ctx.xln[:]), ctx.dC, 16)
                s.wait_ge(ctx.dC.h, b_dL + 16)
    if [d for d in DBG if d != "xln"]:
        phase_dbg(ctx)
    if getattr(ctx, "xln", None) is not None:
        ctx.xln_scope.close()
    if getattr(ctx, "S0", None) is not None and not getattr(ctx, "_s0_closed", False):
        try:
            ctx.S0_scope.close()
        except Exception:
            pass
    es.close()
    return nc


def endbar(ctx, blk, waits=None):
    sems = [ctx.dC, ctx.dH, ctx.PE, ctx.VE, ctx.SE, ctx.PO, ctx.CC]
    for p in (ctx.P_LD, ctx.P_W, ctx.P_ST, ctx.P_X, ctx.P_PS, ctx.P_H):
        sems += p.s
    targets = [(s.h, s.v) for s in sems]
    ctx.bar_n += 5
    for name in ("sync", "gpsimd", "tensor", "scalar", "vector"):
        def body(eng):
            for h, v in targets:
                if v:
                    eng.wait_ge(h, v)
            eng.sem_inc(ctx.BAR.h, 1)
        getattr(blk, name)(body)
    ctx.BAR.v += 5


def wb(eng, ctx):
    if ctx.bar_n:
        eng.wait_ge(ctx.BAR.h, ctx.bar_n)


def phase_init(ctx):
    nc = ctx.nc
    dC, PE, VE, SE = ctx.dC, ctx.PE, ctx.VE, ctx.SE
    b_dC, b_PE, b_VE, b_SE = dC.v, PE.v, VE.v, SE.v
    with nc.Block("init") as blk:
        @blk.sync
        def _(s):
            inc(s.dma_start(out=ctx.maskb[:], in_=ctx.maskb_c), dC, 16)
            inc(s.dma_start(out=ctx.onesrow[:], in_=ctx.onesrow_c), dC, 16)
            inc(s.dma_start(out=ctx.onescolbf[:], in_=ctx.onescolbf_c), dC, 16)
            inc(s.dma_start(out=ctx.idf32[:], in_=ctx.idf32_c), dC, 16)
            inc(s.dma_start(out=ctx.segmask[:], in_=ctx.segmask_c), dC, 16)
            inc(s.dma_start(out=ctx.cols[:, 0:1], in_=ctx.dtb), dC, 16)
            inc(s.dma_start(out=ctx.cols[:, 2:3], in_=ctx.alog), dC, 16)
            inc(s.dma_start(out=ctx.cols[:, 3:4], in_=ctx.zcol_c), dC, 16)
            inc(s.dma_start(out=ctx.nw[:], in_=ctx.nwin), dC, 16)
            inc(s.dma_start(out=ctx.dprow_sb[:], in_=ctx.dprow), dC, 16)
            inc(s.dma_start(out=ctx.halves_sb[:], in_=ctx.halves_c), dC, 16)
            inc(s.dma_start(out=ctx.dppk_sb[:],
                            in_=ctx.dprow.rearrange("u (m par) -> (u par) m", par=2),
                            allow_slow_non_contiguous=True), dC, 16)
        NL = 12

        @blk.scalar
        def _(sc):
            sc.wait_ge(dC.h, b_dC + 16 * NL)
            inc(sc.activation(out=ctx.cols[:, 1:2], in_=ctx.cols[:, 2:3], func=AF.Exp),
                SE, 1)

        @blk.tensor
        def _(t):
            t.wait_ge(dC.h, b_dC + 16 * NL)
            inc(t.matmul(ctx.ps[0][:, 0:NHEADS // 2], ctx.halves_sb[:], ctx.dppk_sb[:],
                         start=True, stop=True), PE, 1)

        @blk.vector
        def _(v):
            v.wait_ge(SE.h, b_SE + 1)
            v.tensor_scalar_mul(out=ctx.cols[:, 1:2], in0=ctx.cols[:, 1:2], scalar1=-1.0)
            v.wait_ge(PE.h, b_PE + 1)
            inc(v.tensor_copy(out=ctx.dp_pair[:], in_=ctx.ps[0][:, 0:NHEADS // 2]),
                VE, 1)

        endbar(ctx, blk)


def phase_ln(ctx, src, width, name):
    nc = ctx.nc
    tsubs = TSUBS_A if width == W_A else TSUBS_F
    NT = len(tsubs)
    P_LD, PE, VE, SE = ctx.P_LD, ctx.PE, ctx.VE, ctx.SE
    b_LD, b_PE, b_VE, b_SE = P_LD.base(), PE.v, VE.v, SE.v

    ctx.xln_scope = contextlib.ExitStack()
    ctx.xln = ctx.xln_scope.enter_context(
        nc.sbuf_tensor(f"{name}_xln", [128, NJ * width], BF16))
    ctx.xln_w = width

    with contextlib.ExitStack() as psx:
        ubuf = psx.enter_context(nc.sbuf_tensor(f"{name}_ubuf", [128, NJ * 512], F32))
        xb = [psx.enter_context(nc.sbuf_tensor(f"{name}_xb{i}", [128, 512], BF16)) for i in range(2)]
        sq = [psx.enter_context(nc.sbuf_tensor(f"{name}_sq{i}", [128, 512], BF16)) for i in range(2)]
        rows = psx.enter_context(nc.sbuf_tensor(f"{name}_rows", [1, 1536], F32))
        blk = psx.enter_context(nc.Block(name))

        @blk.sync
        def _(s):
            wb(s, ctx)
            for ti, (t0, tw) in enumerate(tsubs):
                if ti >= 1:
                    s.wait_ge(VE.h, b_VE + ti * 5)
                sp_inc(s.dma_start(
                    out=ubuf[:].rearrange("p (j t) -> p j t", j=NJ)[:, :, 0:tw],
                    in_=src[:, t0:t0 + tw].rearrange("(j p) t -> p j t", p=128)),
                    P_LD, ti)

        @blk.scalar
        def _(sc):
            wb(sc, ctx)
            for ti, (t0, tw) in enumerate(tsubs):
                sp_wait(sc, P_LD, b_LD, ti)
                for j in range(NJ):
                    bb = j % 2
                    gi = ti * NJ + j
                    if gi >= 2:
                        sc.wait_ge(PE.h, b_PE + (gi - 2) // NJ * 17 + (gi - 2) % NJ + 1)
                    uj = ubuf[:, j * 512:j * 512 + tw]
                    inc(sc.activation(out=xb[bb][:, 0:tw], in_=uj, func=AF.Copy), SE, 1)
                    inc(sc.activation(out=sq[bb][:, 0:tw], in_=uj, func=AF.Square), SE, 1)
                sc.wait_ge(VE.h, b_VE + ti * 5 + 3)
                inc(sc.activation(out=rows[0:1, 1024:1024 + tw],
                                  in_=rows[0:1, 1024:1024 + tw], func=AF.Ln), SE, 1)
                sc.wait_ge(SE.h, b_SE + ti * 35 + 33)
                inc(sc.activation(out=rows[0:1, 1024:1024 + tw],
                                  in_=rows[0:1, 1024:1024 + tw], func=AF.Exp,
                                  scale=-0.5), SE, 2)

        @blk.tensor
        def _(t):
            wb(t, ctx)
            for ti, (t0, tw) in enumerate(tsubs):
                if ti >= 1:
                    t.wait_ge(VE.h, b_VE + (ti - 1) * 5 + 2)
                for j in range(NJ):
                    bb = j % 2
                    t.wait_ge(SE.h, b_SE + ti * 35 + 2 * (j + 1))
                    t.matmul(ctx.ps[0][0:1, 0:tw], ctx.onescolbf[:], xb[bb][:, 0:tw],
                             start=(j == 0), stop=(j == NJ - 1))
                    inc(t.matmul(ctx.ps[1][0:1, 0:tw], ctx.onescolbf[:], sq[bb][:, 0:tw],
                                 start=(j == 0), stop=(j == NJ - 1)), PE, 1)
                t.wait_ge(SE.h, b_SE + (ti + 1) * 35)
                if ti >= 1:
                    t.wait_ge(VE.h, b_VE + ti * 5)
                t.matmul(ctx.ps[2][:, 0:tw], ctx.onesrow[:], rows[0:1, 0:tw],
                         start=True, stop=True)
                inc(t.matmul(ctx.ps[3][:, 0:tw], ctx.onesrow[:], rows[0:1, 1024:1024 + tw],
                             start=True, stop=True), PE, 1)

        @blk.vector
        def _(v):
            wb(v, ctx)
            for ti, (t0, tw) in enumerate(tsubs):
                v.wait_ge(PE.h, b_PE + ti * 17 + 16)
                v.tensor_scalar_mul(out=rows[0:1, 0:tw], in0=ctx.ps[0][0:1, 0:tw],
                                    scalar1=1.0 / D_MODEL)
                inc(v.tensor_scalar_mul(out=rows[0:1, 512:512 + tw],
                                        in0=ctx.ps[1][0:1, 0:tw],
                                        scalar1=1.0 / D_MODEL), VE, 1)
                v.wait_ge(VE.h, b_VE + ti * 5 + 1)
                inc(v.tensor_mul(out=rows[0:1, 1024:1024 + tw], in0=rows[0:1, 0:tw],
                                 in1=rows[0:1, 0:tw]), VE, 1)
                v.wait_ge(VE.h, b_VE + ti * 5 + 2)
                inc(v.scalar_tensor_tensor(out=rows[0:1, 1024:1024 + tw],
                                           in0=rows[0:1, 512:512 + tw], scalar=1e-5,
                                           in1=rows[0:1, 1024:1024 + tw],
                                           op0=ALU.add, op1=ALU.subtract), VE, 1)
                v.wait_ge(PE.h, b_PE + (ti + 1) * 17)
                last = None
                for j in range(NJ):
                    uj = ubuf[:, j * 512:j * 512 + tw]
                    last = v.tensor_sub(out=uj, in0=uj, in1=ctx.ps[2][:, 0:tw])
                inc(last, VE, 1)
                v.wait_ge(VE.h, b_VE + ti * 5 + 4)
                last = None
                for j in range(NJ):
                    uj = ubuf[:, j * 512:j * 512 + tw]
                    xo = ctx.xln[:, j * width + t0:j * width + t0 + tw]
                    last = v.tensor_mul(out=xo, in0=uj, in1=ctx.ps[3][:, 0:tw])
                inc(last, VE, 1)

        endbar(ctx, blk)


def phase_gateup(ctx, width, name):
    nc = ctx.nc
    tsubs = TSUBS_A if width == W_A else TSUBS_F
    NT = len(tsubs)
    NP = NI
    P_W, P_ST, PE, VE, SE = ctx.P_W, ctx.P_ST, ctx.PE, ctx.VE, ctx.SE
    b_W, b_ST, b_PE, b_VE, b_SE = P_W.base(), P_ST.base(), PE.v, VE.v, SE.v

    with contextlib.ExitStack() as psx:
        wg = [psx.enter_context(nc.sbuf_tensor(f"{name}_wg{i}", [128, NJ * 128], BF16)) for i in range(2)]
        wu = [psx.enter_context(nc.sbuf_tensor(f"{name}_wu{i}", [128, NJ * 128], BF16)) for i in range(2)]
        sg = [psx.enter_context(nc.sbuf_tensor(f"{name}_sg{i}", [128, 512], BF16)) for i in range(2)]
        hb = [psx.enter_context(nc.sbuf_tensor(f"{name}_hb{i}", [128, width], BF16)) for i in range(2)]
        blk = psx.enter_context(nc.Block(name))

        @blk.gpsimd
        def _(g):
            wb(g, ctx)
            for k in range(NP):
                bb = k % 2
                if k >= 2:
                    g.wait_ge(PE.h, b_PE + (k - 1) * NT)
                sp_inc(g.dma_start(out=wg[bb][:].rearrange("p (j o) -> p j o", j=NJ),
                                   in_=ctx.wgu[:, k * 128:(k + 1) * 128]
                                   .rearrange("(j p) o -> p j o", p=128)), P_W, k)
                sp_inc(g.dma_start(out=wu[bb][:].rearrange("p (j o) -> p j o", j=NJ),
                                   in_=ctx.wgu[:, INTER + k * 128:INTER + (k + 1) * 128]
                                   .rearrange("(j p) o -> p j o", p=128)), P_W, k)

        @blk.tensor
        def _(t):
            wb(t, ctx)
            for k in range(NP):
                bb = k % 2
                sp_wait(t, P_W, b_W, k, 32)
                for ti, (t0, tw) in enumerate(tsubs):
                    it = k * NT + ti
                    pg = ctx.ps[4 + 2 * (it % 2)]
                    pu = ctx.ps[5 + 2 * (it % 2)]
                    if it >= 2:
                        t.wait_ge(VE.h, b_VE + it - 1)
                    mm = None
                    for j in range(NJ):
                        t.matmul(pg[:, 0:tw], wg[bb][:, j * 128:(j + 1) * 128],
                                 ctx.xln[:, j * width + t0:j * width + t0 + tw],
                                 start=(j == 0), stop=(j == NJ - 1))
                        mm = t.matmul(pu[:, 0:tw], wu[bb][:, j * 128:(j + 1) * 128],
                                      ctx.xln[:, j * width + t0:j * width + t0 + tw],
                                      start=(j == 0), stop=(j == NJ - 1))
                    inc(mm, PE, 1)

        @blk.scalar
        def _(sc):
            wb(sc, ctx)
            for k in range(NP):
                for ti, (t0, tw) in enumerate(tsubs):
                    it = k * NT + ti
                    pg = ctx.ps[4 + 2 * (it % 2)]
                    sc.wait_ge(PE.h, b_PE + it + 1)
                    if it >= 2:
                        sc.wait_ge(VE.h, b_VE + it - 1)
                    inc(sc.activation(out=sg[it % 2][:, 0:tw], in_=pg[:, 0:tw],
                                      func=AF.Silu), SE, 1)

        @blk.vector
        def _(v):
            wb(v, ctx)
            for k in range(NP):
                hbuf = hb[k % 2]
                if k >= 2:
                    sp_wait(v, P_ST, b_ST, k - 2)
                for ti, (t0, tw) in enumerate(tsubs):
                    it = k * NT + ti
                    pu = ctx.ps[5 + 2 * (it % 2)]
                    v.wait_ge(SE.h, b_SE + it + 1)
                    inc(v.tensor_mul(out=hbuf[:, t0:t0 + tw], in0=sg[it % 2][:, 0:tw],
                                     in1=pu[:, 0:tw]), VE, 1)

        @blk.sync
        def _(s):
            wb(s, ctx)
            for k in range(NP):
                s.wait_ge(VE.h, b_VE + (k + 1) * NT)
                sp_inc(s.dma_start(out=ctx.h_scr[k * 128:(k + 1) * 128, 0:width],
                                   in_=hb[k % 2][:]), P_ST, k)

        endbar(ctx, blk)
    ctx.xln_scope.close()
    ctx.xln = None


def phase_down(ctx, res_src, dst, width, name):
    nc = ctx.nc
    tsubs = TSUBS_A if width == W_A else TSUBS_F
    NT = len(tsubs)
    P_H, P_LD, P_W, P_ST, PE, VE = ctx.P_H, ctx.P_LD, ctx.P_W, ctx.P_ST, ctx.PE, ctx.VE
    b_H, b_LD, b_W, b_ST, b_PE, b_VE = P_H.base(), P_LD.base(), P_W.base(), P_ST.base(), PE.v, VE.v

    with contextlib.ExitStack() as psx:
        hseg = [psx.enter_context(nc.sbuf_tensor(f"{name}_hseg{i}", [128, NI * 512], BF16))
                for i in range(2)]
        wd = [psx.enter_context(nc.sbuf_tensor(f"{name}_wd{i}", [128, NI * 128], BF16)) for i in range(2)]
        ub = [psx.enter_context(nc.sbuf_tensor(f"{name}_ub{i}", [128, 512], F32)) for i in range(2)]
        xo = [psx.enter_context(nc.sbuf_tensor(f"{name}_xo{i}", [128, 512], F32)) for i in range(2)]
        blk = psx.enter_context(nc.Block(name))

        @blk.sync
        def _(s):
            wb(s, ctx)
            for ti, (t0, tw) in enumerate(tsubs):
                if ti >= 2:
                    s.wait_ge(PE.h, b_PE + (ti - 1) * NJ)
                sp_inc(s.dma_start(
                    out=hseg[ti % 2][:].rearrange("p (i t) -> p i t", i=NI)[:, :, 0:tw],
                    in_=ctx.h_scr[:, t0:t0 + tw].rearrange("(i p) t -> p i t", p=128)),
                    P_H, ti)
                for d in range(NJ):
                    it = ti * NJ + d
                    if it >= 2:
                        s.wait_ge(VE.h, b_VE + it - 1)
                    sp_inc(s.dma_start(out=ub[d % 2][:, 0:tw],
                                       in_=res_src[d * 128:(d + 1) * 128, t0:t0 + tw]),
                           P_LD, it)

        @blk.gpsimd
        def _(g):
            wb(g, ctx)
            for ti in range(NT):
                for d in range(NJ):
                    it = ti * NJ + d
                    if it >= 2:
                        g.wait_ge(PE.h, b_PE + it - 1)
                    sp_inc(g.dma_start(out=wd[d % 2][:].rearrange("p (i d) -> p i d", i=NI),
                                       in_=ctx.wdn[:, d * 128:(d + 1) * 128]
                                       .rearrange("(i p) d -> p i d", p=128)), P_W, it)

        @blk.tensor
        def _(t):
            wb(t, ctx)
            for ti, (t0, tw) in enumerate(tsubs):
                sp_wait(t, P_H, b_H, ti)
                for d in range(NJ):
                    it = ti * NJ + d
                    pd = ctx.ps[4 + it % 2]
                    sp_wait(t, P_W, b_W, it)
                    if it >= 2:
                        t.wait_ge(VE.h, b_VE + it - 1)
                    mm = None
                    for i in range(NI):
                        mm = t.matmul(pd[:, 0:tw], wd[d % 2][:, i * 128:(i + 1) * 128],
                                      hseg[ti % 2][:, i * 512:i * 512 + tw],
                                      start=(i == 0), stop=(i == NI - 1))
                    inc(mm, PE, 1)

        @blk.vector
        def _(v):
            wb(v, ctx)
            for ti, (t0, tw) in enumerate(tsubs):
                for d in range(NJ):
                    it = ti * NJ + d
                    pd = ctx.ps[4 + it % 2]
                    v.wait_ge(PE.h, b_PE + it + 1)
                    sp_wait(v, P_LD, b_LD, it)
                    if it >= 2:
                        sp_wait(v, P_ST, b_ST, it - 2)
                    inc(v.scalar_tensor_tensor(out=xo[it % 2][:, 0:tw], in0=pd[:, 0:tw],
                                               scalar=0.5, in1=ub[d % 2][:, 0:tw],
                                               op0=ALU.mult, op1=ALU.add), VE, 1)

        @blk.scalar
        def _(sc):
            wb(sc, ctx)
            for ti, (t0, tw) in enumerate(tsubs):
                for d in range(NJ):
                    it = ti * NJ + d
                    sc.wait_ge(VE.h, b_VE + it + 1)
                    sp_inc(sc.dma_start(out=dst[d * 128:(d + 1) * 128, t0:t0 + tw],
                                        in_=xo[it % 2][:, 0:tw]), P_ST, it)

        endbar(ctx, blk)


def phase_inproj(ctx):
    nc = ctx.nc
    tsubs = TSUBS_A
    NT = len(tsubs)
    P_W, P_ST, PE, SE = ctx.P_W, ctx.P_ST, ctx.PE, ctx.SE
    b_W, b_ST, b_PE, b_SE = P_W.base(), P_ST.base(), PE.v, SE.v

    with contextlib.ExitStack() as psx:
        wbuf = [psx.enter_context(nc.sbuf_tensor(f"ip_wb{i}", [128, NJ * 128], BF16)) for i in range(2)]
        ob = [psx.enter_context(nc.sbuf_tensor(f"ip_ob{i}", [128, W_A], BF16)) for i in range(2)]
        blk = psx.enter_context(nc.Block("inproj"))

        @blk.gpsimd
        def _(g):
            wb(g, ctx)
            for k in range(NO_IN):
                ow = 128 if k < 66 else 64
                if k >= 2:
                    g.wait_ge(PE.h, b_PE + (k - 1) * NT)
                sp_inc(g.dma_start(out=wbuf[k % 2][:].rearrange("p (j o) -> p j o", j=NJ)
                                   [:, :, 0:ow],
                                   in_=ctx.win[:, k * 128:k * 128 + ow]
                                   .rearrange("(j p) o -> p j o", p=128)), P_W, k)

        @blk.tensor
        def _(t):
            wb(t, ctx)
            for k in range(NO_IN):
                ow = 128 if k < 66 else 64
                sp_wait(t, P_W, b_W, k)
                for ti, (t0, tw) in enumerate(tsubs):
                    it = k * NT + ti
                    pp = ctx.ps[4 + it % 2]
                    if it >= 2:
                        t.wait_ge(SE.h, b_SE + it - 1)
                    mm = None
                    for j in range(NJ):
                        mm = t.matmul(pp[0:ow, 0:tw], wbuf[k % 2][:, j * 128:j * 128 + ow],
                                      ctx.xln[:, j * W_A + t0:j * W_A + t0 + tw],
                                      start=(j == 0), stop=(j == NJ - 1))
                    inc(mm, PE, 1)

        @blk.scalar
        def _(sc):
            wb(sc, ctx)
            for k in range(NO_IN):
                ow = 128 if k < 66 else 64
                for ti, (t0, tw) in enumerate(tsubs):
                    it = k * NT + ti
                    pp = ctx.ps[4 + it % 2]
                    sc.wait_ge(PE.h, b_PE + it + 1)
                    if k >= 2 and ti == 0 and k < 66:
                        sp_wait(sc, P_ST, b_ST, k - 2)
                    if k == 66:
                        inc(sc.activation(out=ctx.dtraw[:, t0:t0 + tw],
                                          in_=pp[0:64, 0:tw], func=AF.Copy), SE, 1)
                    else:
                        inc(sc.activation(out=ob[k % 2][:, t0:t0 + tw], in_=pp[:, 0:tw],
                                          func=AF.Copy), SE, 1)

        @blk.sync
        def _(s):
            wb(s, ctx)
            for k in range(66):
                s.wait_ge(SE.h, b_SE + (k + 1) * NT)
                if k < 32:
                    sp_inc(s.dma_start(out=ctx.z_scr[k * 128:(k + 1) * 128, :],
                                       in_=ob[k % 2][:, HALO:W_A]), P_ST, k)
                else:
                    kk = k - 32
                    sp_inc(s.dma_start(out=ctx.xbcp_scr[kk * 128:(kk + 1) * 128, :],
                                       in_=ob[k % 2][:]), P_ST, k)

        endbar(ctx, blk)
    ctx.xln_scope.close()
    ctx.xln = None


def phase_dt(ctx):
    nc = ctx.nc
    dC, PE, VE, SE = ctx.dC, ctx.PE, ctx.VE, ctx.SE
    b_dC, b_PE, b_VE, b_SE = dC.v, PE.v, VE.v, SE.v

    with contextlib.ExitStack() as psx:
        tmp = psx.enter_context(nc.sbuf_tensor("dt_tmp", [NHEADS, TOK], F32))
        tmp2 = psx.enter_context(nc.sbuf_tensor("dt_tmp2", [NHEADS, TOK], F32))
        trow = psx.enter_context(nc.sbuf_tensor("dt_trow", [16, NHEADS], F32))
        trow2 = psx.enter_context(nc.sbuf_tensor("dt_trow2", [16, NHEADS], F32))
        trow1 = psx.enter_context(nc.sbuf_tensor("dt_trow1", [1, NHEADS], F32))
        blk = psx.enter_context(nc.Block("dt"))

        @blk.vector
        def _(v):
            wb(v, ctx)
            inc(v.tensor_scalar_add(out=tmp[:], in0=ctx.dtraw[:, HALO:W_A],
                                    scalar1=ctx.cols[:, 0:1]), VE, 1)
            v.wait_ge(SE.h, b_SE + 3)
            inc(v.scalar_tensor_tensor(out=tmp[:], in0=tmp[:], scalar=0.0,
                                       in1=tmp2[:], op0=ALU.max, op1=ALU.add), VE, 1)
            v.wait_ge(VE.h, b_VE + 2)
            inc(v.tensor_scalar_mul(out=ctx.Ab[:], in0=tmp[:],
                                    scalar1=ctx.cols[:, 1:2]), VE, 1)
            v.wait_ge(VE.h, b_VE + 3)
            inc(v.tensor_tensor_scan(out=ctx.cumA[:], data0=ctx.segmask[:],
                                     data1=ctx.Ab[:], initial=0.0,
                                     op0=ALU.mult, op1=ALU.add), VE, 1)
            v.wait_ge(VE.h, b_VE + 4)
            inc(v.tensor_copy(out=ctx.tots[:].rearrange("h (c u) -> h c u", u=1),
                              in_=ctx.cumA[:].rearrange("h (c l) -> h c l", l=CHUNK)
                              [:, :, CHUNK - 1:CHUNK]), VE, 1)
            v.wait_ge(VE.h, b_VE + 5)
            v.tensor_tensor_scan(out=ctx.ptot[:, 1:NC + 1], data0=ctx.segmask[:, 0:NC],
                                 data1=ctx.tots[:], initial=0.0,
                                 op0=ALU.mult, op1=ALU.add)
            inc(v.tensor_copy(out=ctx.ptot[:, 0:1], in_=ctx.cols[:, 3:4]), VE, 1)
            v.wait_ge(PE.h, b_PE + 1)
            v.tensor_copy(out=trow[:], in_=ctx.ps[0][0:16, 0:NHEADS])
            v.tensor_copy(out=trow2[:], in_=ctx.ps[1][0:16, 0:NHEADS])
            inc(v.tensor_copy(out=trow1[:], in_=ctx.ps[2][0:1, 0:NHEADS]), VE, 1)

        @blk.scalar
        def _(sc):
            wb(sc, ctx)
            sc.wait_ge(VE.h, b_VE + 1)
            inc(sc.activation(out=tmp2[:], in_=tmp[:], func=AF.Abs), SE, 1)
            sc.wait_ge(SE.h, b_SE + 1)
            inc(sc.activation(out=tmp2[:], in_=tmp2[:], func=AF.Exp, scale=-1.0), SE, 1)
            sc.wait_ge(SE.h, b_SE + 2)
            inc(sc.activation(out=tmp2[:], in_=tmp2[:], func=AF.Ln, bias=1.0), SE, 1)

        @blk.tensor
        def _(t):
            wb(t, ctx)
            t.wait_ge(VE.h, b_VE + 6)
            t.transpose(ctx.ps[0][0:16, 0:NHEADS], ctx.tots[:], ctx.idf32[0:NHEADS, 0:NHEADS])
            t.transpose(ctx.ps[1][0:16, 0:NHEADS], ctx.ptot[:, 0:NC],
                        ctx.idf32[0:NHEADS, 0:NHEADS])
            inc(t.transpose(ctx.ps[2][0:1, 0:NHEADS], ctx.ptot[:, NC:NC + 1],
                            ctx.idf32[0:NHEADS, 0:NHEADS]), PE, 1)

        @blk.sync
        def _(s):
            wb(s, ctx)
            s.wait_ge(VE.h, b_VE + 7)
            inc(s.dma_start(out=ctx.cums_scr, in_=ctx.cumA[:]), dC, 16)
            inc(s.dma_start(out=ctx.rows_scr[0:16, :], in_=trow[:]), dC, 16)
            inc(s.dma_start(out=ctx.rows_scr[16:32, :], in_=trow2[:]), dC, 16)
            inc(s.dma_start(out=ctx.pub_scr[0:1, 128 * D_INNER:PUBN], in_=trow1[:]),
                dC, 16)

        endbar(ctx, blk)


def phase_conv(ctx):
    nc = ctx.nc
    dC, P_LD, P_PS, VE, SE = ctx.dC, ctx.P_LD, ctx.P_PS, ctx.VE, ctx.SE
    b_dC, b_LD, b_PS, b_VE, b_SE = dC.v, P_LD.base(), P_PS.base(), VE.v, SE.v

    with contextlib.ExitStack() as psx:
        xin = [psx.enter_context(nc.sbuf_tensor(f"cv_xin{i}", [128, W_A], BF16)) for i in range(2)]
        acc = [psx.enter_context(nc.sbuf_tensor(f"cv_acc{i}", [128, TOK], F32)) for i in range(2)]
        outb = [psx.enter_context(nc.sbuf_tensor(f"cv_out{i}", [128, TOK], BF16)) for i in range(2)]
        wcv = psx.enter_context(nc.sbuf_tensor("cv_w", [128, NDC * 4], F32))
        bcv = psx.enter_context(nc.sbuf_tensor("cv_b", [128, NDC], F32))
        blk = psx.enter_context(nc.Block("conv"))

        @blk.sync
        def _(s):
            wb(s, ctx)
            inc(s.dma_start(out=wcv[:], in_=ctx.wconv), dC, 16)
            inc(s.dma_start(out=bcv[:], in_=ctx.bconv), dC, 16)
            for dc in range(NDC):
                if dc >= 2:
                    s.wait_ge(VE.h, b_VE + 4 * (dc - 1))
                sp_inc(s.dma_start(out=xin[dc % 2][:],
                                   in_=ctx.xbcp_scr[dc * 128:(dc + 1) * 128, :]), P_LD, dc)

        @blk.vector
        def _(v):
            wb(v, ctx)
            v.wait_ge(dC.h, b_dC + 32)
            for dc in range(NDC):
                bb = dc % 2
                sp_wait(v, P_LD, b_LD, dc)
                if dc >= 2:
                    v.wait_ge(SE.h, b_SE + dc - 1)
                inc(v.tensor_scalar_mul(out=acc[bb][:], in0=xin[bb][:, 0:TOK],
                                        scalar1=wcv[:, 4 * dc:4 * dc + 1]), VE, 1)
                for kk in range(1, 4):
                    v.wait_ge(VE.h, b_VE + 4 * dc + kk)
                    inc(v.scalar_tensor_tensor(out=acc[bb][:], in0=xin[bb][:, kk:kk + TOK],
                                               scalar=wcv[:, 4 * dc + kk:4 * dc + kk + 1],
                                               in1=acc[bb][:], op0=ALU.mult, op1=ALU.add),
                        VE, 1)

        @blk.scalar
        def _(sc):
            wb(sc, ctx)
            for dc in range(NDC):
                bb = dc % 2
                sc.wait_ge(VE.h, b_VE + 4 * (dc + 1))
                if dc >= 2:
                    sp_wait(sc, P_PS, b_PS, dc - 2)
                inc(sc.activation(out=outb[bb][:], in_=acc[bb][:], func=AF.Silu,
                                  bias=bcv[:, dc:dc + 1]), SE, 1)

        @blk.gpsimd
        def _(g):
            wb(g, ctx)
            for dc in range(NDC):
                g.wait_ge(SE.h, b_SE + dc + 1)
                sp_inc(g.dma_start(out=ctx.xbc_scr[dc * 128:(dc + 1) * 128, :],
                                   in_=outb[dc % 2][:]), P_PS, dc)

        endbar(ctx, blk)


def phase_d1(ctx):
    nc = ctx.nc
    P_LD, P_W, P_ST, PE, VE, SE = ctx.P_LD, ctx.P_W, ctx.P_ST, ctx.PE, ctx.VE, ctx.SE
    b_LD, b_W, b_ST, b_PE, b_VE, b_SE = P_LD.base(), P_W.base(), P_ST.base(), PE.v, VE.v, SE.v
    PE_PER = 2 + NHEADS
    VE_PER = 2 + NHEADS

    def ve_nrev(c):
        return b_VE + c * VE_PER + 1

    def ve_xsc(c):
        return b_VE + c * VE_PER + 2

    def ve_scan(c, h):
        return b_VE + c * VE_PER + 2 + (h + 1)

    def pe_tr(c):
        return b_PE + c * PE_PER + 1

    def pe_repl(c):
        return b_PE + c * PE_PER + 2

    def pe_mmc(c, h):
        return b_PE + c * PE_PER + 2 + (h + 1)

    with contextlib.ExitStack() as psx:
        xtm = [psx.enter_context(nc.sbuf_tensor(f"d1_xtm{i}", [128, D_INNER], BF16)) for i in range(2)]
        btm = [psx.enter_context(nc.sbuf_tensor(f"d1_btm{i}", [128, 128], BF16)) for i in range(2)]
        xsc = [psx.enter_context(nc.sbuf_tensor(f"d1_xsc{i}", [128, D_INNER], BF16)) for i in range(2)]
        sel = psx.enter_context(nc.sbuf_tensor("d1_sel", [128, D_INNER], F32))
        dect = [psx.enter_context(nc.sbuf_tensor(f"d1_dect{i}", [128, NHEADS], F32)) for i in range(2)]
        decc = [psx.enter_context(nc.sbuf_tensor(f"d1_decc{i}", [128, NHEADS], F32)) for i in range(2)]
        nrev = [psx.enter_context(nc.sbuf_tensor(f"d1_nrev{i}", [NHEADS, CHUNK], F32)) for i in range(2)]
        rowb = [psx.enter_context(nc.sbuf_tensor(f"d1_rowb{i}", [1, NHEADS], F32)) for i in range(2)]
        blk = psx.enter_context(nc.Block("d1"))

        @blk.sync
        def _(s):
            wb(s, ctx)
            for c in range(NC):
                if c >= 2:
                    s.wait_ge(PE.h, pe_mmc(c - 1, NHEADS - 1))
                    s.wait_ge(VE.h, ve_xsc(c - 1))
                sp_inc(s.dma_start_transpose(out=xtm[c % 2][:],
                                             in_=ctx.xbc_scr[0:D_INNER,
                                                             c * CHUNK:(c + 1) * CHUNK]),
                       P_LD, c)
                sp_inc(s.dma_start_transpose(out=btm[c % 2][:],
                                             in_=ctx.xbc_scr[D_INNER:D_INNER + 128,
                                                             c * CHUNK:(c + 1) * CHUNK]),
                       P_LD, c)

        @blk.gpsimd
        def _(g):
            wb(g, ctx)
            for c in range(NC):
                if c >= 2:
                    g.wait_ge(PE.h, pe_repl(c - 1))
                sp_inc(g.dma_start(out=rowb[c % 2][:], in_=ctx.rows_scr[c:c + 1, :]),
                       P_W, c)

        @blk.vector
        def _(v):
            wb(v, ctx)
            for c in range(NC):
                cs = ctx.cumA[:, c * CHUNK:(c + 1) * CHUNK]
                if c >= 2:
                    v.wait_ge(PE.h, pe_tr(c - 1))
                inc(v.tensor_scalar(out=nrev[c % 2][:], in0=cs,
                                    scalar1=ctx.cumA[:, (c + 1) * CHUNK - 1:(c + 1) * CHUNK],
                                    scalar2=None, op0=ALU.subtract), VE, 1)
                v.wait_ge(SE.h, b_SE + 2 * c + 1)
                sp_wait(v, P_LD, b_LD, c, 32)
                if c >= 2:
                    v.wait_ge(PE.h, pe_mmc(c - 2, NHEADS - 1))
                inc(v.tensor_mul(out=xsc[c % 2][:].rearrange("p (h w) -> p h w", h=NHEADS),
                                 in0=xtm[c % 2][:].rearrange("p (h w) -> p h w", h=NHEADS),
                                 in1=dect[c % 2][:].rearrange("p (h w) -> p h w", w=1)
                                 .broadcast_to([128, NHEADS, HEADDIM])), VE, 1)
                v.wait_ge(SE.h, b_SE + 2 * c + 2)
                if c >= 1:
                    sp_wait(v, P_ST, b_ST, c - 1)
                for h in range(NHEADS):
                    pb = ctx.ps[4 + h % 2]
                    v.wait_ge(PE.h, pe_mmc(c, h))
                    if c == 0:
                        inc(v.tensor_copy(out=sel[:, h * 64:(h + 1) * 64],
                                          in_=pb[:, 0:64]), VE, 1)
                    else:
                        inc(v.scalar_tensor_tensor(out=sel[:, h * 64:(h + 1) * 64],
                                                   in0=sel[:, h * 64:(h + 1) * 64],
                                                   scalar=decc[c % 2][:, h:h + 1],
                                                   in1=pb[:, 0:64],
                                                   op0=ALU.mult, op1=ALU.add), VE, 1)

        @blk.scalar
        def _(sc):
            wb(sc, ctx)
            for c in range(NC):
                sc.wait_ge(PE.h, pe_tr(c))
                if c >= 2:
                    sc.wait_ge(VE.h, ve_xsc(c - 1))
                inc(sc.activation(out=dect[c % 2][:], in_=ctx.ps[2][:, 0:NHEADS],
                                  func=AF.Exp, scale=-1.0), SE, 1)
                sc.wait_ge(PE.h, pe_repl(c))
                if c >= 2:
                    sc.wait_ge(VE.h, ve_scan(c - 1, NHEADS - 1))
                inc(sc.activation(out=decc[c % 2][:], in_=ctx.ps[3][:, 0:NHEADS],
                                  func=AF.Exp), SE, 1)
                sc.wait_ge(VE.h, ve_scan(c, NHEADS - 1))
                if c < NC - 1:
                    sp_inc(sc.dma_start(out=ctx.sel_scr[(c + 1) * 128:(c + 2) * 128, :],
                                        in_=sel[:]), P_ST, c)
                else:
                    sp_inc(sc.dma_start(out=ctx.pub_scr[0:1, 0:128 * D_INNER]
                                        .rearrange("u (p f) -> (u p) f", p=128),
                                        in_=sel[:]), P_ST, c)

        @blk.tensor
        def _(t):
            wb(t, ctx)
            for c in range(NC):
                t.wait_ge(VE.h, ve_nrev(c))
                if c >= 1:
                    t.wait_ge(SE.h, b_SE + 2 * (c - 1) + 1)
                inc(t.transpose(ctx.ps[2][:, 0:NHEADS], nrev[c % 2][:],
                                ctx.idf32[0:NHEADS, 0:NHEADS]), PE, 1)
                sp_wait(t, P_W, b_W, c)
                if c >= 1:
                    t.wait_ge(SE.h, b_SE + 2 * (c - 1) + 2)
                inc(t.matmul(ctx.ps[3][:, 0:NHEADS], ctx.onesrow[:], rowb[c % 2][:],
                             start=True, stop=True), PE, 1)
                t.wait_ge(VE.h, ve_xsc(c))
                for h in range(NHEADS):
                    pb = ctx.ps[4 + h % 2]
                    if h >= 2:
                        t.wait_ge(VE.h, ve_scan(c, h - 2))
                    elif c >= 1:
                        t.wait_ge(VE.h, ve_scan(c - 1, NHEADS - 2 + h))
                    inc(t.matmul(pb[:, 0:64], btm[c % 2][:],
                                 xsc[c % 2][:, h * 64:(h + 1) * 64],
                                 start=True, stop=True), PE, 1)

        endbar(ctx, blk)


def phase_exchange(ctx):
    nc = ctx.nc
    P_W, P_X, P_ST, PE, VE, SE, CC = (ctx.P_W, ctx.P_X, ctx.P_ST,
                                      ctx.PE, ctx.VE, ctx.SE, ctx.CC)
    b_W, b_X, b_ST = P_W.base(), P_X.base(), P_ST.base()
    b_PE, b_VE, b_SE, b_CC = PE.v, VE.v, SE.v, CC.v

    ctx.S0_scope = contextlib.ExitStack()
    ctx.S0 = ctx.S0_scope.enter_context(nc.sbuf_tensor("S0", [128, D_INNER], F32))

    with contextlib.ExitStack() as psx:
        logd = psx.enter_context(nc.sbuf_tensor("ex_logd", [8, NHEADS], F32))
        btw = psx.enter_context(nc.sbuf_tensor("ex_btw", [8, 8], F32))
        prd = psx.enter_context(nc.sbuf_tensor("ex_prd", [8, 1], F32))
        coef = psx.enter_context(nc.sbuf_tensor("ex_coef", [8, NHEADS], F32))
        crow = [psx.enter_context(nc.sbuf_tensor(f"ex_crow{i}", [1, NHEADS], F32)) for i in range(2)]
        lbuf = [psx.enter_context(nc.sbuf_tensor(f"ex_lb{i}", [128, D_INNER], F32)) for i in range(2)]
        tmp = psx.enter_context(nc.sbuf_tensor("ex_tmp", [128, D_INNER], F32))
        blk = psx.enter_context(nc.Block("exchange"))

        @blk.gpsimd
        def _(g):
            wb(g, ctx)
            cc_i = g.collective_compute("AllGather", ALU.bypass,
                                        replica_groups=[list(range(8))],
                                        ins=[ctx.pub_scr.opt()],
                                        outs=[ctx.gat_scr.opt()])
            inc(cc_i, CC, 1)
            g.wait_ge(CC.h, b_CC + 1)
            inc(g.dma_start(out=logd[:],
                            in_=ctx.gat_scr[:, 128 * D_INNER:PUBN]), P_W.s[0], 16)
            inc(g.dma_start(out=btw[:], in_=ctx.btwT), P_W.s[0], 16)
            inc(g.dma_start(out=prd[:], in_=ctx.predc), P_W.s[0], 16)
            for j in range(8):
                if j == 0:
                    sp_wait(g, P_ST, b_ST, 0)
                if j >= 1:
                    g.wait_ge(PE.h, b_PE + 1 + j)
                inc(g.dma_start(out=crow[j % 2][:],
                                in_=ctx.rows_scr[32 + j:33 + j, :]), P_X.s[0], 16)
                if j >= 1:
                    g.wait_ge(VE.h, b_VE + 1 + 2 * j)
                inc(g.dma_start(out=lbuf[j % 2][:],
                                in_=ctx.gat_scr[j:j + 1, 0:128 * D_INNER]
                                .rearrange("u (p f) -> (u p) f", p=128)), P_X.s[1], 16)

        @blk.tensor
        def _(t):
            wb(t, ctx)
            t.wait_ge(P_W.s[0].h, b_W[0] + 48)
            inc(t.matmul(ctx.ps[0][0:8, 0:NHEADS], btw[:], logd[:], start=True,
                         stop=True), PE, 1)
            for j in range(8):
                t.wait_ge(P_X.s[0].h, b_X[0] + 16 * (j + 1))
                if j >= 1:
                    t.wait_ge(VE.h, b_VE + 1 + 2 * j)
                inc(t.matmul(ctx.ps[1][:, 0:NHEADS], ctx.onesrow[:], crow[j % 2][:],
                             start=True, stop=True), PE, 1)

        @blk.scalar
        def _(sc):
            wb(sc, ctx)
            sc.wait_ge(PE.h, b_PE + 1)
            inc(sc.activation(out=coef[:], in_=ctx.ps[0][0:8, 0:NHEADS], func=AF.Exp),
                SE, 1)

        @blk.vector
        def _(v):
            wb(v, ctx)
            v.wait_ge(SE.h, b_SE + 1)
            inc(v.tensor_scalar_mul(out=coef[:], in0=coef[:], scalar1=prd[:]), VE, 1)
            for j in range(8):
                v.wait_ge(PE.h, b_PE + 1 + (j + 1))
                v.wait_ge(P_X.s[1].h, b_X[1] + 16 * (j + 1))
                inc(v.tensor_mul(out=tmp[:].rearrange("p (h w) -> p h w", h=NHEADS),
                                 in0=lbuf[j % 2][:].rearrange("p (h w) -> p h w", h=NHEADS),
                                 in1=ctx.ps[1][:, 0:NHEADS]
                                 .rearrange("p (h w) -> p h w", w=1)
                                 .broadcast_to([128, NHEADS, HEADDIM])), VE, 1)
                v.wait_ge(VE.h, b_VE + 2 + 2 * j)
                if j == 0:
                    inc(v.tensor_copy(out=ctx.S0[:], in_=tmp[:]), VE, 1)
                else:
                    inc(v.tensor_add(out=ctx.S0[:], in0=ctx.S0[:], in1=tmp[:]), VE, 1)

        @blk.sync
        def _(s):
            wb(s, ctx)
            s.wait_ge(VE.h, b_VE + 1)
            sp_inc(s.dma_start(out=ctx.rows_scr[32:40, :], in_=coef[:]), P_ST, 0)

        endbar(ctx, blk)


def phase_d2(ctx):
    nc = ctx.nc
    P_LD, P_W, P_PS, PE, VE, SE, PO = (ctx.P_LD, ctx.P_W, ctx.P_PS,
                                       ctx.PE, ctx.VE, ctx.SE, ctx.PO)
    b_LD, b_W, b_PS = P_LD.base(), P_W.base(), P_PS.base()
    b_PE, b_VE, b_SE, b_PO = PE.v, VE.v, SE.v, PO.v
    NPAIRH = NHEADS // 2

    PE_PER = 3 + NHEADS + 2
    SE_PER = 3 + NHEADS + 3
    VE_PER = 4 + 8 + NHEADS + NPAIRH + 3 + NZ
    PO_PER = 1

    def pe_cbt(c):
        return b_PE + c * PE_PER + 1

    def pe_prec(c):
        return b_PE + c * PE_PER + 2

    def pe_cam(c):
        return b_PE + c * PE_PER + 3

    def pe_ymm(c, h):
        return b_PE + c * PE_PER + 3 + (h + 1)

    def pe_sq(c):
        return b_PE + c * PE_PER + 3 + NHEADS + 1

    def pe_rr(c):
        return b_PE + c * PE_PER + 3 + NHEADS + 2

    def se_sz(c):
        return b_SE + c * SE_PER + 1

    def se_ebc(c):
        return b_SE + c * SE_PER + 2

    def se_prec(c):
        return b_SE + c * SE_PER + 3

    def se_e0(c, h):
        return b_SE + c * SE_PER + 3 + (h + 1)

    def se_sq(c):
        return b_SE + c * SE_PER + 3 + NHEADS + 1

    def se_rstd(c):
        return b_SE + c * SE_PER + 3 + NHEADS + 3

    def ve_cam(c):
        return b_VE + c * VE_PER + 1

    def ve_cbt(c):
        return b_VE + c * VE_PER + 2

    def ve_mask(c):
        return b_VE + c * VE_PER + 3

    def ve_sv(c):
        return b_VE + c * VE_PER + 4

    def ve_stg(c, g):
        return b_VE + c * VE_PER + 4 + 9 * g + 1

    def ve_seg(c, h):
        g, j = divmod(h, 8)
        return b_VE + c * VE_PER + 4 + 9 * g + 2 + j

    def ve_yev(c, m):
        return b_VE + c * VE_PER + 76 + (m + 1)

    def ve_ygt(c):
        return b_VE + c * VE_PER + 109

    def ve_rr(c):
        return b_VE + c * VE_PER + 110

    def ve_rrs(c):
        return b_VE + c * VE_PER + 111

    def ve_end(c):
        return b_VE + (c + 1) * VE_PER

    def po_mask(c):
        return b_PO + c * PO_PER + 1

    with contextlib.ExitStack() as psx:
        bc = psx.enter_context(nc.sbuf_tensor("d2_bc", [128, NHEADS * CHUNK], F32))
        ebc = psx.enter_context(nc.sbuf_tensor("d2_ebc", [128, NHEADS * CHUNK], BF16))
        sva = psx.enter_context(nc.sbuf_tensor("d2_sva", [128, NHEADS * CHUNK], BF16))
        e0a = psx.enter_context(nc.sbuf_tensor("d2_e0a", [128, NHEADS * CHUNK], BF16))
        xtm = psx.enter_context(nc.sbuf_tensor("d2_xtm", [128, D_INNER], BF16))
        xfm = psx.enter_context(nc.sbuf_tensor("d2_xfm", [128, NZ * CHUNK], BF16))
        cfm = psx.enter_context(nc.sbuf_tensor("d2_cfm", [128, CHUNK], BF16))
        bfm = psx.enter_context(nc.sbuf_tensor("d2_bfm", [128, CHUNK], BF16))
        cbt = psx.enter_context(nc.sbuf_tensor("d2_cbt", [128, CHUNK], BF16))
        zt = psx.enter_context(nc.sbuf_tensor("d2_zt", [128, NZ * CHUNK], BF16))
        selb = psx.enter_context(nc.sbuf_tensor("d2_selb", [128, D_INNER], F32))
        seg = psx.enter_context(nc.sbuf_tensor("d2_seg", [128, D_INNER], BF16))
        prec = psx.enter_context(nc.sbuf_tensor("d2_prec", [128, NHEADS], F32))
        camt = psx.enter_context(nc.sbuf_tensor("d2_camt", [128, NHEADS], F32))
        yfm = psx.enter_context(nc.sbuf_tensor("d2_yfm", [128, NZ * CHUNK], F32))
        ynb = psx.enter_context(nc.sbuf_tensor("d2_ynb", [128, NZ * CHUNK], BF16))
        rr = psx.enter_context(nc.sbuf_tensor("d2_rr", [1, CHUNK], F32))
        rrs = psx.enter_context(nc.sbuf_tensor("d2_rrs", [128, CHUNK], F32))
        rowp = psx.enter_context(nc.sbuf_tensor("d2_rowp", [1, NHEADS], F32))
        blk = psx.enter_context(nc.Block("d2"))

        def hb3(t2d, nh=NHEADS):
            return t2d[:].rearrange("p (h l) -> p h l", h=nh)

        @blk.sync
        def _(s):
            wb(s, ctx)
            for c in range(NC):
                if c >= 1:
                    s.wait_ge(PE.h, pe_rr(c - 1))
                    s.wait_ge(VE.h, ve_end(c - 1))
                sp_inc(s.dma_start_transpose(out=xtm[:],
                                             in_=ctx.xbc_scr[0:D_INNER,
                                                             c * CHUNK:(c + 1) * CHUNK]),
                       P_LD, c)
                sp_inc(s.dma_start(out=cfm[:],
                                   in_=ctx.xbc_scr[D_INNER + 128:D_INNER + 256,
                                                   c * CHUNK:(c + 1) * CHUNK]), P_LD, c)
                sp_inc(s.dma_start(out=bfm[:],
                                   in_=ctx.xbc_scr[D_INNER:D_INNER + 128,
                                                   c * CHUNK:(c + 1) * CHUNK]), P_LD, c)
                sp_inc(s.dma_start(out=xfm[:].rearrange("p (i t) -> p i t", i=NZ),
                                   in_=ctx.xbc_scr[0:D_INNER, c * CHUNK:(c + 1) * CHUNK]
                                   .rearrange("(i p) t -> p i t", p=128)), P_LD, c)
                sp_inc(s.dma_start(out=zt[:].rearrange("p (i t) -> p i t", i=NZ),
                                   in_=ctx.z_scr[:, c * CHUNK:(c + 1) * CHUNK]
                                   .rearrange("(i p) t -> p i t", p=128)), P_LD, c)
                cc_ld = max(c, 1)
                sp_inc(s.dma_start(out=selb[:],
                                   in_=ctx.sel_scr[cc_ld * 128:(cc_ld + 1) * 128, :]),
                       P_LD, c)

        @blk.gpsimd
        def _(g):
            wb(g, ctx)
            for c in range(NC):
                if c >= 1:
                    g.wait_ge(SE.h, se_e0(c - 1, NHEADS - 1))
                for q in range(4):
                    sp_inc(g.dma_start(out=bc[:, q * 16 * CHUNK:(q + 1) * 16 * CHUNK]
                                       .rearrange("p (h l) -> p h l", h=16),
                                       in_=ctx.cums_scr
                                       .rearrange("h (cc l) -> cc h l", l=CHUNK)
                                       [c:c + 1, q * 16:(q + 1) * 16, :]
                                       .broadcast_to([128, 16, CHUNK])), P_W, c)
                if c >= 1:
                    g.wait_ge(PE.h, pe_prec(c - 1))
                sp_inc(g.dma_start(out=rowp[:], in_=ctx.rows_scr[16 + c:17 + c, :]),
                       P_W, c)
                g.wait_ge(SE.h, se_ebc(c))
                inc(g.tensor_tensor(out=hb3(bc)[:, 32:64, :], in0=hb3(bc)[:, 32:64, :],
                                    in1=ctx.maskb[:].rearrange("s (u l) -> s u l", u=1)
                                    .broadcast_to([128, 32, CHUNK]), op=ALU.add), PO, 1)
                g.wait_ge(VE.h, ve_end(c))
                if c >= 1:
                    sp_wait(g, P_PS, b_PS, c - 1)
                sp_inc(g.dma_start(out=ctx.y_scr[:, c * CHUNK:(c + 1) * CHUNK]
                                   .rearrange("(i p) t -> p i t", p=128),
                                   in_=ynb[:].rearrange("p (i t) -> p i t", i=NZ)),
                       P_PS, c)

        @blk.scalar
        def _(sc):
            wb(sc, ctx)
            for c in range(NC):
                sp_wait(sc, P_LD, b_LD, c, 96)
                if c >= 1:
                    sc.wait_ge(VE.h, ve_ygt(c - 1))
                inc(sc.activation(out=zt[:], in_=zt[:], func=AF.Silu), SE, 1)
                sp_wait(sc, P_W, b_W, c, 80)
                if c >= 1:
                    sc.wait_ge(VE.h, ve_sv(c - 1))
                inc(sc.activation(out=ebc[:], in_=bc[:], func=AF.Exp), SE, 1)
                sc.wait_ge(PE.h, pe_prec(c))
                if c >= 1:
                    sc.wait_ge(VE.h, ve_seg(c - 1, NHEADS - 1))
                inc(sc.activation(out=prec[:], in_=ctx.ps[2][:, 0:NHEADS], func=AF.Exp),
                    SE, 1)
                sc.wait_ge(VE.h, ve_mask(c))
                sc.wait_ge(PO.h, po_mask(c))
                sc.wait_ge(VE.h, ve_cam(c))
                if c >= 1:
                    sc.wait_ge(PE.h, pe_ymm(c - 1, NHEADS - 1))
                for h in range(NHEADS):
                    inc(sc.activation(out=e0a[:, h * CHUNK:(h + 1) * CHUNK],
                                      in_=bc[:, h * CHUNK:(h + 1) * CHUNK],
                                      func=AF.Exp, bias=camt[:, h:h + 1]), SE, 1)
                sc.wait_ge(VE.h, ve_ygt(c))
                if c >= 1:
                    sp_wait(sc, P_PS, b_PS, c - 1)
                inc(sc.activation(out=ynb[:], in_=yfm[:], func=AF.Square), SE, 1)
                sc.wait_ge(VE.h, ve_rr(c))
                inc(sc.activation(out=rr[:], in_=rr[:], func=AF.Ln), SE, 1)
                sc.wait_ge(SE.h, b_SE + c * SE_PER + 3 + NHEADS + 2)
                inc(sc.activation(out=rr[:], in_=rr[:], func=AF.Exp, scale=-0.5), SE, 1)

        @blk.tensor
        def _(t):
            wb(t, ctx)
            for c in range(NC):
                sp_wait(t, P_LD, b_LD, c, 96)
                if c >= 1:
                    t.wait_ge(VE.h, ve_cbt(c - 1))
                inc(t.matmul(ctx.ps[0][:, 0:CHUNK], bfm[:], cfm[:], start=True,
                             stop=True), PE, 1)
                sp_wait(t, P_W, b_W, c, 80)
                if c >= 1:
                    t.wait_ge(SE.h, se_prec(c - 1))
                inc(t.matmul(ctx.ps[2][:, 0:NHEADS], ctx.onesrow[:], rowp[:],
                             start=True, stop=True), PE, 1)
                if c >= 1:
                    t.wait_ge(VE.h, ve_cam(c - 1))
                inc(t.transpose(ctx.ps[3][:, 0:NHEADS],
                                ctx.cumA[:, c * CHUNK:(c + 1) * CHUNK],
                                ctx.idf32[0:NHEADS, 0:NHEADS]), PE, 1)
                t.wait_ge(VE.h, ve_sv(c))
                for h in range(NHEADS):
                    m = h // 2
                    odd = h % 2
                    pb = ctx.ps[4 + m % 2]
                    po_r = pb[64:128, 0:CHUNK] if odd else pb[0:64, 0:CHUNK]
                    tp = (0, 64) if odd else None
                    t.wait_ge(VE.h, ve_stg(c, h // 8))
                    t.wait_ge(VE.h, ve_seg(c, h))
                    if not odd:
                        if m >= 2:
                            t.wait_ge(VE.h, ve_yev(c, m - 2))
                        elif c >= 1:
                            t.wait_ge(VE.h, ve_yev(c - 1, NPAIRH - 2 + m))
                    t.matmul(po_r, xtm[:, h * 64:(h + 1) * 64],
                             e0a[:, h * CHUNK:(h + 1) * CHUNK],
                             start=True, stop=False, tile_position=tp)
                    inc(t.matmul(po_r, seg[:, h * 64:(h + 1) * 64],
                                 sva[:, h * CHUNK:(h + 1) * CHUNK],
                                 start=False, stop=True, tile_position=tp), PE, 1)
                t.wait_ge(SE.h, se_sq(c))
                if c >= 1:
                    t.wait_ge(VE.h, ve_rr(c - 1))
                mm = None
                for j in range(NZ):
                    mm = t.matmul(ctx.ps[6][0:1, 0:CHUNK], ctx.onescolbf[:],
                                  ynb[:, j * CHUNK:(j + 1) * CHUNK],
                                  start=(j == 0), stop=(j == NZ - 1))
                inc(mm, PE, 1)
                t.wait_ge(SE.h, se_rstd(c))
                if c >= 1:
                    t.wait_ge(VE.h, ve_rrs(c - 1))
                inc(t.matmul(ctx.ps[7][:, 0:CHUNK], ctx.onesrow[:], rr[:],
                             start=True, stop=True), PE, 1)

        @blk.vector
        def _(v):
            wb(v, ctx)
            for c in range(NC):
                v.wait_ge(PE.h, pe_cam(c))
                if c >= 1:
                    v.wait_ge(SE.h, se_e0(c - 1, NHEADS - 1))
                inc(v.tensor_scalar_mul(out=camt[:], in0=ctx.ps[3][:, 0:NHEADS],
                                        scalar1=-1.0), VE, 1)
                inc(v.tensor_copy(out=cbt[:], in_=ctx.ps[0][:, 0:CHUNK]), VE, 1)
                v.wait_ge(SE.h, se_ebc(c))
                inc(v.tensor_tensor(out=hb3(bc)[:, 0:32, :], in0=hb3(bc)[:, 0:32, :],
                                    in1=ctx.maskb[:].rearrange("s (u l) -> s u l", u=1)
                                    .broadcast_to([128, 32, CHUNK]), op=ALU.add), VE, 1)
                if c >= 1:
                    v.wait_ge(PE.h, pe_ymm(c - 1, NHEADS - 1))
                inc(v.tensor_mul(out=hb3(sva), in0=hb3(ebc),
                                 in1=cfm[:].rearrange("p (u l) -> p u l", u=1)
                                 .broadcast_to([128, NHEADS, CHUNK])), VE, 1)
                v.wait_ge(SE.h, se_prec(c))
                sp_wait(v, P_LD, b_LD, c, 96)
                if c >= 1:
                    v.wait_ge(PO.h, b_PO + c * PO_PER)
                for g in range(8):
                    v.wait_ge(SE.h, se_e0(c, 8 * g + 7))
                    inc(v.tensor_mul(out=hb3(e0a)[:, 8 * g:8 * g + 8, :],
                                     in0=hb3(e0a)[:, 8 * g:8 * g + 8, :],
                                     in1=cbt[:].rearrange("p (u l) -> p u l", u=1)
                                     .broadcast_to([128, 8, CHUNK])), VE, 1)
                    for j in range(8):
                        h = 8 * g + j
                        if c == 0:
                            inc(v.tensor_scalar_mul(out=seg[:, h * 64:(h + 1) * 64],
                                                    in0=ctx.S0[:, h * 64:(h + 1) * 64],
                                                    scalar1=prec[:, h:h + 1]), VE, 1)
                        else:
                            inc(v.scalar_tensor_tensor(out=seg[:, h * 64:(h + 1) * 64],
                                                       in0=ctx.S0[:, h * 64:(h + 1) * 64],
                                                       scalar=prec[:, h:h + 1],
                                                       in1=selb[:, h * 64:(h + 1) * 64],
                                                       op0=ALU.mult, op1=ALU.add), VE, 1)
                if c >= 1:
                    v.wait_ge(SE.h, se_sq(c - 1))
                for m in range(NPAIRH):
                    pb = ctx.ps[4 + m % 2]
                    v.wait_ge(PE.h, pe_ymm(c, 2 * m + 1))
                    inc(v.scalar_tensor_tensor(
                        out=yfm[:, m * CHUNK:(m + 1) * CHUNK],
                        in0=xfm[:, m * CHUNK:(m + 1) * CHUNK],
                        scalar=ctx.dp_pair[:, m:m + 1], in1=pb[:, 0:CHUNK],
                        op0=ALU.mult, op1=ALU.add), VE, 1)
                v.wait_ge(SE.h, se_sz(c))
                v.wait_ge(VE.h, ve_yev(c, NPAIRH - 1))
                inc(v.tensor_mul(out=yfm[:], in0=yfm[:], in1=zt[:]), VE, 1)
                v.wait_ge(PE.h, pe_sq(c))
                inc(v.tensor_scalar(out=rr[:], in0=ctx.ps[6][0:1, 0:CHUNK],
                                    scalar1=1.0 / D_INNER, scalar2=1e-5,
                                    op0=ALU.mult, op1=ALU.add), VE, 1)
                v.wait_ge(PE.h, pe_rr(c))
                inc(v.tensor_copy(out=rrs[:], in_=ctx.ps[7][:, 0:CHUNK]), VE, 1)
                v.wait_ge(VE.h, ve_rrs(c))
                for j in range(NZ):
                    inc(v.scalar_tensor_tensor(out=ynb[:, j * CHUNK:(j + 1) * CHUNK],
                                               in0=yfm[:, j * CHUNK:(j + 1) * CHUNK],
                                               scalar=ctx.nw[:, j:j + 1], in1=rrs[:],
                                               op0=ALU.mult, op1=ALU.mult), VE, 1)

        endbar(ctx, blk)
    ctx.S0_scope.close()


def phase_outproj(ctx):
    nc = ctx.nc
    dH, P_LD, P_W, P_ST, PE, VE = ctx.dH, ctx.P_LD, ctx.P_W, ctx.P_ST, ctx.PE, ctx.VE
    b_dH, b_LD, b_W, b_ST, b_PE, b_VE = (dH.v, P_LD.base(), P_W.base(), P_ST.base(),
                                         PE.v, VE.v)
    NSEG = 2
    SW = 1024

    with contextlib.ExitStack() as psx:
        yseg = psx.enter_context(nc.sbuf_tensor("op_yseg", [128, NZ * SW], BF16))
        wo = [psx.enter_context(nc.sbuf_tensor(f"op_wo{i}", [128, NZ * 128], BF16)) for i in range(2)]
        x1b = [psx.enter_context(nc.sbuf_tensor(f"op_x1{i}", [128, 512], F32)) for i in range(2)]
        xob = [psx.enter_context(nc.sbuf_tensor(f"op_xo{i}", [128, 512], F32)) for i in range(2)]
        blk = psx.enter_context(nc.Block("outproj"))

        @blk.sync
        def _(s):
            wb(s, ctx)
            for sg in range(NSEG):
                if sg >= 1:
                    s.wait_ge(PE.h, b_PE + sg * 32)
                inc(s.dma_start(out=yseg[:].rearrange("p (i t) -> p i t", i=NZ),
                                in_=ctx.y_scr[:, sg * SW:(sg + 1) * SW]
                                .rearrange("(i p) t -> p i t", p=128)), dH, 16)
                for d in range(NJ):
                    for ts2 in range(2):
                        it = sg * 32 + d * 2 + ts2
                        t0 = sg * SW + ts2 * 512
                        if it >= 2:
                            s.wait_ge(VE.h, b_VE + it - 1)
                        sp_inc(s.dma_start(out=x1b[it % 2][:],
                                           in_=ctx.x1_scr[d * 128:(d + 1) * 128,
                                                          HALO + t0:HALO + t0 + 512]),
                               P_LD, it)

        @blk.gpsimd
        def _(g):
            wb(g, ctx)
            for sg in range(NSEG):
                for d in range(NJ):
                    kd = sg * NJ + d
                    if kd >= 2:
                        g.wait_ge(PE.h, b_PE + (kd - 1) * 2)
                    sp_inc(g.dma_start(out=wo[d % 2][:].rearrange("p (i d) -> p i d", i=NZ),
                                       in_=ctx.wout[:, d * 128:(d + 1) * 128]
                                       .rearrange("(i p) d -> p i d", p=128)), P_W, kd)

        @blk.tensor
        def _(t):
            wb(t, ctx)
            for sg in range(NSEG):
                t.wait_ge(dH.h, b_dH + 16 * (sg + 1))
                for d in range(NJ):
                    kd = sg * NJ + d
                    sp_wait(t, P_W, b_W, kd)
                    for ts2 in range(2):
                        it = sg * 32 + d * 2 + ts2
                        pp = ctx.ps[4 + it % 2]
                        if it >= 2:
                            t.wait_ge(VE.h, b_VE + it - 1)
                        mm = None
                        for i in range(NZ):
                            mm = t.matmul(pp[:], wo[d % 2][:, i * 128:(i + 1) * 128],
                                          yseg[:, i * SW + ts2 * 512:i * SW + ts2 * 512 + 512],
                                          start=(i == 0), stop=(i == NZ - 1))
                        inc(mm, PE, 1)

        @blk.vector
        def _(v):
            wb(v, ctx)
            for sg in range(NSEG):
                for d in range(NJ):
                    for ts2 in range(2):
                        it = sg * 32 + d * 2 + ts2
                        pp = ctx.ps[4 + it % 2]
                        v.wait_ge(PE.h, b_PE + it + 1)
                        sp_wait(v, P_LD, b_LD, it)
                        if it >= 2:
                            sp_wait(v, P_ST, b_ST, it - 2)
                        inc(v.tensor_add(out=xob[it % 2][:], in0=pp[:],
                                         in1=x1b[it % 2][:]), VE, 1)

        @blk.scalar
        def _(sc):
            wb(sc, ctx)
            for sg in range(NSEG):
                for d in range(NJ):
                    for ts2 in range(2):
                        it = sg * 32 + d * 2 + ts2
                        t0 = sg * SW + ts2 * 512
                        sc.wait_ge(VE.h, b_VE + it + 1)
                        sp_inc(sc.dma_start(out=ctx.x2_scr[d * 128:(d + 1) * 128,
                                                           t0:t0 + 512],
                                            in_=xob[it % 2][:]), P_ST, it)

        endbar(ctx, blk)


def phase_dbg(ctx):
    nc = ctx.nc
    dL = ctx.dC
    b_dL = dL.v
    outs = []
    for nm in [d for d in DBG if d != "xln"]:
        src = getattr(ctx, nm)
        outs.append((nc.dram_tensor("dbg_" + nm, list(src.shape), src.dtype,
                                    kind="ExternalOutput").ap(), src))
    with nc.Block("dbg") as blk:
        @blk.sync
        def _(s):
            wb(s, ctx)
            n = 0
            for dst, src in outs:
                inc(s.dma_start(out=dst, in_=src), dL, 16)
                n += 1
            s.wait_ge(dL.h, b_dL + 16 * n)


_NC_CACHE = {}


def _get_nc():
    key = (tuple(DBG), STOP_AFTER)
    if key not in _NC_CACHE:
        _NC_CACHE[key] = build_program()
    return _NC_CACHE[key]


def kernel(L0_attn_u, L0_attn_W_in, L0_attn_W_conv, L0_attn_b_conv,
           L0_attn_dt_bias, L0_attn_A_log, L0_attn_Dp, L0_attn_norm_w,
           L0_attn_W_out, L0_ffn_w_gate_up, L0_ffn_w_down):
    f32 = np.float32
    u = np.asarray(L0_attn_u, f32)
    ut = np.ascontiguousarray(u.transpose(0, 2, 1))
    wgu = np.ascontiguousarray(np.asarray(L0_ffn_w_gate_up, f32).T)
    wdn = np.ascontiguousarray(np.asarray(L0_ffn_w_down, f32).T)
    win = np.ascontiguousarray(np.asarray(L0_attn_W_in, f32).T)
    wout = np.ascontiguousarray(np.asarray(L0_attn_W_out, f32).T)
    wconv = np.ascontiguousarray(
        np.asarray(L0_attn_W_conv, f32).reshape(NDC, 128, 4).transpose(1, 0, 2)
        .reshape(128, NDC * 4))
    bconv = np.ascontiguousarray(np.asarray(L0_attn_b_conv, f32).reshape(NDC, 128).T)
    dtb = np.asarray(L0_attn_dt_bias, f32).reshape(NHEADS, 1)
    alog = np.asarray(L0_attn_A_log, f32).reshape(NHEADS, 1)
    dprow = np.asarray(L0_attn_Dp, f32).reshape(1, NHEADS)
    nwin = np.ascontiguousarray(np.asarray(L0_attn_norm_w, f32).reshape(NZ, 128).T)

    in_maps = []
    for core in range(8):
        b, q = divmod(core, 4)
        u_pad = np.zeros((D_MODEL, W_A), f32)
        t0 = q * TOK
        lo = max(0, t0 - HALO)
        u_pad[:, HALO - (t0 - lo):] = ut[b][:, lo:t0 + TOK]
        btwT = np.zeros((8, 8), f32)
        predc = np.zeros((8, 1), f32)
        for j in range(8):
            if j // 4 == b and j % 4 < q:
                predc[j, 0] = 1.0
                for i in range(8):
                    if i // 4 == b and (j % 4) < (i % 4) < q:
                        btwT[i, j] = 1.0
        in_maps.append({
            "u": np.ascontiguousarray(u_pad), "wgu": wgu, "wdn": wdn, "win": win,
            "wout": wout, "wconv": wconv, "bconv": bconv, "dtb": dtb, "alog": alog,
            "dprow": dprow, "nwin": nwin, "btwT": btwT, "predc": predc,
        })

    nc = _get_nc()
    res = run_bass_kernel_spmd(nc, in_maps, core_ids=list(range(8)))
    out = np.empty((2, 8192, 2048), f32)
    for core in range(8):
        b, q = divmod(core, 4)
        out[b, q * TOK:(q + 1) * TOK, :] = np.asarray(res.results[core]["o"]).T
    if DBG:
        kernel._dbg = [{("dbg_" + nm): np.asarray(res.results[c]["dbg_" + nm])
                        for nm in DBG} for c in range(8)]
    return out


# version 04
# speedup vs baseline: 1.1625x; 1.0365x over previous
import contextlib
import numpy as np
import ml_dtypes
import concourse.bass as bass
import concourse.mybir as mybir
from concourse.bass_utils import run_bass_kernel_spmd

F32 = mybir.dt.float32
BF16 = mybir.dt.bfloat16
AF = mybir.ActivationFunctionType
ALU = mybir.AluOpType
bf16 = ml_dtypes.bfloat16

D_MODEL = 2048
D_INNER = 4096
D_STATE = 128
NHEADS = 64
HEADDIM = 64
CONV_DIM = D_INNER + 2 * D_STATE
D_IN_PROJ = 2 * D_INNER + 2 * D_STATE + NHEADS
INTER = 8192
TOK = 2048
HALO = 3
W_A = TOK + HALO
CHUNK = 128
NC = TOK // CHUNK
NJ = D_MODEL // 128
NI = INTER // 128
NZ = D_INNER // 128
NDC = CONV_DIM // 128
NO_IN = 67
MASK_NEG = -30000.0
PUBN = 128 * D_INNER + NHEADS

TSUBS_A = [(0, 512), (512, 512), (1024, 512), (1536, 256), (1792, 259)]
TSUBS_F = [(0, 512), (512, 512), (1024, 512), (1536, 512)]

DBG = []
STOP_AFTER = None


class Sem:
    def __init__(self, nc, es, name):
        self.h = es.enter_context(nc.semaphore(name))
        self.v = 0


def inc(instr, sem, n=1):
    instr.then_inc(sem.h, n)
    sem.v += n
    return instr


class SPair:

    def __init__(self, nc, es, name):
        self.s = [Sem(nc, es, name + "a"), Sem(nc, es, name + "b")]

    def base(self):
        return (self.s[0].v, self.s[1].v)


def sp_inc(instr, sp, g, n=16):
    inc(instr, sp.s[g % 2], n)


def sp_wait(eng, sp, b, g, n=16):
    eng.wait_ge(sp.s[g % 2].h, b[g % 2] + n * (g // 2 + 1))


class Ctx:
    pass


def build_program():
    nc = bass.Bass("TRN2", target_bir_lowering=False, debug=False)
    ctx = Ctx()
    ctx.nc = nc
    es = contextlib.ExitStack()
    ctx.es = es
    ap = lambda *a, **k: nc.dram_tensor(*a, **k).ap()

    ctx.u = ap("u", [D_MODEL, W_A], F32, kind="ExternalInput")
    ctx.wgu = ap("wgu", [D_MODEL, 2 * INTER], F32, kind="ExternalInput")
    ctx.wdn = ap("wdn", [INTER, D_MODEL], F32, kind="ExternalInput")
    ctx.win = ap("win", [D_MODEL, D_IN_PROJ], F32, kind="ExternalInput")
    ctx.wout = ap("wout", [D_INNER, D_MODEL], F32, kind="ExternalInput")
    ctx.wconv = ap("wconv", [128, NDC * 4], F32, kind="ExternalInput")
    ctx.bconv = ap("bconv", [128, NDC], F32, kind="ExternalInput")
    ctx.dtb = ap("dtb", [NHEADS, 1], F32, kind="ExternalInput")
    ctx.alog = ap("alog", [NHEADS, 1], F32, kind="ExternalInput")
    ctx.dprow = ap("dprow", [1, NHEADS], F32, kind="ExternalInput")
    ctx.nwin = ap("nwin", [128, NZ], F32, kind="ExternalInput")
    ctx.btwT = ap("btwT", [8, 8], F32, kind="ExternalInput")
    ctx.predc = ap("predc", [8, 1], F32, kind="ExternalInput")
    ctx.o = ap("o", [D_MODEL, TOK], F32, kind="ExternalOutput")
    ctx.h_scr = ap("h_scr", [INTER, W_A], BF16)
    ctx.x1_scr = ap("x1_scr", [D_MODEL, W_A], F32)
    ctx.z_scr = ap("z_scr", [D_INNER, TOK], BF16)
    ctx.xbcp_scr = ap("xbcp_scr", [CONV_DIM, W_A], BF16)
    ctx.xbc_scr = ap("xbc_scr", [CONV_DIM, TOK], BF16)
    ctx.sel_scr = ap("sel_scr", [NC * 128, D_INNER], BF16)
    ctx.y_scr = ap("y_scr", [D_INNER, TOK], BF16)
    ctx.x2_scr = ap("x2_scr", [D_MODEL, TOK], F32)
    ctx.cums_scr = ap("cums_scr", [NHEADS, TOK], F32)
    ctx.rows_scr = ap("rows_scr", [64, NHEADS], F32)
    ctx.pub_scr = ap("pub_scr", [1, PUBN], F32)
    ctx.gat_scr = ap("gat_scr", [8, PUBN], F32)

    maskb_np = np.where(np.arange(CHUNK)[None, :] >= np.arange(CHUNK)[:, None],
                        0.0, MASK_NEG).astype(np.float32)
    ctx.maskb_c = nc.inline_tensor(maskb_np, name="c_maskb").ap()
    ctx.onesrow_c = nc.inline_tensor(np.ones((1, 128), np.float32), name="c_onesrow").ap()
    ctx.onescolbf_c = nc.inline_tensor(np.ones((128, 1), bf16), name="c_onescolbf").ap()
    ctx.idf32_c = nc.inline_tensor(np.eye(128, dtype=np.float32), name="c_idf32").ap()
    segm = np.ones((NHEADS, TOK), np.float32)
    segm[:, ::CHUNK] = 0.0
    ctx.segmask_c = nc.inline_tensor(segm, name="c_segmask").ap()
    ctx.zcol_c = nc.inline_tensor(np.zeros((NHEADS, 1), np.float32), name="c_zcol").ap()
    halves = np.zeros((2, 128), np.float32)
    halves[0, 0:64] = 1.0
    halves[1, 64:128] = 1.0
    ctx.halves_c = nc.inline_tensor(halves, name="c_halves").ap()

    sb = lambda name, shape, dt: es.enter_context(nc.sbuf_tensor(name, shape, dt))
    ctx.dtraw = sb("g_dtraw", [NHEADS, W_A], F32)
    ctx.Ab = sb("g_Ab", [NHEADS, TOK], F32)
    ctx.cumA = sb("g_cumA", [NHEADS, TOK], F32)
    ctx.tots = sb("g_tots", [NHEADS, NC], F32)
    ctx.ptot = sb("g_ptot", [NHEADS, NC + 1], F32)
    ctx.dp_pair = sb("g_dp_pair", [128, NHEADS // 2], F32)
    ctx.maskb = sb("g_maskb", [CHUNK, CHUNK], F32)
    ctx.onesrow = sb("g_onesrow", [1, 128], F32)
    ctx.onescolbf = sb("g_onescolbf", [128, 1], BF16)
    ctx.idf32 = sb("g_idf32", [128, 128], F32)
    ctx.cols = sb("g_cols", [NHEADS, 4], F32)
    ctx.nw = sb("g_nw", [128, NZ], F32)
    ctx.dprow_sb = sb("g_dprow", [1, NHEADS], F32)
    ctx.dppk_sb = sb("g_dppk", [2, NHEADS // 2], F32)
    ctx.halves_sb = sb("g_halves", [2, 128], F32)

    ctx.ps = [es.enter_context(nc.psum_tensor(f"ps{i}", [128, 512], F32)) for i in range(8)]

    for s in ("dC", "dH", "PE", "VE", "SE", "PO", "CC", "BAR"):
        setattr(ctx, s, Sem(nc, es, "s_" + s))
    ctx.P_LD = SPair(nc, es, "p_ld")
    ctx.P_W = SPair(nc, es, "p_w")
    ctx.P_ST = SPair(nc, es, "p_st")
    ctx.P_X = SPair(nc, es, "p_x")
    ctx.P_PS = SPair(nc, es, "p_ps")
    ctx.P_H = SPair(nc, es, "p_h")
    ctx.bar_n = 0

    phases = [
        ("init", lambda: phase_init(ctx)),
        ("ln1", lambda: phase_ln(ctx, src=ctx.u, width=W_A, name="ln1")),
        ("gu1", lambda: phase_gateup(ctx, width=W_A, name="gu1")),
        ("dn1", lambda: phase_down(ctx, res_src=ctx.u, dst=ctx.x1_scr, width=W_A,
                                   name="dn1")),
        ("ln2", lambda: phase_ln(ctx, src=ctx.x1_scr, width=W_A, name="ln2")),
        ("inproj", lambda: phase_inproj(ctx)),
        ("dt", lambda: phase_dt(ctx)),
        ("conv", lambda: phase_conv(ctx)),
        ("d1", lambda: phase_d1(ctx)),
        ("exchange", lambda: phase_exchange(ctx)),
        ("d2", lambda: phase_d2(ctx)),
        ("outproj", lambda: phase_outproj(ctx)),
        ("ln3", lambda: phase_ln(ctx, src=ctx.x2_scr, width=TOK, name="ln3")),
        ("gu2", lambda: phase_gateup(ctx, width=TOK, name="gu2")),
        ("dn2", lambda: phase_down(ctx, res_src=ctx.x2_scr, dst=ctx.o, width=TOK,
                                   name="dn2")),
    ]
    for pname, fn in phases:
        fn()
        if STOP_AFTER == pname:
            break
    if "xln" in DBG and ctx.xln is not None:
        w = ctx.xln_w
        xs = nc.dram_tensor("dbg_xln", [128, NJ * w], BF16, kind="ExternalOutput").ap()
        b_dL = ctx.dC.v
        with nc.Block("dumpx") as blkx:
            @blkx.sync
            def _(s):
                wb(s, ctx)
                inc(s.dma_start(out=xs, in_=ctx.xln[:]), ctx.dC, 16)
                s.wait_ge(ctx.dC.h, b_dL + 16)
    if [d for d in DBG if d != "xln"]:
        phase_dbg(ctx)
    if getattr(ctx, "xln", None) is not None:
        ctx.xln_scope.close()
    if getattr(ctx, "S0", None) is not None and not getattr(ctx, "_s0_closed", False):
        try:
            ctx.S0_scope.close()
        except Exception:
            pass
    es.close()
    return nc


def endbar(ctx, blk, waits=None):
    sems = [ctx.dC, ctx.dH, ctx.PE, ctx.VE, ctx.SE, ctx.PO, ctx.CC]
    for p in (ctx.P_LD, ctx.P_W, ctx.P_ST, ctx.P_X, ctx.P_PS, ctx.P_H):
        sems += p.s
    targets = [(s.h, s.v) for s in sems]
    ctx.bar_n += 5
    for name in ("sync", "gpsimd", "tensor", "scalar", "vector"):
        def body(eng):
            for h, v in targets:
                if v:
                    eng.wait_ge(h, v)
            eng.sem_inc(ctx.BAR.h, 1)
        getattr(blk, name)(body)
    ctx.BAR.v += 5


def wb(eng, ctx):
    if ctx.bar_n:
        eng.wait_ge(ctx.BAR.h, ctx.bar_n)


def phase_init(ctx):
    nc = ctx.nc
    dC, PE, VE, SE = ctx.dC, ctx.PE, ctx.VE, ctx.SE
    b_dC, b_PE, b_VE, b_SE = dC.v, PE.v, VE.v, SE.v
    with nc.Block("init") as blk:
        @blk.sync
        def _(s):
            inc(s.dma_start(out=ctx.maskb[:], in_=ctx.maskb_c), dC, 16)
            inc(s.dma_start(out=ctx.onesrow[:], in_=ctx.onesrow_c), dC, 16)
            inc(s.dma_start(out=ctx.onescolbf[:], in_=ctx.onescolbf_c), dC, 16)
            inc(s.dma_start(out=ctx.idf32[:], in_=ctx.idf32_c), dC, 16)
            inc(s.dma_start(out=ctx.cols[:, 0:1], in_=ctx.dtb), dC, 16)
            inc(s.dma_start(out=ctx.cols[:, 2:3], in_=ctx.alog), dC, 16)
            inc(s.dma_start(out=ctx.cols[:, 3:4], in_=ctx.zcol_c), dC, 16)
            inc(s.dma_start(out=ctx.nw[:], in_=ctx.nwin), dC, 16)
            inc(s.dma_start(out=ctx.dprow_sb[:], in_=ctx.dprow), dC, 16)
            inc(s.dma_start(out=ctx.halves_sb[:], in_=ctx.halves_c), dC, 16)
            inc(s.dma_start(out=ctx.dppk_sb[:],
                            in_=ctx.dprow.rearrange("u (m par) -> (u par) m", par=2),
                            allow_slow_non_contiguous=True), dC, 16)
        NL = 11

        @blk.scalar
        def _(sc):
            sc.wait_ge(dC.h, b_dC + 16 * NL)
            inc(sc.activation(out=ctx.cols[:, 1:2], in_=ctx.cols[:, 2:3], func=AF.Exp),
                SE, 1)

        @blk.tensor
        def _(t):
            t.wait_ge(dC.h, b_dC + 16 * NL)
            inc(t.matmul(ctx.ps[0][:, 0:NHEADS // 2], ctx.halves_sb[:], ctx.dppk_sb[:],
                         start=True, stop=True), PE, 1)

        @blk.vector
        def _(v):
            v.wait_ge(SE.h, b_SE + 1)
            v.tensor_scalar_mul(out=ctx.cols[:, 1:2], in0=ctx.cols[:, 1:2], scalar1=-1.0)
            v.wait_ge(PE.h, b_PE + 1)
            inc(v.tensor_copy(out=ctx.dp_pair[:], in_=ctx.ps[0][:, 0:NHEADS // 2]),
                VE, 1)

        endbar(ctx, blk)


def phase_ln(ctx, src, width, name):
    nc = ctx.nc
    tsubs = TSUBS_A if width == W_A else TSUBS_F
    NT = len(tsubs)
    P_LD, PE, VE, SE = ctx.P_LD, ctx.PE, ctx.VE, ctx.SE
    b_LD, b_PE, b_VE, b_SE = P_LD.base(), PE.v, VE.v, SE.v

    ctx.xln_scope = contextlib.ExitStack()
    ctx.xln = ctx.xln_scope.enter_context(
        nc.sbuf_tensor(f"{name}_xln", [128, NJ * width], BF16))
    ctx.xln_w = width

    with contextlib.ExitStack() as psx:
        ubuf = psx.enter_context(nc.sbuf_tensor(f"{name}_ubuf", [128, NJ * 512], F32))
        xb = [psx.enter_context(nc.sbuf_tensor(f"{name}_xb{i}", [128, 512], BF16)) for i in range(2)]
        sq = [psx.enter_context(nc.sbuf_tensor(f"{name}_sq{i}", [128, 512], BF16)) for i in range(2)]
        rows = psx.enter_context(nc.sbuf_tensor(f"{name}_rows", [1, 1536], F32))
        blk = psx.enter_context(nc.Block(name))

        @blk.sync
        def _(s):
            wb(s, ctx)
            for ti, (t0, tw) in enumerate(tsubs):
                if ti >= 1:
                    s.wait_ge(VE.h, b_VE + ti * 5)
                sp_inc(s.dma_start(
                    out=ubuf[:].rearrange("p (j t) -> p j t", j=NJ)[:, :, 0:tw],
                    in_=src[:, t0:t0 + tw].rearrange("(j p) t -> p j t", p=128)),
                    P_LD, ti)

        @blk.scalar
        def _(sc):
            wb(sc, ctx)
            for ti, (t0, tw) in enumerate(tsubs):
                sp_wait(sc, P_LD, b_LD, ti)
                for j in range(NJ):
                    bb = j % 2
                    gi = ti * NJ + j
                    if gi >= 2:
                        sc.wait_ge(PE.h, b_PE + (gi - 2) // NJ * 17 + (gi - 2) % NJ + 1)
                    uj = ubuf[:, j * 512:j * 512 + tw]
                    inc(sc.activation(out=xb[bb][:, 0:tw], in_=uj, func=AF.Copy), SE, 1)
                    inc(sc.activation(out=sq[bb][:, 0:tw], in_=uj, func=AF.Square), SE, 1)
                sc.wait_ge(VE.h, b_VE + ti * 5 + 3)
                inc(sc.activation(out=rows[0:1, 1024:1024 + tw],
                                  in_=rows[0:1, 1024:1024 + tw], func=AF.Ln), SE, 1)
                sc.wait_ge(SE.h, b_SE + ti * 35 + 33)
                inc(sc.activation(out=rows[0:1, 1024:1024 + tw],
                                  in_=rows[0:1, 1024:1024 + tw], func=AF.Exp,
                                  scale=-0.5), SE, 2)

        @blk.tensor
        def _(t):
            wb(t, ctx)
            for ti, (t0, tw) in enumerate(tsubs):
                if ti >= 1:
                    t.wait_ge(VE.h, b_VE + (ti - 1) * 5 + 2)
                for j in range(NJ):
                    bb = j % 2
                    t.wait_ge(SE.h, b_SE + ti * 35 + 2 * (j + 1))
                    t.matmul(ctx.ps[0][0:1, 0:tw], ctx.onescolbf[:], xb[bb][:, 0:tw],
                             start=(j == 0), stop=(j == NJ - 1))
                    inc(t.matmul(ctx.ps[1][0:1, 0:tw], ctx.onescolbf[:], sq[bb][:, 0:tw],
                                 start=(j == 0), stop=(j == NJ - 1)), PE, 1)
                t.wait_ge(SE.h, b_SE + (ti + 1) * 35)
                if ti >= 1:
                    t.wait_ge(VE.h, b_VE + ti * 5)
                t.matmul(ctx.ps[2][:, 0:tw], ctx.onesrow[:], rows[0:1, 0:tw],
                         start=True, stop=True)
                inc(t.matmul(ctx.ps[3][:, 0:tw], ctx.onesrow[:], rows[0:1, 1024:1024 + tw],
                             start=True, stop=True), PE, 1)

        @blk.vector
        def _(v):
            wb(v, ctx)
            for ti, (t0, tw) in enumerate(tsubs):
                v.wait_ge(PE.h, b_PE + ti * 17 + 16)
                v.tensor_scalar_mul(out=rows[0:1, 0:tw], in0=ctx.ps[0][0:1, 0:tw],
                                    scalar1=1.0 / D_MODEL)
                inc(v.tensor_scalar_mul(out=rows[0:1, 512:512 + tw],
                                        in0=ctx.ps[1][0:1, 0:tw],
                                        scalar1=1.0 / D_MODEL), VE, 1)
                v.wait_ge(VE.h, b_VE + ti * 5 + 1)
                inc(v.tensor_mul(out=rows[0:1, 1024:1024 + tw], in0=rows[0:1, 0:tw],
                                 in1=rows[0:1, 0:tw]), VE, 1)
                v.wait_ge(VE.h, b_VE + ti * 5 + 2)
                inc(v.scalar_tensor_tensor(out=rows[0:1, 1024:1024 + tw],
                                           in0=rows[0:1, 512:512 + tw], scalar=1e-5,
                                           in1=rows[0:1, 1024:1024 + tw],
                                           op0=ALU.add, op1=ALU.subtract), VE, 1)
                v.wait_ge(PE.h, b_PE + (ti + 1) * 17)
                last = None
                for j in range(NJ):
                    uj = ubuf[:, j * 512:j * 512 + tw]
                    last = v.tensor_sub(out=uj, in0=uj, in1=ctx.ps[2][:, 0:tw])
                inc(last, VE, 1)
                v.wait_ge(VE.h, b_VE + ti * 5 + 4)
                last = None
                for j in range(NJ):
                    uj = ubuf[:, j * 512:j * 512 + tw]
                    xo = ctx.xln[:, j * width + t0:j * width + t0 + tw]
                    last = v.tensor_mul(out=xo, in0=uj, in1=ctx.ps[3][:, 0:tw])
                inc(last, VE, 1)

        endbar(ctx, blk)


def phase_gateup(ctx, width, name):
    nc = ctx.nc
    tsubs = TSUBS_A if width == W_A else TSUBS_F
    NT = len(tsubs)
    NP = NI
    P_W, P_ST, PE, VE, SE = ctx.P_W, ctx.P_ST, ctx.PE, ctx.VE, ctx.SE
    b_W, b_ST, b_PE, b_VE, b_SE = P_W.base(), P_ST.base(), PE.v, VE.v, SE.v

    with contextlib.ExitStack() as psx:
        wg = [psx.enter_context(nc.sbuf_tensor(f"{name}_wg{i}", [128, NJ * 128], BF16)) for i in range(2)]
        wu = [psx.enter_context(nc.sbuf_tensor(f"{name}_wu{i}", [128, NJ * 128], BF16)) for i in range(2)]
        sg = [psx.enter_context(nc.sbuf_tensor(f"{name}_sg{i}", [128, 512], BF16)) for i in range(2)]
        hb = [psx.enter_context(nc.sbuf_tensor(f"{name}_hb{i}", [128, width], BF16)) for i in range(2)]
        blk = psx.enter_context(nc.Block(name))

        @blk.gpsimd
        def _(g):
            wb(g, ctx)
            for k in range(NP):
                bb = k % 2
                if k >= 2:
                    g.wait_ge(PE.h, b_PE + (k - 1) * NT)
                sp_inc(g.dma_start(out=wg[bb][:].rearrange("p (j o) -> p j o", j=NJ),
                                   in_=ctx.wgu[:, k * 128:(k + 1) * 128]
                                   .rearrange("(j p) o -> p j o", p=128)), P_W, k)
                sp_inc(g.dma_start(out=wu[bb][:].rearrange("p (j o) -> p j o", j=NJ),
                                   in_=ctx.wgu[:, INTER + k * 128:INTER + (k + 1) * 128]
                                   .rearrange("(j p) o -> p j o", p=128)), P_W, k)

        @blk.tensor
        def _(t):
            wb(t, ctx)
            for k in range(NP):
                bb = k % 2
                sp_wait(t, P_W, b_W, k, 32)
                for ti, (t0, tw) in enumerate(tsubs):
                    it = k * NT + ti
                    pg = ctx.ps[4 + 2 * (it % 2)]
                    pu = ctx.ps[5 + 2 * (it % 2)]
                    if it >= 2:
                        t.wait_ge(VE.h, b_VE + it - 1)
                    mm = None
                    for j in range(NJ):
                        t.matmul(pg[:, 0:tw], wg[bb][:, j * 128:(j + 1) * 128],
                                 ctx.xln[:, j * width + t0:j * width + t0 + tw],
                                 start=(j == 0), stop=(j == NJ - 1))
                        mm = t.matmul(pu[:, 0:tw], wu[bb][:, j * 128:(j + 1) * 128],
                                      ctx.xln[:, j * width + t0:j * width + t0 + tw],
                                      start=(j == 0), stop=(j == NJ - 1))
                    inc(mm, PE, 1)

        @blk.scalar
        def _(sc):
            wb(sc, ctx)
            for k in range(NP):
                for ti, (t0, tw) in enumerate(tsubs):
                    it = k * NT + ti
                    pg = ctx.ps[4 + 2 * (it % 2)]
                    sc.wait_ge(PE.h, b_PE + it + 1)
                    if it >= 2:
                        sc.wait_ge(VE.h, b_VE + it - 1)
                    inc(sc.activation(out=sg[it % 2][:, 0:tw], in_=pg[:, 0:tw],
                                      func=AF.Silu), SE, 1)

        @blk.vector
        def _(v):
            wb(v, ctx)
            for k in range(NP):
                hbuf = hb[k % 2]
                if k >= 2:
                    sp_wait(v, P_ST, b_ST, k - 2)
                for ti, (t0, tw) in enumerate(tsubs):
                    it = k * NT + ti
                    pu = ctx.ps[5 + 2 * (it % 2)]
                    v.wait_ge(SE.h, b_SE + it + 1)
                    inc(v.tensor_mul(out=hbuf[:, t0:t0 + tw], in0=sg[it % 2][:, 0:tw],
                                     in1=pu[:, 0:tw]), VE, 1)

        @blk.sync
        def _(s):
            wb(s, ctx)
            for k in range(NP):
                s.wait_ge(VE.h, b_VE + (k + 1) * NT)
                sp_inc(s.dma_start(out=ctx.h_scr[k * 128:(k + 1) * 128, 0:width],
                                   in_=hb[k % 2][:]), P_ST, k)

        endbar(ctx, blk)
    ctx.xln_scope.close()
    ctx.xln = None


def phase_down(ctx, res_src, dst, width, name):
    nc = ctx.nc
    tsubs = TSUBS_A if width == W_A else TSUBS_F
    NT = len(tsubs)
    P_H, P_LD, P_W, P_ST, PE, VE = ctx.P_H, ctx.P_LD, ctx.P_W, ctx.P_ST, ctx.PE, ctx.VE
    b_H, b_LD, b_W, b_ST, b_PE, b_VE = P_H.base(), P_LD.base(), P_W.base(), P_ST.base(), PE.v, VE.v

    with contextlib.ExitStack() as psx:
        hseg = [psx.enter_context(nc.sbuf_tensor(f"{name}_hseg{i}", [128, NI * 512], BF16))
                for i in range(2)]
        wd = [psx.enter_context(nc.sbuf_tensor(f"{name}_wd{i}", [128, NI * 128], BF16)) for i in range(2)]
        ub = [psx.enter_context(nc.sbuf_tensor(f"{name}_ub{i}", [128, 512], F32)) for i in range(2)]
        xo = [psx.enter_context(nc.sbuf_tensor(f"{name}_xo{i}", [128, 512], F32)) for i in range(2)]
        blk = psx.enter_context(nc.Block(name))

        @blk.sync
        def _(s):
            wb(s, ctx)
            for ti, (t0, tw) in enumerate(tsubs):
                if ti >= 2:
                    s.wait_ge(PE.h, b_PE + (ti - 1) * NJ)
                sp_inc(s.dma_start(
                    out=hseg[ti % 2][:].rearrange("p (i t) -> p i t", i=NI)[:, :, 0:tw],
                    in_=ctx.h_scr[:, t0:t0 + tw].rearrange("(i p) t -> p i t", p=128)),
                    P_H, ti)
                for d in range(NJ):
                    it = ti * NJ + d
                    if it >= 2:
                        s.wait_ge(VE.h, b_VE + it - 1)
                    sp_inc(s.dma_start(out=ub[d % 2][:, 0:tw],
                                       in_=res_src[d * 128:(d + 1) * 128, t0:t0 + tw]),
                           P_LD, it)

        @blk.gpsimd
        def _(g):
            wb(g, ctx)
            for ti in range(NT):
                for d in range(NJ):
                    it = ti * NJ + d
                    if it >= 2:
                        g.wait_ge(PE.h, b_PE + it - 1)
                    sp_inc(g.dma_start(out=wd[d % 2][:].rearrange("p (i d) -> p i d", i=NI),
                                       in_=ctx.wdn[:, d * 128:(d + 1) * 128]
                                       .rearrange("(i p) d -> p i d", p=128)), P_W, it)

        @blk.tensor
        def _(t):
            wb(t, ctx)
            for ti, (t0, tw) in enumerate(tsubs):
                sp_wait(t, P_H, b_H, ti)
                for d in range(NJ):
                    it = ti * NJ + d
                    pd = ctx.ps[4 + it % 2]
                    sp_wait(t, P_W, b_W, it)
                    if it >= 2:
                        t.wait_ge(VE.h, b_VE + it - 1)
                    mm = None
                    for i in range(NI):
                        mm = t.matmul(pd[:, 0:tw], wd[d % 2][:, i * 128:(i + 1) * 128],
                                      hseg[ti % 2][:, i * 512:i * 512 + tw],
                                      start=(i == 0), stop=(i == NI - 1))
                    inc(mm, PE, 1)

        @blk.vector
        def _(v):
            wb(v, ctx)
            for ti, (t0, tw) in enumerate(tsubs):
                for d in range(NJ):
                    it = ti * NJ + d
                    pd = ctx.ps[4 + it % 2]
                    v.wait_ge(PE.h, b_PE + it + 1)
                    sp_wait(v, P_LD, b_LD, it)
                    if it >= 2:
                        sp_wait(v, P_ST, b_ST, it - 2)
                    inc(v.scalar_tensor_tensor(out=xo[it % 2][:, 0:tw], in0=pd[:, 0:tw],
                                               scalar=0.5, in1=ub[d % 2][:, 0:tw],
                                               op0=ALU.mult, op1=ALU.add), VE, 1)

        @blk.scalar
        def _(sc):
            wb(sc, ctx)
            for ti, (t0, tw) in enumerate(tsubs):
                for d in range(NJ):
                    it = ti * NJ + d
                    sc.wait_ge(VE.h, b_VE + it + 1)
                    sp_inc(sc.dma_start(out=dst[d * 128:(d + 1) * 128, t0:t0 + tw],
                                        in_=xo[it % 2][:, 0:tw]), P_ST, it)

        endbar(ctx, blk)


def phase_inproj(ctx):
    nc = ctx.nc
    tsubs = TSUBS_A
    NT = len(tsubs)
    P_W, P_ST, PE, SE = ctx.P_W, ctx.P_ST, ctx.PE, ctx.SE
    b_W, b_ST, b_PE, b_SE = P_W.base(), P_ST.base(), PE.v, SE.v

    with contextlib.ExitStack() as psx:
        wbuf = [psx.enter_context(nc.sbuf_tensor(f"ip_wb{i}", [128, NJ * 128], BF16)) for i in range(2)]
        ob = [psx.enter_context(nc.sbuf_tensor(f"ip_ob{i}", [128, W_A], BF16)) for i in range(2)]
        blk = psx.enter_context(nc.Block("inproj"))

        @blk.gpsimd
        def _(g):
            wb(g, ctx)
            for k in range(NO_IN):
                ow = 128 if k < 66 else 64
                if k >= 2:
                    g.wait_ge(PE.h, b_PE + (k - 1) * NT)
                sp_inc(g.dma_start(out=wbuf[k % 2][:].rearrange("p (j o) -> p j o", j=NJ)
                                   [:, :, 0:ow],
                                   in_=ctx.win[:, k * 128:k * 128 + ow]
                                   .rearrange("(j p) o -> p j o", p=128)), P_W, k)

        @blk.tensor
        def _(t):
            wb(t, ctx)
            for k in range(NO_IN):
                ow = 128 if k < 66 else 64
                sp_wait(t, P_W, b_W, k)
                for ti, (t0, tw) in enumerate(tsubs):
                    it = k * NT + ti
                    pp = ctx.ps[4 + it % 2]
                    if it >= 2:
                        t.wait_ge(SE.h, b_SE + it - 1)
                    mm = None
                    for j in range(NJ):
                        mm = t.matmul(pp[0:ow, 0:tw], wbuf[k % 2][:, j * 128:j * 128 + ow],
                                      ctx.xln[:, j * W_A + t0:j * W_A + t0 + tw],
                                      start=(j == 0), stop=(j == NJ - 1))
                    inc(mm, PE, 1)

        @blk.scalar
        def _(sc):
            wb(sc, ctx)
            for k in range(NO_IN):
                ow = 128 if k < 66 else 64
                for ti, (t0, tw) in enumerate(tsubs):
                    it = k * NT + ti
                    pp = ctx.ps[4 + it % 2]
                    sc.wait_ge(PE.h, b_PE + it + 1)
                    if k >= 2 and ti == 0 and k < 66:
                        sp_wait(sc, P_ST, b_ST, k - 2)
                    if k == 66:
                        inc(sc.activation(out=ctx.dtraw[:, t0:t0 + tw],
                                          in_=pp[0:64, 0:tw], func=AF.Copy), SE, 1)
                    else:
                        inc(sc.activation(out=ob[k % 2][:, t0:t0 + tw], in_=pp[:, 0:tw],
                                          func=AF.Copy), SE, 1)

        @blk.sync
        def _(s):
            wb(s, ctx)
            for k in range(66):
                s.wait_ge(SE.h, b_SE + (k + 1) * NT)
                if k < 32:
                    sp_inc(s.dma_start(out=ctx.z_scr[k * 128:(k + 1) * 128, :],
                                       in_=ob[k % 2][:, HALO:W_A]), P_ST, k)
                else:
                    kk = k - 32
                    sp_inc(s.dma_start(out=ctx.xbcp_scr[kk * 128:(kk + 1) * 128, :],
                                       in_=ob[k % 2][:]), P_ST, k)

        endbar(ctx, blk)
    ctx.xln_scope.close()
    ctx.xln = None


def phase_dt(ctx):
    nc = ctx.nc
    dC, PE, VE, SE = ctx.dC, ctx.PE, ctx.VE, ctx.SE
    b_dC, b_PE, b_VE, b_SE = dC.v, PE.v, VE.v, SE.v

    with contextlib.ExitStack() as psx:
        tmp = psx.enter_context(nc.sbuf_tensor("dt_tmp", [NHEADS, TOK], F32))
        segmask = psx.enter_context(nc.sbuf_tensor("dt_segmask", [NHEADS, TOK], F32))
        tmp2 = psx.enter_context(nc.sbuf_tensor("dt_tmp2", [NHEADS, TOK], F32))
        trow = psx.enter_context(nc.sbuf_tensor("dt_trow", [16, NHEADS], F32))
        trow2 = psx.enter_context(nc.sbuf_tensor("dt_trow2", [16, NHEADS], F32))
        trow1 = psx.enter_context(nc.sbuf_tensor("dt_trow1", [1, NHEADS], F32))
        blk = psx.enter_context(nc.Block("dt"))

        @blk.vector
        def _(v):
            wb(v, ctx)
            inc(v.tensor_scalar_add(out=tmp[:], in0=ctx.dtraw[:, HALO:W_A],
                                    scalar1=ctx.cols[:, 0:1]), VE, 1)
            v.wait_ge(SE.h, b_SE + 3)
            inc(v.scalar_tensor_tensor(out=tmp[:], in0=tmp[:], scalar=0.0,
                                       in1=tmp2[:], op0=ALU.max, op1=ALU.add), VE, 1)
            v.wait_ge(VE.h, b_VE + 2)
            inc(v.tensor_scalar_mul(out=ctx.Ab[:], in0=tmp[:],
                                    scalar1=ctx.cols[:, 1:2]), VE, 1)
            v.wait_ge(VE.h, b_VE + 3)
            v.wait_ge(dC.h, b_dC + 16)
            inc(v.tensor_tensor_scan(out=ctx.cumA[:], data0=segmask[:],
                                     data1=ctx.Ab[:], initial=0.0,
                                     op0=ALU.mult, op1=ALU.add), VE, 1)
            v.wait_ge(VE.h, b_VE + 4)
            inc(v.tensor_copy(out=ctx.tots[:].rearrange("h (c u) -> h c u", u=1),
                              in_=ctx.cumA[:].rearrange("h (c l) -> h c l", l=CHUNK)
                              [:, :, CHUNK - 1:CHUNK]), VE, 1)
            v.wait_ge(VE.h, b_VE + 5)
            v.tensor_tensor_scan(out=ctx.ptot[:, 1:NC + 1], data0=segmask[:, 0:NC],
                                 data1=ctx.tots[:], initial=0.0,
                                 op0=ALU.mult, op1=ALU.add)
            inc(v.tensor_copy(out=ctx.ptot[:, 0:1], in_=ctx.cols[:, 3:4]), VE, 1)
            v.wait_ge(PE.h, b_PE + 1)
            v.tensor_copy(out=trow[:], in_=ctx.ps[0][0:16, 0:NHEADS])
            v.tensor_copy(out=trow2[:], in_=ctx.ps[1][0:16, 0:NHEADS])
            inc(v.tensor_copy(out=trow1[:], in_=ctx.ps[2][0:1, 0:NHEADS]), VE, 1)

        @blk.scalar
        def _(sc):
            wb(sc, ctx)
            sc.wait_ge(VE.h, b_VE + 1)
            inc(sc.activation(out=tmp2[:], in_=tmp[:], func=AF.Abs), SE, 1)
            sc.wait_ge(SE.h, b_SE + 1)
            inc(sc.activation(out=tmp2[:], in_=tmp2[:], func=AF.Exp, scale=-1.0), SE, 1)
            sc.wait_ge(SE.h, b_SE + 2)
            inc(sc.activation(out=tmp2[:], in_=tmp2[:], func=AF.Ln, bias=1.0), SE, 1)

        @blk.tensor
        def _(t):
            wb(t, ctx)
            t.wait_ge(VE.h, b_VE + 6)
            t.transpose(ctx.ps[0][0:16, 0:NHEADS], ctx.tots[:], ctx.idf32[0:NHEADS, 0:NHEADS])
            t.transpose(ctx.ps[1][0:16, 0:NHEADS], ctx.ptot[:, 0:NC],
                        ctx.idf32[0:NHEADS, 0:NHEADS])
            inc(t.transpose(ctx.ps[2][0:1, 0:NHEADS], ctx.ptot[:, NC:NC + 1],
                            ctx.idf32[0:NHEADS, 0:NHEADS]), PE, 1)

        @blk.sync
        def _(s):
            wb(s, ctx)
            inc(s.dma_start(out=segmask[:], in_=ctx.segmask_c), dC, 16)
            s.wait_ge(VE.h, b_VE + 7)
            inc(s.dma_start(out=ctx.cums_scr, in_=ctx.cumA[:]), dC, 16)
            inc(s.dma_start(out=ctx.rows_scr[0:16, :], in_=trow[:]), dC, 16)
            inc(s.dma_start(out=ctx.rows_scr[16:32, :], in_=trow2[:]), dC, 16)
            inc(s.dma_start(out=ctx.pub_scr[0:1, 128 * D_INNER:PUBN], in_=trow1[:]),
                dC, 16)

        endbar(ctx, blk)


def phase_conv(ctx):
    nc = ctx.nc
    dC, P_LD, P_PS, VE, SE = ctx.dC, ctx.P_LD, ctx.P_PS, ctx.VE, ctx.SE
    b_dC, b_LD, b_PS, b_VE, b_SE = dC.v, P_LD.base(), P_PS.base(), VE.v, SE.v

    with contextlib.ExitStack() as psx:
        xin = [psx.enter_context(nc.sbuf_tensor(f"cv_xin{i}", [128, W_A], BF16)) for i in range(2)]
        acc = [psx.enter_context(nc.sbuf_tensor(f"cv_acc{i}", [128, TOK], F32)) for i in range(2)]
        outb = [psx.enter_context(nc.sbuf_tensor(f"cv_out{i}", [128, TOK], BF16)) for i in range(2)]
        wcv = psx.enter_context(nc.sbuf_tensor("cv_w", [128, NDC * 4], F32))
        bcv = psx.enter_context(nc.sbuf_tensor("cv_b", [128, NDC], F32))
        blk = psx.enter_context(nc.Block("conv"))

        @blk.sync
        def _(s):
            wb(s, ctx)
            inc(s.dma_start(out=wcv[:], in_=ctx.wconv), dC, 16)
            inc(s.dma_start(out=bcv[:], in_=ctx.bconv), dC, 16)
            for dc in range(NDC):
                if dc >= 2:
                    s.wait_ge(VE.h, b_VE + 4 * (dc - 1))
                sp_inc(s.dma_start(out=xin[dc % 2][:],
                                   in_=ctx.xbcp_scr[dc * 128:(dc + 1) * 128, :]), P_LD, dc)

        @blk.vector
        def _(v):
            wb(v, ctx)
            v.wait_ge(dC.h, b_dC + 32)
            for dc in range(NDC):
                bb = dc % 2
                sp_wait(v, P_LD, b_LD, dc)
                if dc >= 2:
                    v.wait_ge(SE.h, b_SE + dc - 1)
                inc(v.tensor_scalar_mul(out=acc[bb][:], in0=xin[bb][:, 0:TOK],
                                        scalar1=wcv[:, 4 * dc:4 * dc + 1]), VE, 1)
                for kk in range(1, 4):
                    v.wait_ge(VE.h, b_VE + 4 * dc + kk)
                    inc(v.scalar_tensor_tensor(out=acc[bb][:], in0=xin[bb][:, kk:kk + TOK],
                                               scalar=wcv[:, 4 * dc + kk:4 * dc + kk + 1],
                                               in1=acc[bb][:], op0=ALU.mult, op1=ALU.add),
                        VE, 1)

        @blk.scalar
        def _(sc):
            wb(sc, ctx)
            for dc in range(NDC):
                bb = dc % 2
                sc.wait_ge(VE.h, b_VE + 4 * (dc + 1))
                if dc >= 2:
                    sp_wait(sc, P_PS, b_PS, dc - 2)
                inc(sc.activation(out=outb[bb][:], in_=acc[bb][:], func=AF.Silu,
                                  bias=bcv[:, dc:dc + 1]), SE, 1)

        @blk.gpsimd
        def _(g):
            wb(g, ctx)
            for dc in range(NDC):
                g.wait_ge(SE.h, b_SE + dc + 1)
                sp_inc(g.dma_start(out=ctx.xbc_scr[dc * 128:(dc + 1) * 128, :],
                                   in_=outb[dc % 2][:]), P_PS, dc)

        endbar(ctx, blk)


def phase_d1(ctx):
    nc = ctx.nc
    P_LD, P_W, P_ST, P_PS, PE, VE, SE = (ctx.P_LD, ctx.P_W, ctx.P_ST, ctx.P_PS,
                                         ctx.PE, ctx.VE, ctx.SE)
    b_LD, b_W, b_ST, b_PS = P_LD.base(), P_W.base(), P_ST.base(), P_PS.base()
    b_PE, b_VE, b_SE = PE.v, VE.v, SE.v
    PE_PER = 2 + NHEADS
    VE_PER = 2 + NHEADS

    def ve_nrev(c):
        return b_VE + c * VE_PER + 1

    def ve_xsc(c):
        return b_VE + c * VE_PER + 2

    def ve_scan(c, h):
        return b_VE + c * VE_PER + 2 + (h + 1)

    def pe_tr(c):
        return b_PE + c * PE_PER + 1

    def pe_repl(c):
        return b_PE + c * PE_PER + 2

    def pe_mmc(c, h):
        return b_PE + c * PE_PER + 2 + (h + 1)

    with contextlib.ExitStack() as psx:
        xtm = [psx.enter_context(nc.sbuf_tensor(f"d1_xtm{i}", [128, D_INNER], BF16)) for i in range(2)]
        btm = [psx.enter_context(nc.sbuf_tensor(f"d1_btm{i}", [128, 128], BF16)) for i in range(2)]
        xsc = [psx.enter_context(nc.sbuf_tensor(f"d1_xsc{i}", [128, D_INNER], BF16)) for i in range(2)]
        sel = psx.enter_context(nc.sbuf_tensor("d1_sel", [128, D_INNER], F32))
        dect = [psx.enter_context(nc.sbuf_tensor(f"d1_dect{i}", [128, NHEADS], F32)) for i in range(2)]
        decc = [psx.enter_context(nc.sbuf_tensor(f"d1_decc{i}", [128, NHEADS], F32)) for i in range(2)]
        nrev = [psx.enter_context(nc.sbuf_tensor(f"d1_nrev{i}", [NHEADS, CHUNK], F32)) for i in range(2)]
        rowb = [psx.enter_context(nc.sbuf_tensor(f"d1_rowb{i}", [1, NHEADS], F32)) for i in range(2)]
        blk = psx.enter_context(nc.Block("d1"))

        @blk.sync
        def _(s):
            wb(s, ctx)
            for c in range(NC):
                if c >= 2:
                    s.wait_ge(PE.h, pe_mmc(c - 1, NHEADS - 1))
                    s.wait_ge(VE.h, ve_xsc(c - 1))
                sp_inc(s.dma_start_transpose(out=xtm[c % 2][:],
                                             in_=ctx.xbc_scr[0:D_INNER,
                                                             c * CHUNK:(c + 1) * CHUNK]),
                       P_LD, c)
                sp_inc(s.dma_start_transpose(out=btm[c % 2][:],
                                             in_=ctx.xbc_scr[D_INNER:D_INNER + 128,
                                                             c * CHUNK:(c + 1) * CHUNK]),
                       P_LD, c)

        @blk.gpsimd
        def _(g):
            wb(g, ctx)
            for c in range(NC):
                if c >= 2:
                    g.wait_ge(PE.h, pe_repl(c - 1))
                sp_inc(g.dma_start(out=rowb[c % 2][:], in_=ctx.rows_scr[c:c + 1, :]),
                       P_W, c)
                g.wait_ge(VE.h, ve_scan(c, NHEADS - 1))
                if c < NC - 1:
                    sp_inc(g.dma_start(out=ctx.sel_scr[(c + 1) * 128:(c + 2) * 128, :],
                                       in_=sel[:]), P_PS, c)

        @blk.vector
        def _(v):
            wb(v, ctx)
            for c in range(NC):
                cs = ctx.cumA[:, c * CHUNK:(c + 1) * CHUNK]
                if c >= 2:
                    v.wait_ge(PE.h, pe_tr(c - 1))
                inc(v.tensor_scalar(out=nrev[c % 2][:], in0=cs,
                                    scalar1=ctx.cumA[:, (c + 1) * CHUNK - 1:(c + 1) * CHUNK],
                                    scalar2=None, op0=ALU.subtract), VE, 1)
                v.wait_ge(SE.h, b_SE + 2 * c + 1)
                sp_wait(v, P_LD, b_LD, c, 32)
                if c >= 2:
                    v.wait_ge(PE.h, pe_mmc(c - 2, NHEADS - 1))
                inc(v.tensor_mul(out=xsc[c % 2][:].rearrange("p (h w) -> p h w", h=NHEADS),
                                 in0=xtm[c % 2][:].rearrange("p (h w) -> p h w", h=NHEADS),
                                 in1=dect[c % 2][:].rearrange("p (h w) -> p h w", w=1)
                                 .broadcast_to([128, NHEADS, HEADDIM])), VE, 1)
                v.wait_ge(SE.h, b_SE + 2 * c + 2)
                if c >= 1:
                    sp_wait(v, P_PS, b_PS, c - 1)
                for h in range(NHEADS):
                    pb = ctx.ps[4 + h % 2]
                    v.wait_ge(PE.h, pe_mmc(c, h))
                    if c == 0:
                        inc(v.tensor_copy(out=sel[:, h * 64:(h + 1) * 64],
                                          in_=pb[:, 0:64]), VE, 1)
                    else:
                        inc(v.scalar_tensor_tensor(out=sel[:, h * 64:(h + 1) * 64],
                                                   in0=sel[:, h * 64:(h + 1) * 64],
                                                   scalar=decc[c % 2][:, h:h + 1],
                                                   in1=pb[:, 0:64],
                                                   op0=ALU.mult, op1=ALU.add), VE, 1)

        @blk.scalar
        def _(sc):
            wb(sc, ctx)
            for c in range(NC):
                sc.wait_ge(PE.h, pe_tr(c))
                if c >= 2:
                    sc.wait_ge(VE.h, ve_xsc(c - 1))
                inc(sc.activation(out=dect[c % 2][:], in_=ctx.ps[2][:, 0:NHEADS],
                                  func=AF.Exp, scale=-1.0), SE, 1)
                sc.wait_ge(PE.h, pe_repl(c))
                if c >= 2:
                    sc.wait_ge(VE.h, ve_scan(c - 1, NHEADS - 1))
                inc(sc.activation(out=decc[c % 2][:], in_=ctx.ps[3][:, 0:NHEADS],
                                  func=AF.Exp), SE, 1)
                if c == NC - 1:
                    sc.wait_ge(VE.h, ve_scan(c, NHEADS - 1))
                    sp_inc(sc.dma_start(out=ctx.pub_scr[0:1, 0:128 * D_INNER]
                                        .rearrange("u (p f) -> (u p) f", p=128),
                                        in_=sel[:]), P_ST, c)

        @blk.tensor
        def _(t):
            wb(t, ctx)
            for c in range(NC):
                t.wait_ge(VE.h, ve_nrev(c))
                if c >= 1:
                    t.wait_ge(SE.h, b_SE + 2 * (c - 1) + 1)
                inc(t.transpose(ctx.ps[2][:, 0:NHEADS], nrev[c % 2][:],
                                ctx.idf32[0:NHEADS, 0:NHEADS]), PE, 1)
                sp_wait(t, P_W, b_W, c)
                if c >= 1:
                    t.wait_ge(SE.h, b_SE + 2 * (c - 1) + 2)
                inc(t.matmul(ctx.ps[3][:, 0:NHEADS], ctx.onesrow[:], rowb[c % 2][:],
                             start=True, stop=True), PE, 1)
                t.wait_ge(VE.h, ve_xsc(c))
                for h in range(NHEADS):
                    pb = ctx.ps[4 + h % 2]
                    if h >= 2:
                        t.wait_ge(VE.h, ve_scan(c, h - 2))
                    elif c >= 1:
                        t.wait_ge(VE.h, ve_scan(c - 1, NHEADS - 2 + h))
                    inc(t.matmul(pb[:, 0:64], btm[c % 2][:],
                                 xsc[c % 2][:, h * 64:(h + 1) * 64],
                                 start=True, stop=True), PE, 1)

        endbar(ctx, blk)


def phase_exchange(ctx):
    nc = ctx.nc
    P_W, P_X, P_ST, PE, VE, SE, CC = (ctx.P_W, ctx.P_X, ctx.P_ST,
                                      ctx.PE, ctx.VE, ctx.SE, ctx.CC)
    b_W, b_X, b_ST = P_W.base(), P_X.base(), P_ST.base()
    b_PE, b_VE, b_SE, b_CC = PE.v, VE.v, SE.v, CC.v

    ctx.S0_scope = contextlib.ExitStack()
    ctx.S0 = ctx.S0_scope.enter_context(nc.sbuf_tensor("S0", [128, D_INNER], F32))

    with contextlib.ExitStack() as psx:
        logd = psx.enter_context(nc.sbuf_tensor("ex_logd", [8, NHEADS], F32))
        btw = psx.enter_context(nc.sbuf_tensor("ex_btw", [8, 8], F32))
        prd = psx.enter_context(nc.sbuf_tensor("ex_prd", [8, 1], F32))
        coef = psx.enter_context(nc.sbuf_tensor("ex_coef", [8, NHEADS], F32))
        crow = [psx.enter_context(nc.sbuf_tensor(f"ex_crow{i}", [1, NHEADS], F32)) for i in range(2)]
        lbuf = [psx.enter_context(nc.sbuf_tensor(f"ex_lb{i}", [128, D_INNER], F32)) for i in range(2)]
        tmp = psx.enter_context(nc.sbuf_tensor("ex_tmp", [128, D_INNER], F32))
        blk = psx.enter_context(nc.Block("exchange"))

        @blk.gpsimd
        def _(g):
            wb(g, ctx)
            cc_i = g.collective_compute("AllGather", ALU.bypass,
                                        replica_groups=[list(range(8))],
                                        ins=[ctx.pub_scr.opt()],
                                        outs=[ctx.gat_scr.opt()])
            inc(cc_i, CC, 1)
            g.wait_ge(CC.h, b_CC + 1)
            inc(g.dma_start(out=logd[:],
                            in_=ctx.gat_scr[:, 128 * D_INNER:PUBN]), P_W.s[0], 16)
            inc(g.dma_start(out=btw[:], in_=ctx.btwT), P_W.s[0], 16)
            inc(g.dma_start(out=prd[:], in_=ctx.predc), P_W.s[0], 16)
            for j in range(8):
                if j == 0:
                    sp_wait(g, P_ST, b_ST, 0)
                if j >= 1:
                    g.wait_ge(PE.h, b_PE + 1 + j)
                inc(g.dma_start(out=crow[j % 2][:],
                                in_=ctx.rows_scr[32 + j:33 + j, :]), P_X.s[0], 16)
                if j >= 1:
                    g.wait_ge(VE.h, b_VE + 1 + 2 * j)
                inc(g.dma_start(out=lbuf[j % 2][:],
                                in_=ctx.gat_scr[j:j + 1, 0:128 * D_INNER]
                                .rearrange("u (p f) -> (u p) f", p=128)), P_X.s[1], 16)

        @blk.tensor
        def _(t):
            wb(t, ctx)
            t.wait_ge(P_W.s[0].h, b_W[0] + 48)
            inc(t.matmul(ctx.ps[0][0:8, 0:NHEADS], btw[:], logd[:], start=True,
                         stop=True), PE, 1)
            for j in range(8):
                t.wait_ge(P_X.s[0].h, b_X[0] + 16 * (j + 1))
                if j >= 1:
                    t.wait_ge(VE.h, b_VE + 1 + 2 * j)
                inc(t.matmul(ctx.ps[1][:, 0:NHEADS], ctx.onesrow[:], crow[j % 2][:],
                             start=True, stop=True), PE, 1)

        @blk.scalar
        def _(sc):
            wb(sc, ctx)
            sc.wait_ge(PE.h, b_PE + 1)
            inc(sc.activation(out=coef[:], in_=ctx.ps[0][0:8, 0:NHEADS], func=AF.Exp),
                SE, 1)

        @blk.vector
        def _(v):
            wb(v, ctx)
            v.wait_ge(SE.h, b_SE + 1)
            inc(v.tensor_scalar_mul(out=coef[:], in0=coef[:], scalar1=prd[:]), VE, 1)
            for j in range(8):
                v.wait_ge(PE.h, b_PE + 1 + (j + 1))
                v.wait_ge(P_X.s[1].h, b_X[1] + 16 * (j + 1))
                inc(v.tensor_mul(out=tmp[:].rearrange("p (h w) -> p h w", h=NHEADS),
                                 in0=lbuf[j % 2][:].rearrange("p (h w) -> p h w", h=NHEADS),
                                 in1=ctx.ps[1][:, 0:NHEADS]
                                 .rearrange("p (h w) -> p h w", w=1)
                                 .broadcast_to([128, NHEADS, HEADDIM])), VE, 1)
                v.wait_ge(VE.h, b_VE + 2 + 2 * j)
                if j == 0:
                    inc(v.tensor_copy(out=ctx.S0[:], in_=tmp[:]), VE, 1)
                else:
                    inc(v.tensor_add(out=ctx.S0[:], in0=ctx.S0[:], in1=tmp[:]), VE, 1)

        @blk.sync
        def _(s):
            wb(s, ctx)
            s.wait_ge(VE.h, b_VE + 1)
            sp_inc(s.dma_start(out=ctx.rows_scr[32:40, :], in_=coef[:]), P_ST, 0)

        endbar(ctx, blk)


def phase_d2(ctx):
    nc = ctx.nc
    P_LD, P_W, P_PS, PE, VE, SE, PO = (ctx.P_LD, ctx.P_W, ctx.P_PS,
                                       ctx.PE, ctx.VE, ctx.SE, ctx.PO)
    b_LD, b_W, b_PS = P_LD.base(), P_W.base(), P_PS.base()
    b_PE, b_VE, b_SE, b_PO = PE.v, VE.v, SE.v, PO.v
    NPAIRH = NHEADS // 2

    PE_PER = 3 + NHEADS + 2
    SE_PER = 3 + NHEADS + 3
    VE_PER = 4 + 8 + NHEADS + NPAIRH + 3 + NZ
    PO_PER = 1

    def pe_cbt(c):
        return b_PE + c * PE_PER + 1

    def pe_prec(c):
        return b_PE + c * PE_PER + 2

    def pe_cam(c):
        return b_PE + c * PE_PER + 3

    def pe_ymm(c, h):
        return b_PE + c * PE_PER + 3 + (h + 1)

    def pe_sq(c):
        return b_PE + c * PE_PER + 3 + NHEADS + 1

    def pe_rr(c):
        return b_PE + c * PE_PER + 3 + NHEADS + 2

    def se_sz(c):
        return b_SE + c * SE_PER + 1

    def se_ebc(c):
        return b_SE + c * SE_PER + 2

    def se_prec(c):
        return b_SE + c * SE_PER + 3

    def se_e0(c, h):
        return b_SE + c * SE_PER + 3 + (h + 1)

    def se_sq(c):
        return b_SE + c * SE_PER + 3 + NHEADS + 1

    def se_rstd(c):
        return b_SE + c * SE_PER + 3 + NHEADS + 3

    def ve_cam(c):
        return b_VE + c * VE_PER + 1

    def ve_cbt(c):
        return b_VE + c * VE_PER + 2

    def ve_mask(c):
        return b_VE + c * VE_PER + 3

    def ve_sv(c):
        return b_VE + c * VE_PER + 4

    def ve_stg(c, g):
        return b_VE + c * VE_PER + 4 + 9 * g + 1

    def ve_seg(c, h):
        g, j = divmod(h, 8)
        return b_VE + c * VE_PER + 4 + 9 * g + 2 + j

    def ve_yev(c, m):
        return b_VE + c * VE_PER + 76 + (m + 1)

    def ve_ygt(c):
        return b_VE + c * VE_PER + 109

    def ve_rr(c):
        return b_VE + c * VE_PER + 110

    def ve_rrs(c):
        return b_VE + c * VE_PER + 111

    def ve_end(c):
        return b_VE + (c + 1) * VE_PER

    def po_mask(c):
        return b_PO + c * PO_PER + 1

    with contextlib.ExitStack() as psx:
        bc = psx.enter_context(nc.sbuf_tensor("d2_bc", [128, NHEADS * CHUNK], F32))
        ebc = psx.enter_context(nc.sbuf_tensor("d2_ebc", [128, NHEADS * CHUNK], BF16))
        sva = psx.enter_context(nc.sbuf_tensor("d2_sva", [128, NHEADS * CHUNK], BF16))
        e0a = psx.enter_context(nc.sbuf_tensor("d2_e0a", [128, NHEADS * CHUNK], BF16))
        xtm = [psx.enter_context(nc.sbuf_tensor(f"d2_xtm{i}", [128, D_INNER], BF16)) for i in range(2)]
        xfm = [psx.enter_context(nc.sbuf_tensor(f"d2_xfm{i}", [128, NZ * CHUNK], BF16)) for i in range(2)]
        cfm = [psx.enter_context(nc.sbuf_tensor(f"d2_cfm{i}", [128, CHUNK], BF16)) for i in range(2)]
        bfm = [psx.enter_context(nc.sbuf_tensor(f"d2_bfm{i}", [128, CHUNK], BF16)) for i in range(2)]
        cbt = psx.enter_context(nc.sbuf_tensor("d2_cbt", [128, CHUNK], BF16))
        zt = psx.enter_context(nc.sbuf_tensor("d2_zt", [128, NZ * CHUNK], BF16))
        selb = [psx.enter_context(nc.sbuf_tensor(f"d2_selb{i}", [128, D_INNER], BF16)) for i in range(2)]
        seg = psx.enter_context(nc.sbuf_tensor("d2_seg", [128, D_INNER], BF16))
        prec = psx.enter_context(nc.sbuf_tensor("d2_prec", [128, NHEADS], F32))
        camt = psx.enter_context(nc.sbuf_tensor("d2_camt", [128, NHEADS], F32))
        yfm = psx.enter_context(nc.sbuf_tensor("d2_yfm", [128, NZ * CHUNK], BF16))
        ynb = psx.enter_context(nc.sbuf_tensor("d2_ynb", [128, NZ * CHUNK], BF16))
        rr = psx.enter_context(nc.sbuf_tensor("d2_rr", [1, CHUNK], F32))
        rrs = psx.enter_context(nc.sbuf_tensor("d2_rrs", [128, CHUNK], F32))
        rowp = psx.enter_context(nc.sbuf_tensor("d2_rowp", [1, NHEADS], F32))
        blk = psx.enter_context(nc.Block("d2"))

        def hb3(t2d, nh=NHEADS):
            return t2d[:].rearrange("p (h l) -> p h l", h=nh)

        @blk.sync
        def _(s):
            wb(s, ctx)
            for c in range(NC):
                if c >= 2:
                    s.wait_ge(PE.h, pe_rr(c - 2))
                    s.wait_ge(VE.h, ve_end(c - 2))
                sp_inc(s.dma_start_transpose(out=xtm[c % 2][:],
                                             in_=ctx.xbc_scr[0:D_INNER,
                                                             c * CHUNK:(c + 1) * CHUNK]),
                       P_LD, c)
                sp_inc(s.dma_start(out=cfm[c % 2][:],
                                   in_=ctx.xbc_scr[D_INNER + 128:D_INNER + 256,
                                                   c * CHUNK:(c + 1) * CHUNK]), P_LD, c)
                sp_inc(s.dma_start(out=bfm[c % 2][:],
                                   in_=ctx.xbc_scr[D_INNER:D_INNER + 128,
                                                   c * CHUNK:(c + 1) * CHUNK]), P_LD, c)
                sp_inc(s.dma_start(out=xfm[c % 2][:].rearrange("p (i t) -> p i t", i=NZ),
                                   in_=ctx.xbc_scr[0:D_INNER, c * CHUNK:(c + 1) * CHUNK]
                                   .rearrange("(i p) t -> p i t", p=128)), P_LD, c)
                cc_ld = max(c, 1)
                sp_inc(s.dma_start(out=selb[c % 2][:],
                                   in_=ctx.sel_scr[cc_ld * 128:(cc_ld + 1) * 128, :]),
                       P_LD, c)
                if c >= 1:
                    s.wait_ge(VE.h, ve_ygt(c - 1))
                sp_inc(s.dma_start(out=zt[:].rearrange("p (i t) -> p i t", i=NZ),
                                   in_=ctx.z_scr[:, c * CHUNK:(c + 1) * CHUNK]
                                   .rearrange("(i p) t -> p i t", p=128)), P_LD, c)

        @blk.gpsimd
        def _(g):
            wb(g, ctx)
            for c in range(NC):
                if c >= 1:
                    g.wait_ge(SE.h, se_e0(c - 1, NHEADS - 1))
                for q in range(4):
                    sp_inc(g.dma_start(out=bc[:, q * 16 * CHUNK:(q + 1) * 16 * CHUNK]
                                       .rearrange("p (h l) -> p h l", h=16),
                                       in_=ctx.cums_scr
                                       .rearrange("h (cc l) -> cc h l", l=CHUNK)
                                       [c:c + 1, q * 16:(q + 1) * 16, :]
                                       .broadcast_to([128, 16, CHUNK])), P_W, c)
                if c >= 1:
                    g.wait_ge(PE.h, pe_prec(c - 1))
                sp_inc(g.dma_start(out=rowp[:], in_=ctx.rows_scr[16 + c:17 + c, :]),
                       P_W, c)
                g.wait_ge(SE.h, se_ebc(c))
                inc(g.tensor_tensor(out=hb3(bc)[:, 32:64, :], in0=hb3(bc)[:, 32:64, :],
                                    in1=ctx.maskb[:].rearrange("s (u l) -> s u l", u=1)
                                    .broadcast_to([128, 32, CHUNK]), op=ALU.add), PO, 1)
                g.wait_ge(VE.h, ve_end(c))
                if c >= 1:
                    sp_wait(g, P_PS, b_PS, c - 1)
                sp_inc(g.dma_start(out=ctx.y_scr[:, c * CHUNK:(c + 1) * CHUNK]
                                   .rearrange("(i p) t -> p i t", p=128),
                                   in_=ynb[:].rearrange("p (i t) -> p i t", i=NZ)),
                       P_PS, c)

        @blk.scalar
        def _(sc):
            wb(sc, ctx)
            for c in range(NC):
                sp_wait(sc, P_LD, b_LD, c, 96)
                if c >= 1:
                    sc.wait_ge(VE.h, ve_ygt(c - 1))
                inc(sc.activation(out=zt[:], in_=zt[:], func=AF.Silu), SE, 1)
                sp_wait(sc, P_W, b_W, c, 80)
                if c >= 1:
                    sc.wait_ge(VE.h, ve_sv(c - 1))
                inc(sc.activation(out=ebc[:], in_=bc[:], func=AF.Exp), SE, 1)
                sc.wait_ge(PE.h, pe_prec(c))
                if c >= 1:
                    sc.wait_ge(VE.h, ve_seg(c - 1, NHEADS - 1))
                inc(sc.activation(out=prec[:], in_=ctx.ps[2][:, 0:NHEADS], func=AF.Exp),
                    SE, 1)
                sc.wait_ge(VE.h, ve_mask(c))
                sc.wait_ge(PO.h, po_mask(c))
                sc.wait_ge(VE.h, ve_cam(c))
                if c >= 1:
                    sc.wait_ge(PE.h, pe_ymm(c - 1, NHEADS - 1))
                for h in range(NHEADS):
                    inc(sc.activation(out=e0a[:, h * CHUNK:(h + 1) * CHUNK],
                                      in_=bc[:, h * CHUNK:(h + 1) * CHUNK],
                                      func=AF.Exp, bias=camt[:, h:h + 1]), SE, 1)
                sc.wait_ge(VE.h, ve_ygt(c))
                if c >= 1:
                    sp_wait(sc, P_PS, b_PS, c - 1)
                inc(sc.activation(out=ynb[:], in_=yfm[:], func=AF.Square), SE, 1)
                sc.wait_ge(VE.h, ve_rr(c))
                inc(sc.activation(out=rr[:], in_=rr[:], func=AF.Ln), SE, 1)
                sc.wait_ge(SE.h, b_SE + c * SE_PER + 3 + NHEADS + 2)
                inc(sc.activation(out=rr[:], in_=rr[:], func=AF.Exp, scale=-0.5), SE, 1)

        @blk.tensor
        def _(t):
            wb(t, ctx)
            for c in range(NC):
                sp_wait(t, P_LD, b_LD, c, 96)
                if c >= 1:
                    t.wait_ge(VE.h, ve_cbt(c - 1))
                inc(t.matmul(ctx.ps[0][:, 0:CHUNK], bfm[c % 2][:], cfm[c % 2][:],
                             start=True, stop=True), PE, 1)
                sp_wait(t, P_W, b_W, c, 80)
                if c >= 1:
                    t.wait_ge(SE.h, se_prec(c - 1))
                inc(t.matmul(ctx.ps[2][:, 0:NHEADS], ctx.onesrow[:], rowp[:],
                             start=True, stop=True), PE, 1)
                if c >= 1:
                    t.wait_ge(VE.h, ve_cam(c - 1))
                inc(t.transpose(ctx.ps[3][:, 0:NHEADS],
                                ctx.cumA[:, c * CHUNK:(c + 1) * CHUNK],
                                ctx.idf32[0:NHEADS, 0:NHEADS]), PE, 1)
                t.wait_ge(VE.h, ve_sv(c))
                for h in range(NHEADS):
                    m = h // 2
                    odd = h % 2
                    pb = ctx.ps[4 + m % 2]
                    po_r = pb[64:128, 0:CHUNK] if odd else pb[0:64, 0:CHUNK]
                    tp = (0, 64) if odd else None
                    t.wait_ge(VE.h, ve_stg(c, h // 8))
                    t.wait_ge(VE.h, ve_seg(c, h))
                    if not odd:
                        if m >= 2:
                            t.wait_ge(VE.h, ve_yev(c, m - 2))
                        elif c >= 1:
                            t.wait_ge(VE.h, ve_yev(c - 1, NPAIRH - 2 + m))
                    t.matmul(po_r, xtm[c % 2][:, h * 64:(h + 1) * 64],
                             e0a[:, h * CHUNK:(h + 1) * CHUNK],
                             start=True, stop=False, tile_position=tp)
                    inc(t.matmul(po_r, seg[:, h * 64:(h + 1) * 64],
                                 sva[:, h * CHUNK:(h + 1) * CHUNK],
                                 start=False, stop=True, tile_position=tp), PE, 1)
                t.wait_ge(SE.h, se_sq(c))
                if c >= 1:
                    t.wait_ge(VE.h, ve_rr(c - 1))
                mm = None
                for j in range(NZ):
                    mm = t.matmul(ctx.ps[6][0:1, 0:CHUNK], ctx.onescolbf[:],
                                  ynb[:, j * CHUNK:(j + 1) * CHUNK],
                                  start=(j == 0), stop=(j == NZ - 1))
                inc(mm, PE, 1)
                t.wait_ge(SE.h, se_rstd(c))
                if c >= 1:
                    t.wait_ge(VE.h, ve_rrs(c - 1))
                inc(t.matmul(ctx.ps[7][:, 0:CHUNK], ctx.onesrow[:], rr[:],
                             start=True, stop=True), PE, 1)

        @blk.vector
        def _(v):
            wb(v, ctx)
            for c in range(NC):
                v.wait_ge(PE.h, pe_cam(c))
                if c >= 1:
                    v.wait_ge(SE.h, se_e0(c - 1, NHEADS - 1))
                inc(v.tensor_scalar_mul(out=camt[:], in0=ctx.ps[3][:, 0:NHEADS],
                                        scalar1=-1.0), VE, 1)
                inc(v.tensor_copy(out=cbt[:], in_=ctx.ps[0][:, 0:CHUNK]), VE, 1)
                v.wait_ge(SE.h, se_ebc(c))
                inc(v.tensor_tensor(out=hb3(bc)[:, 0:32, :], in0=hb3(bc)[:, 0:32, :],
                                    in1=ctx.maskb[:].rearrange("s (u l) -> s u l", u=1)
                                    .broadcast_to([128, 32, CHUNK]), op=ALU.add), VE, 1)
                if c >= 1:
                    v.wait_ge(PE.h, pe_ymm(c - 1, NHEADS - 1))
                inc(v.tensor_mul(out=hb3(sva), in0=hb3(ebc),
                                 in1=cfm[c % 2][:].rearrange("p (u l) -> p u l", u=1)
                                 .broadcast_to([128, NHEADS, CHUNK])), VE, 1)
                v.wait_ge(SE.h, se_prec(c))
                sp_wait(v, P_LD, b_LD, c, 96)
                if c >= 1:
                    v.wait_ge(PO.h, b_PO + c * PO_PER)
                for g in range(8):
                    v.wait_ge(SE.h, se_e0(c, 8 * g + 7))
                    inc(v.tensor_mul(out=hb3(e0a)[:, 8 * g:8 * g + 8, :],
                                     in0=hb3(e0a)[:, 8 * g:8 * g + 8, :],
                                     in1=cbt[:].rearrange("p (u l) -> p u l", u=1)
                                     .broadcast_to([128, 8, CHUNK])), VE, 1)
                    for j in range(8):
                        h = 8 * g + j
                        if c == 0:
                            inc(v.tensor_scalar_mul(out=seg[:, h * 64:(h + 1) * 64],
                                                    in0=ctx.S0[:, h * 64:(h + 1) * 64],
                                                    scalar1=prec[:, h:h + 1]), VE, 1)
                        else:
                            inc(v.scalar_tensor_tensor(out=seg[:, h * 64:(h + 1) * 64],
                                                       in0=ctx.S0[:, h * 64:(h + 1) * 64],
                                                       scalar=prec[:, h:h + 1],
                                                       in1=selb[c % 2][:, h * 64:(h + 1) * 64],
                                                       op0=ALU.mult, op1=ALU.add), VE, 1)
                if c >= 1:
                    v.wait_ge(SE.h, se_sq(c - 1))
                for m in range(NPAIRH):
                    pb = ctx.ps[4 + m % 2]
                    v.wait_ge(PE.h, pe_ymm(c, 2 * m + 1))
                    inc(v.scalar_tensor_tensor(
                        out=yfm[:, m * CHUNK:(m + 1) * CHUNK],
                        in0=xfm[c % 2][:, m * CHUNK:(m + 1) * CHUNK],
                        scalar=ctx.dp_pair[:, m:m + 1], in1=pb[:, 0:CHUNK],
                        op0=ALU.mult, op1=ALU.add), VE, 1)
                v.wait_ge(SE.h, se_sz(c))
                v.wait_ge(VE.h, ve_yev(c, NPAIRH - 1))
                inc(v.tensor_mul(out=yfm[:], in0=yfm[:], in1=zt[:]), VE, 1)
                v.wait_ge(PE.h, pe_sq(c))
                inc(v.tensor_scalar(out=rr[:], in0=ctx.ps[6][0:1, 0:CHUNK],
                                    scalar1=1.0 / D_INNER, scalar2=1e-5,
                                    op0=ALU.mult, op1=ALU.add), VE, 1)
                v.wait_ge(PE.h, pe_rr(c))
                inc(v.tensor_copy(out=rrs[:], in_=ctx.ps[7][:, 0:CHUNK]), VE, 1)
                v.wait_ge(VE.h, ve_rrs(c))
                for j in range(NZ):
                    inc(v.scalar_tensor_tensor(out=ynb[:, j * CHUNK:(j + 1) * CHUNK],
                                               in0=yfm[:, j * CHUNK:(j + 1) * CHUNK],
                                               scalar=ctx.nw[:, j:j + 1], in1=rrs[:],
                                               op0=ALU.mult, op1=ALU.mult), VE, 1)

        endbar(ctx, blk)
    ctx.S0_scope.close()


def phase_outproj(ctx):
    nc = ctx.nc
    dH, P_LD, P_W, P_ST, PE, VE = ctx.dH, ctx.P_LD, ctx.P_W, ctx.P_ST, ctx.PE, ctx.VE
    b_dH, b_LD, b_W, b_ST, b_PE, b_VE = (dH.v, P_LD.base(), P_W.base(), P_ST.base(),
                                         PE.v, VE.v)
    NSEG = 2
    SW = 1024

    with contextlib.ExitStack() as psx:
        yseg = psx.enter_context(nc.sbuf_tensor("op_yseg", [128, NZ * SW], BF16))
        wo = [psx.enter_context(nc.sbuf_tensor(f"op_wo{i}", [128, NZ * 128], BF16)) for i in range(2)]
        x1b = [psx.enter_context(nc.sbuf_tensor(f"op_x1{i}", [128, 512], F32)) for i in range(2)]
        xob = [psx.enter_context(nc.sbuf_tensor(f"op_xo{i}", [128, 512], F32)) for i in range(2)]
        blk = psx.enter_context(nc.Block("outproj"))

        @blk.sync
        def _(s):
            wb(s, ctx)
            for sg in range(NSEG):
                if sg >= 1:
                    s.wait_ge(PE.h, b_PE + sg * 32)
                inc(s.dma_start(out=yseg[:].rearrange("p (i t) -> p i t", i=NZ),
                                in_=ctx.y_scr[:, sg * SW:(sg + 1) * SW]
                                .rearrange("(i p) t -> p i t", p=128)), dH, 16)
                for d in range(NJ):
                    for ts2 in range(2):
                        it = sg * 32 + d * 2 + ts2
                        t0 = sg * SW + ts2 * 512
                        if it >= 2:
                            s.wait_ge(VE.h, b_VE + it - 1)
                        sp_inc(s.dma_start(out=x1b[it % 2][:],
                                           in_=ctx.x1_scr[d * 128:(d + 1) * 128,
                                                          HALO + t0:HALO + t0 + 512]),
                               P_LD, it)

        @blk.gpsimd
        def _(g):
            wb(g, ctx)
            for sg in range(NSEG):
                for d in range(NJ):
                    kd = sg * NJ + d
                    if kd >= 2:
                        g.wait_ge(PE.h, b_PE + (kd - 1) * 2)
                    sp_inc(g.dma_start(out=wo[d % 2][:].rearrange("p (i d) -> p i d", i=NZ),
                                       in_=ctx.wout[:, d * 128:(d + 1) * 128]
                                       .rearrange("(i p) d -> p i d", p=128)), P_W, kd)

        @blk.tensor
        def _(t):
            wb(t, ctx)
            for sg in range(NSEG):
                t.wait_ge(dH.h, b_dH + 16 * (sg + 1))
                for d in range(NJ):
                    kd = sg * NJ + d
                    sp_wait(t, P_W, b_W, kd)
                    for ts2 in range(2):
                        it = sg * 32 + d * 2 + ts2
                        pp = ctx.ps[4 + it % 2]
                        if it >= 2:
                            t.wait_ge(VE.h, b_VE + it - 1)
                        mm = None
                        for i in range(NZ):
                            mm = t.matmul(pp[:], wo[d % 2][:, i * 128:(i + 1) * 128],
                                          yseg[:, i * SW + ts2 * 512:i * SW + ts2 * 512 + 512],
                                          start=(i == 0), stop=(i == NZ - 1))
                        inc(mm, PE, 1)

        @blk.vector
        def _(v):
            wb(v, ctx)
            for sg in range(NSEG):
                for d in range(NJ):
                    for ts2 in range(2):
                        it = sg * 32 + d * 2 + ts2
                        pp = ctx.ps[4 + it % 2]
                        v.wait_ge(PE.h, b_PE + it + 1)
                        sp_wait(v, P_LD, b_LD, it)
                        if it >= 2:
                            sp_wait(v, P_ST, b_ST, it - 2)
                        inc(v.tensor_add(out=xob[it % 2][:], in0=pp[:],
                                         in1=x1b[it % 2][:]), VE, 1)

        @blk.scalar
        def _(sc):
            wb(sc, ctx)
            for sg in range(NSEG):
                for d in range(NJ):
                    for ts2 in range(2):
                        it = sg * 32 + d * 2 + ts2
                        t0 = sg * SW + ts2 * 512
                        sc.wait_ge(VE.h, b_VE + it + 1)
                        sp_inc(sc.dma_start(out=ctx.x2_scr[d * 128:(d + 1) * 128,
                                                           t0:t0 + 512],
                                            in_=xob[it % 2][:]), P_ST, it)

        endbar(ctx, blk)


def phase_dbg(ctx):
    nc = ctx.nc
    dL = ctx.dC
    b_dL = dL.v
    outs = []
    for nm in [d for d in DBG if d != "xln"]:
        src = getattr(ctx, nm)
        outs.append((nc.dram_tensor("dbg_" + nm, list(src.shape), src.dtype,
                                    kind="ExternalOutput").ap(), src))
    with nc.Block("dbg") as blk:
        @blk.sync
        def _(s):
            wb(s, ctx)
            n = 0
            for dst, src in outs:
                inc(s.dma_start(out=dst, in_=src), dL, 16)
                n += 1
            s.wait_ge(dL.h, b_dL + 16 * n)


_NC_CACHE = {}


def _get_nc():
    key = (tuple(DBG), STOP_AFTER)
    if key not in _NC_CACHE:
        _NC_CACHE[key] = build_program()
    return _NC_CACHE[key]


def kernel(L0_attn_u, L0_attn_W_in, L0_attn_W_conv, L0_attn_b_conv,
           L0_attn_dt_bias, L0_attn_A_log, L0_attn_Dp, L0_attn_norm_w,
           L0_attn_W_out, L0_ffn_w_gate_up, L0_ffn_w_down):
    f32 = np.float32
    u = np.asarray(L0_attn_u, f32)
    ut = np.ascontiguousarray(u.transpose(0, 2, 1))
    wgu = np.ascontiguousarray(np.asarray(L0_ffn_w_gate_up, f32).T)
    wdn = np.ascontiguousarray(np.asarray(L0_ffn_w_down, f32).T)
    win = np.ascontiguousarray(np.asarray(L0_attn_W_in, f32).T)
    wout = np.ascontiguousarray(np.asarray(L0_attn_W_out, f32).T)
    wconv = np.ascontiguousarray(
        np.asarray(L0_attn_W_conv, f32).reshape(NDC, 128, 4).transpose(1, 0, 2)
        .reshape(128, NDC * 4))
    bconv = np.ascontiguousarray(np.asarray(L0_attn_b_conv, f32).reshape(NDC, 128).T)
    dtb = np.asarray(L0_attn_dt_bias, f32).reshape(NHEADS, 1)
    alog = np.asarray(L0_attn_A_log, f32).reshape(NHEADS, 1)
    dprow = np.asarray(L0_attn_Dp, f32).reshape(1, NHEADS)
    nwin = np.ascontiguousarray(np.asarray(L0_attn_norm_w, f32).reshape(NZ, 128).T)

    in_maps = []
    for core in range(8):
        b, q = divmod(core, 4)
        u_pad = np.zeros((D_MODEL, W_A), f32)
        t0 = q * TOK
        lo = max(0, t0 - HALO)
        u_pad[:, HALO - (t0 - lo):] = ut[b][:, lo:t0 + TOK]
        btwT = np.zeros((8, 8), f32)
        predc = np.zeros((8, 1), f32)
        for j in range(8):
            if j // 4 == b and j % 4 < q:
                predc[j, 0] = 1.0
                for i in range(8):
                    if i // 4 == b and (j % 4) < (i % 4) < q:
                        btwT[i, j] = 1.0
        in_maps.append({
            "u": np.ascontiguousarray(u_pad), "wgu": wgu, "wdn": wdn, "win": win,
            "wout": wout, "wconv": wconv, "bconv": bconv, "dtb": dtb, "alog": alog,
            "dprow": dprow, "nwin": nwin, "btwT": btwT, "predc": predc,
        })

    nc = _get_nc()
    res = run_bass_kernel_spmd(nc, in_maps, core_ids=list(range(8)))
    out = np.empty((2, 8192, 2048), f32)
    for core in range(8):
        b, q = divmod(core, 4)
        out[b, q * TOK:(q + 1) * TOK, :] = np.asarray(res.results[core]["o"]).T
    if DBG:
        kernel._dbg = [{("dbg_" + nm): np.asarray(res.results[c]["dbg_" + nm])
                        for nm in DBG} for c in range(8)]
    return out


# version 05 (final)
# speedup vs baseline: 1.4107x; 1.2135x over previous
import contextlib
import numpy as np
import ml_dtypes
import concourse.bass as bass
import concourse.mybir as mybir
from concourse.bass_utils import run_bass_kernel_spmd

F32 = mybir.dt.float32
BF16 = mybir.dt.bfloat16
AF = mybir.ActivationFunctionType
ALU = mybir.AluOpType
bf16 = ml_dtypes.bfloat16

D_MODEL = 2048
D_INNER = 4096
D_STATE = 128
NHEADS = 64
HEADDIM = 64
CONV_DIM = D_INNER + 2 * D_STATE
D_IN_PROJ = 2 * D_INNER + 2 * D_STATE + NHEADS
INTER = 8192
TOK = 2048
HALO = 3
W_A = TOK + HALO
CHUNK = 128
NC = TOK // CHUNK
NJ = D_MODEL // 128
NI = INTER // 128
NZ = D_INNER // 128
NDC = CONV_DIM // 128
NO_IN = 67
MASK_NEG = -30000.0
PUBN = 128 * D_INNER + NHEADS

TSUBS_A = [(0, 512), (512, 512), (1024, 512), (1536, 256), (1792, 259)]
TSUBS_F = [(0, 512), (512, 512), (1024, 512), (1536, 512)]

DBG = []
STOP_AFTER = None


class Sem:
    def __init__(self, nc, es, name):
        self.h = es.enter_context(nc.semaphore(name))
        self.v = 0


def inc(instr, sem, n=1):
    instr.then_inc(sem.h, n)
    sem.v += n
    return instr


class SPair:

    def __init__(self, nc, es, name):
        self.s = [Sem(nc, es, name + "a"), Sem(nc, es, name + "b")]

    def base(self):
        return (self.s[0].v, self.s[1].v)


def sp_inc(instr, sp, g, n=16):
    inc(instr, sp.s[g % 2], n)


def sp_wait(eng, sp, b, g, n=16):
    eng.wait_ge(sp.s[g % 2].h, b[g % 2] + n * (g // 2 + 1))


class Ctx:
    pass


def build_program():
    nc = bass.Bass("TRN2", target_bir_lowering=False, debug=False)
    ctx = Ctx()
    ctx.nc = nc
    es = contextlib.ExitStack()
    ctx.es = es
    ap = lambda *a, **k: nc.dram_tensor(*a, **k).ap()

    ctx.u = ap("u", [D_MODEL, W_A], F32, kind="ExternalInput")
    ctx.wgu = ap("wgu", [2 * INTER, D_MODEL], F32, kind="ExternalInput")
    ctx.wdn = ap("wdn", [D_MODEL, INTER], F32, kind="ExternalInput")
    ctx.win = ap("win", [NO_IN * 128, D_MODEL], F32, kind="ExternalInput")
    ctx.wout = ap("wout", [D_MODEL, D_INNER], F32, kind="ExternalInput")
    ctx.wconv = ap("wconv", [128, NDC * 4], F32, kind="ExternalInput")
    ctx.bconv = ap("bconv", [128, NDC], F32, kind="ExternalInput")
    ctx.dtb = ap("dtb", [NHEADS, 1], F32, kind="ExternalInput")
    ctx.alog = ap("alog", [NHEADS, 1], F32, kind="ExternalInput")
    ctx.dprow = ap("dprow", [1, NHEADS], F32, kind="ExternalInput")
    ctx.nwin = ap("nwin", [128, NZ], F32, kind="ExternalInput")
    ctx.btwT = ap("btwT", [8, 8], F32, kind="ExternalInput")
    ctx.predc = ap("predc", [8, 1], F32, kind="ExternalInput")
    ctx.o = ap("o", [D_MODEL, TOK], F32, kind="ExternalOutput")
    ctx.h_scr = ap("h_scr", [INTER, W_A], BF16)
    ctx.x1_scr = ap("x1_scr", [D_MODEL, W_A], F32)
    ctx.z_scr = ap("z_scr", [D_INNER, TOK], BF16)
    ctx.xbcp_scr = ap("xbcp_scr", [CONV_DIM, W_A], BF16)
    ctx.xbc_scr = ap("xbc_scr", [CONV_DIM, TOK], BF16)
    ctx.sel_scr = ap("sel_scr", [NC * 128, D_INNER], BF16)
    ctx.y_scr = ap("y_scr", [D_INNER, TOK], BF16)
    ctx.x2_scr = ap("x2_scr", [D_MODEL, TOK], F32)
    ctx.cums_scr = ap("cums_scr", [NHEADS, TOK], F32)
    ctx.cums2_scr = ap("cums2_scr", [NC, NHEADS * CHUNK], F32)
    ctx.rows_scr = ap("rows_scr", [64, NHEADS], F32)
    ctx.pub_scr = ap("pub_scr", [1, PUBN], F32)
    ctx.gat_scr = ap("gat_scr", [8, PUBN], F32)

    maskb_np = np.where(np.arange(CHUNK)[None, :] >= np.arange(CHUNK)[:, None],
                        0.0, MASK_NEG).astype(np.float32)
    ctx.maskb_c = nc.inline_tensor(maskb_np, name="c_maskb").ap()
    ctx.onesrow_c = nc.inline_tensor(np.ones((1, 128), np.float32), name="c_onesrow").ap()
    ctx.onescolbf_c = nc.inline_tensor(np.ones((128, 1), bf16), name="c_onescolbf").ap()
    ctx.idf32_c = nc.inline_tensor(np.eye(128, dtype=np.float32), name="c_idf32").ap()
    segm = np.ones((NHEADS, TOK), np.float32)
    segm[:, ::CHUNK] = 0.0
    ctx.segmask_c = nc.inline_tensor(segm, name="c_segmask").ap()
    ctx.zcol_c = nc.inline_tensor(np.zeros((NHEADS, 1), np.float32), name="c_zcol").ap()
    halves = np.zeros((2, 128), np.float32)
    halves[0, 0:64] = 1.0
    halves[1, 64:128] = 1.0
    ctx.halves_c = nc.inline_tensor(halves, name="c_halves").ap()

    sb = lambda name, shape, dt: es.enter_context(nc.sbuf_tensor(name, shape, dt))
    ctx.dtraw = sb("g_dtraw", [NHEADS, W_A], F32)
    ctx.Ab = sb("g_Ab", [NHEADS, TOK], F32)
    ctx.cumA = sb("g_cumA", [NHEADS, TOK], F32)
    ctx.tots = sb("g_tots", [NHEADS, NC], F32)
    ctx.ptot = sb("g_ptot", [NHEADS, NC + 1], F32)
    ctx.dp_pair = sb("g_dp_pair", [128, NHEADS // 2], F32)
    ctx.maskb = sb("g_maskb", [CHUNK, CHUNK], F32)
    ctx.onesrow = sb("g_onesrow", [1, 128], F32)
    ctx.onescolbf = sb("g_onescolbf", [128, 1], BF16)
    ctx.idf32 = sb("g_idf32", [128, 128], F32)
    ctx.cols = sb("g_cols", [NHEADS, 4], F32)
    ctx.nw = sb("g_nw", [128, NZ], F32)
    ctx.dprow_sb = sb("g_dprow", [1, NHEADS], F32)
    ctx.dppk_sb = sb("g_dppk", [2, NHEADS // 2], F32)
    ctx.halves_sb = sb("g_halves", [2, 128], F32)

    ctx.ps = [es.enter_context(nc.psum_tensor(f"ps{i}", [128, 512], F32)) for i in range(8)]

    for s in ("dC", "dH", "PE", "VE", "SE", "PO", "CC", "BAR"):
        setattr(ctx, s, Sem(nc, es, "s_" + s))
    ctx.P_LD = SPair(nc, es, "p_ld")
    ctx.P_W = SPair(nc, es, "p_w")
    ctx.P_ST = SPair(nc, es, "p_st")
    ctx.P_X = SPair(nc, es, "p_x")
    ctx.P_PS = SPair(nc, es, "p_ps")
    ctx.P_H = SPair(nc, es, "p_h")
    ctx.bar_n = 0

    phases = [
        ("init", lambda: phase_init(ctx)),
        ("ln1", lambda: phase_ln(ctx, src=ctx.u, width=W_A, name="ln1")),
        ("gu1", lambda: phase_gateup(ctx, width=W_A, name="gu1")),
        ("dn1", lambda: phase_down(ctx, res_src=ctx.u, dst=ctx.x1_scr, width=W_A,
                                   name="dn1")),
        ("ln2", lambda: phase_ln(ctx, src=ctx.x1_scr, width=W_A, name="ln2")),
        ("inproj", lambda: phase_inproj(ctx)),
        ("dt", lambda: phase_dt(ctx)),
        ("conv", lambda: phase_conv(ctx)),
        ("d1", lambda: phase_d1(ctx)),
        ("exchange", lambda: phase_exchange(ctx)),
        ("d2", lambda: phase_d2(ctx)),
        ("outproj", lambda: phase_outproj(ctx)),
        ("ln3", lambda: phase_ln(ctx, src=ctx.x2_scr, width=TOK, name="ln3")),
        ("gu2", lambda: phase_gateup(ctx, width=TOK, name="gu2")),
        ("dn2", lambda: phase_down(ctx, res_src=ctx.x2_scr, dst=ctx.o, width=TOK,
                                   name="dn2")),
    ]
    for pname, fn in phases:
        fn()
        if STOP_AFTER == pname:
            break
    if "xln" in DBG and ctx.xln is not None:
        w = ctx.xln_w
        xs = nc.dram_tensor("dbg_xln", [128, NJ * w], BF16, kind="ExternalOutput").ap()
        b_dL = ctx.dC.v
        with nc.Block("dumpx") as blkx:
            @blkx.sync
            def _(s):
                wb(s, ctx)
                inc(s.dma_start(out=xs, in_=ctx.xln[:]), ctx.dC, 16)
                s.wait_ge(ctx.dC.h, b_dL + 16)
    if [d for d in DBG if d != "xln"]:
        phase_dbg(ctx)
    if getattr(ctx, "xln", None) is not None:
        ctx.xln_scope.close()
    if getattr(ctx, "S0", None) is not None and not getattr(ctx, "_s0_closed", False):
        try:
            ctx.S0_scope.close()
        except Exception:
            pass
    es.close()
    return nc


def endbar(ctx, blk, waits=None):
    sems = [ctx.dC, ctx.dH, ctx.PE, ctx.VE, ctx.SE, ctx.PO, ctx.CC]
    for p in (ctx.P_LD, ctx.P_W, ctx.P_ST, ctx.P_X, ctx.P_PS, ctx.P_H):
        sems += p.s
    targets = [(s.h, s.v) for s in sems]
    ctx.bar_n += 5
    for name in ("sync", "gpsimd", "tensor", "scalar", "vector"):
        def body(eng):
            for h, v in targets:
                if v:
                    eng.wait_ge(h, v)
            eng.sem_inc(ctx.BAR.h, 1)
        getattr(blk, name)(body)
    ctx.BAR.v += 5


def wb(eng, ctx):
    if ctx.bar_n:
        eng.wait_ge(ctx.BAR.h, ctx.bar_n)


def phase_init(ctx):
    nc = ctx.nc
    dC, PE, VE, SE = ctx.dC, ctx.PE, ctx.VE, ctx.SE
    b_dC, b_PE, b_VE, b_SE = dC.v, PE.v, VE.v, SE.v
    with nc.Block("init") as blk:
        @blk.sync
        def _(s):
            inc(s.dma_start(out=ctx.maskb[:], in_=ctx.maskb_c), dC, 16)
            inc(s.dma_start(out=ctx.onesrow[:], in_=ctx.onesrow_c), dC, 16)
            inc(s.dma_start(out=ctx.onescolbf[:], in_=ctx.onescolbf_c), dC, 16)
            inc(s.dma_start(out=ctx.idf32[:], in_=ctx.idf32_c), dC, 16)
            inc(s.dma_start(out=ctx.cols[:, 0:1], in_=ctx.dtb), dC, 16)
            inc(s.dma_start(out=ctx.cols[:, 2:3], in_=ctx.alog), dC, 16)
            inc(s.dma_start(out=ctx.cols[:, 3:4], in_=ctx.zcol_c), dC, 16)
            inc(s.dma_start(out=ctx.nw[:], in_=ctx.nwin), dC, 16)
            inc(s.dma_start(out=ctx.dprow_sb[:], in_=ctx.dprow), dC, 16)
            inc(s.dma_start(out=ctx.halves_sb[:], in_=ctx.halves_c), dC, 16)
            inc(s.dma_start(out=ctx.dppk_sb[:],
                            in_=ctx.dprow.rearrange("u (m par) -> (u par) m", par=2),
                            allow_slow_non_contiguous=True), dC, 16)
        NL = 11

        @blk.scalar
        def _(sc):
            sc.wait_ge(dC.h, b_dC + 16 * NL)
            inc(sc.activation(out=ctx.cols[:, 1:2], in_=ctx.cols[:, 2:3], func=AF.Exp),
                SE, 1)

        @blk.tensor
        def _(t):
            t.wait_ge(dC.h, b_dC + 16 * NL)
            inc(t.matmul(ctx.ps[0][:, 0:NHEADS // 2], ctx.halves_sb[:], ctx.dppk_sb[:],
                         start=True, stop=True), PE, 1)

        @blk.vector
        def _(v):
            v.wait_ge(SE.h, b_SE + 1)
            v.tensor_scalar_mul(out=ctx.cols[:, 1:2], in0=ctx.cols[:, 1:2], scalar1=-1.0)
            v.wait_ge(PE.h, b_PE + 1)
            inc(v.tensor_copy(out=ctx.dp_pair[:], in_=ctx.ps[0][:, 0:NHEADS // 2]),
                VE, 1)

        endbar(ctx, blk)


def phase_ln(ctx, src, width, name):
    nc = ctx.nc
    tsubs = TSUBS_A if width == W_A else TSUBS_F
    NT = len(tsubs)
    P_LD, PE, VE, SE = ctx.P_LD, ctx.PE, ctx.VE, ctx.SE
    b_LD, b_PE, b_VE, b_SE = P_LD.base(), PE.v, VE.v, SE.v

    ctx.xln_scope = contextlib.ExitStack()
    ctx.xln = ctx.xln_scope.enter_context(
        nc.sbuf_tensor(f"{name}_xln", [128, NJ * width], BF16))
    ctx.xln_w = width

    with contextlib.ExitStack() as psx:
        ubuf = psx.enter_context(nc.sbuf_tensor(f"{name}_ubuf", [128, NJ * 512], F32))
        xb = [psx.enter_context(nc.sbuf_tensor(f"{name}_xb{i}", [128, 512], BF16)) for i in range(2)]
        sq = [psx.enter_context(nc.sbuf_tensor(f"{name}_sq{i}", [128, 512], BF16)) for i in range(2)]
        rows = psx.enter_context(nc.sbuf_tensor(f"{name}_rows", [1, 1536], F32))
        blk = psx.enter_context(nc.Block(name))

        @blk.sync
        def _(s):
            wb(s, ctx)
            for ti, (t0, tw) in enumerate(tsubs):
                if ti >= 1:
                    s.wait_ge(VE.h, b_VE + ti * 5)
                sp_inc(s.dma_start(
                    out=ubuf[:].rearrange("p (j t) -> p j t", j=NJ)[:, :, 0:tw],
                    in_=src[:, t0:t0 + tw].rearrange("(j p) t -> p j t", p=128)),
                    P_LD, ti)

        @blk.scalar
        def _(sc):
            wb(sc, ctx)
            for ti, (t0, tw) in enumerate(tsubs):
                sp_wait(sc, P_LD, b_LD, ti)
                for j in range(NJ):
                    bb = j % 2
                    gi = ti * NJ + j
                    if gi >= 2:
                        sc.wait_ge(PE.h, b_PE + (gi - 2) // NJ * 17 + (gi - 2) % NJ + 1)
                    uj = ubuf[:, j * 512:j * 512 + tw]
                    inc(sc.activation(out=xb[bb][:, 0:tw], in_=uj, func=AF.Copy), SE, 1)
                    inc(sc.activation(out=sq[bb][:, 0:tw], in_=uj, func=AF.Square), SE, 1)
                sc.wait_ge(VE.h, b_VE + ti * 5 + 3)
                inc(sc.activation(out=rows[0:1, 1024:1024 + tw],
                                  in_=rows[0:1, 1024:1024 + tw], func=AF.Ln), SE, 1)
                sc.wait_ge(SE.h, b_SE + ti * 35 + 33)
                inc(sc.activation(out=rows[0:1, 1024:1024 + tw],
                                  in_=rows[0:1, 1024:1024 + tw], func=AF.Exp,
                                  scale=-0.5), SE, 2)

        @blk.tensor
        def _(t):
            wb(t, ctx)
            for ti, (t0, tw) in enumerate(tsubs):
                if ti >= 1:
                    t.wait_ge(VE.h, b_VE + (ti - 1) * 5 + 2)
                for j in range(NJ):
                    bb = j % 2
                    t.wait_ge(SE.h, b_SE + ti * 35 + 2 * (j + 1))
                    t.matmul(ctx.ps[0][0:1, 0:tw], ctx.onescolbf[:], xb[bb][:, 0:tw],
                             start=(j == 0), stop=(j == NJ - 1))
                    inc(t.matmul(ctx.ps[1][0:1, 0:tw], ctx.onescolbf[:], sq[bb][:, 0:tw],
                                 start=(j == 0), stop=(j == NJ - 1)), PE, 1)
                t.wait_ge(SE.h, b_SE + (ti + 1) * 35)
                if ti >= 1:
                    t.wait_ge(VE.h, b_VE + ti * 5)
                t.matmul(ctx.ps[2][:, 0:tw], ctx.onesrow[:], rows[0:1, 0:tw],
                         start=True, stop=True)
                inc(t.matmul(ctx.ps[3][:, 0:tw], ctx.onesrow[:], rows[0:1, 1024:1024 + tw],
                             start=True, stop=True), PE, 1)

        @blk.vector
        def _(v):
            wb(v, ctx)
            for ti, (t0, tw) in enumerate(tsubs):
                v.wait_ge(PE.h, b_PE + ti * 17 + 16)
                v.tensor_scalar_mul(out=rows[0:1, 0:tw], in0=ctx.ps[0][0:1, 0:tw],
                                    scalar1=1.0 / D_MODEL)
                inc(v.tensor_scalar_mul(out=rows[0:1, 512:512 + tw],
                                        in0=ctx.ps[1][0:1, 0:tw],
                                        scalar1=1.0 / D_MODEL), VE, 1)
                v.wait_ge(VE.h, b_VE + ti * 5 + 1)
                inc(v.tensor_mul(out=rows[0:1, 1024:1024 + tw], in0=rows[0:1, 0:tw],
                                 in1=rows[0:1, 0:tw]), VE, 1)
                v.wait_ge(VE.h, b_VE + ti * 5 + 2)
                inc(v.scalar_tensor_tensor(out=rows[0:1, 1024:1024 + tw],
                                           in0=rows[0:1, 512:512 + tw], scalar=1e-5,
                                           in1=rows[0:1, 1024:1024 + tw],
                                           op0=ALU.add, op1=ALU.subtract), VE, 1)
                v.wait_ge(PE.h, b_PE + (ti + 1) * 17)
                last = None
                for j in range(NJ):
                    uj = ubuf[:, j * 512:j * 512 + tw]
                    last = v.tensor_sub(out=uj, in0=uj, in1=ctx.ps[2][:, 0:tw])
                inc(last, VE, 1)
                v.wait_ge(VE.h, b_VE + ti * 5 + 4)
                last = None
                for j in range(NJ):
                    uj = ubuf[:, j * 512:j * 512 + tw]
                    xo = ctx.xln[:, j * width + t0:j * width + t0 + tw]
                    last = v.tensor_mul(out=xo, in0=uj, in1=ctx.ps[3][:, 0:tw])
                inc(last, VE, 1)

        endbar(ctx, blk)


def phase_gateup(ctx, width, name):
    nc = ctx.nc
    tsubs = TSUBS_A if width == W_A else TSUBS_F
    NT = len(tsubs)
    NP = NI
    P_W, P_ST, PE, VE, SE = ctx.P_W, ctx.P_ST, ctx.PE, ctx.VE, ctx.SE
    b_W, b_ST, b_PE, b_VE, b_SE = P_W.base(), P_ST.base(), PE.v, VE.v, SE.v

    with contextlib.ExitStack() as psx:
        wg = [psx.enter_context(nc.sbuf_tensor(f"{name}_wg{i}", [128, NJ * 128], BF16)) for i in range(2)]
        wu = [psx.enter_context(nc.sbuf_tensor(f"{name}_wu{i}", [128, NJ * 128], BF16)) for i in range(2)]
        sg = [psx.enter_context(nc.sbuf_tensor(f"{name}_sg{i}", [128, 512], BF16)) for i in range(2)]
        hb = [psx.enter_context(nc.sbuf_tensor(f"{name}_hb{i}", [128, width], BF16)) for i in range(2)]
        blk = psx.enter_context(nc.Block(name))

        @blk.gpsimd
        def _(g):
            wb(g, ctx)
            for k in range(NP):
                bb = k % 2
                if k >= 2:
                    g.wait_ge(PE.h, b_PE + (k - 1) * NT)
                sp_inc(g.dma_start(out=wg[bb][:],
                                   in_=ctx.wgu[k * 128:(k + 1) * 128, :]), P_W, k)
                sp_inc(g.dma_start(out=wu[bb][:],
                                   in_=ctx.wgu[(NI + k) * 128:(NI + k + 1) * 128, :]),
                       P_W, k)

        @blk.tensor
        def _(t):
            wb(t, ctx)
            for k in range(NP):
                bb = k % 2
                sp_wait(t, P_W, b_W, k, 32)
                for ti, (t0, tw) in enumerate(tsubs):
                    it = k * NT + ti
                    pg = ctx.ps[4 + 2 * (it % 2)]
                    pu = ctx.ps[5 + 2 * (it % 2)]
                    if it >= 2:
                        t.wait_ge(VE.h, b_VE + it - 1)
                    mm = None
                    for j in range(NJ):
                        t.matmul(pg[:, 0:tw], wg[bb][:, j * 128:(j + 1) * 128],
                                 ctx.xln[:, j * width + t0:j * width + t0 + tw],
                                 start=(j == 0), stop=(j == NJ - 1))
                        mm = t.matmul(pu[:, 0:tw], wu[bb][:, j * 128:(j + 1) * 128],
                                      ctx.xln[:, j * width + t0:j * width + t0 + tw],
                                      start=(j == 0), stop=(j == NJ - 1))
                    inc(mm, PE, 1)

        @blk.scalar
        def _(sc):
            wb(sc, ctx)
            for k in range(NP):
                for ti, (t0, tw) in enumerate(tsubs):
                    it = k * NT + ti
                    pg = ctx.ps[4 + 2 * (it % 2)]
                    sc.wait_ge(PE.h, b_PE + it + 1)
                    if it >= 2:
                        sc.wait_ge(VE.h, b_VE + it - 1)
                    inc(sc.activation(out=sg[it % 2][:, 0:tw], in_=pg[:, 0:tw],
                                      func=AF.Silu), SE, 1)

        @blk.vector
        def _(v):
            wb(v, ctx)
            for k in range(NP):
                hbuf = hb[k % 2]
                if k >= 2:
                    sp_wait(v, P_ST, b_ST, k - 2)
                for ti, (t0, tw) in enumerate(tsubs):
                    it = k * NT + ti
                    pu = ctx.ps[5 + 2 * (it % 2)]
                    v.wait_ge(SE.h, b_SE + it + 1)
                    inc(v.tensor_mul(out=hbuf[:, t0:t0 + tw], in0=sg[it % 2][:, 0:tw],
                                     in1=pu[:, 0:tw]), VE, 1)

        @blk.sync
        def _(s):
            wb(s, ctx)
            for k in range(NP):
                s.wait_ge(VE.h, b_VE + (k + 1) * NT)
                sp_inc(s.dma_start(out=ctx.h_scr[k * 128:(k + 1) * 128, 0:width],
                                   in_=hb[k % 2][:]), P_ST, k)

        endbar(ctx, blk)
    ctx.xln_scope.close()
    ctx.xln = None


def phase_down(ctx, res_src, dst, width, name):
    nc = ctx.nc
    tsubs = TSUBS_A if width == W_A else TSUBS_F
    NT = len(tsubs)
    P_H, P_LD, P_W, P_ST, PE, VE = ctx.P_H, ctx.P_LD, ctx.P_W, ctx.P_ST, ctx.PE, ctx.VE
    b_H, b_LD, b_W, b_ST, b_PE, b_VE = P_H.base(), P_LD.base(), P_W.base(), P_ST.base(), PE.v, VE.v

    with contextlib.ExitStack() as psx:
        hseg = [psx.enter_context(nc.sbuf_tensor(f"{name}_hseg{i}", [128, NI * 512], BF16))
                for i in range(2)]
        wd = [psx.enter_context(nc.sbuf_tensor(f"{name}_wd{i}", [128, NI * 128], BF16)) for i in range(2)]
        ub = [psx.enter_context(nc.sbuf_tensor(f"{name}_ub{i}", [128, 512], F32)) for i in range(2)]
        xo = [psx.enter_context(nc.sbuf_tensor(f"{name}_xo{i}", [128, 512], F32)) for i in range(2)]
        blk = psx.enter_context(nc.Block(name))

        @blk.sync
        def _(s):
            wb(s, ctx)
            for ti, (t0, tw) in enumerate(tsubs):
                if ti >= 2:
                    s.wait_ge(PE.h, b_PE + (ti - 1) * NJ)
                sp_inc(s.dma_start(
                    out=hseg[ti % 2][:].rearrange("p (i t) -> p i t", i=NI)[:, :, 0:tw],
                    in_=ctx.h_scr[:, t0:t0 + tw].rearrange("(i p) t -> p i t", p=128)),
                    P_H, ti)
                for d in range(NJ):
                    it = ti * NJ + d
                    if it >= 2:
                        s.wait_ge(VE.h, b_VE + it - 1)
                    sp_inc(s.dma_start(out=ub[d % 2][:, 0:tw],
                                       in_=res_src[d * 128:(d + 1) * 128, t0:t0 + tw]),
                           P_LD, it)

        @blk.gpsimd
        def _(g):
            wb(g, ctx)
            for ti in range(NT):
                for d in range(NJ):
                    it = ti * NJ + d
                    if it >= 2:
                        g.wait_ge(PE.h, b_PE + it - 1)
                    sp_inc(g.dma_start(out=wd[d % 2][:],
                                       in_=ctx.wdn[d * 128:(d + 1) * 128, :]), P_W, it)

        @blk.tensor
        def _(t):
            wb(t, ctx)
            for ti, (t0, tw) in enumerate(tsubs):
                sp_wait(t, P_H, b_H, ti)
                for d in range(NJ):
                    it = ti * NJ + d
                    pd = ctx.ps[4 + it % 2]
                    sp_wait(t, P_W, b_W, it)
                    if it >= 2:
                        t.wait_ge(VE.h, b_VE + it - 1)
                    mm = None
                    for i in range(NI):
                        mm = t.matmul(pd[:, 0:tw], wd[d % 2][:, i * 128:(i + 1) * 128],
                                      hseg[ti % 2][:, i * 512:i * 512 + tw],
                                      start=(i == 0), stop=(i == NI - 1))
                    inc(mm, PE, 1)

        @blk.vector
        def _(v):
            wb(v, ctx)
            for ti, (t0, tw) in enumerate(tsubs):
                for d in range(NJ):
                    it = ti * NJ + d
                    pd = ctx.ps[4 + it % 2]
                    v.wait_ge(PE.h, b_PE + it + 1)
                    sp_wait(v, P_LD, b_LD, it)
                    if it >= 2:
                        sp_wait(v, P_ST, b_ST, it - 2)
                    inc(v.scalar_tensor_tensor(out=xo[it % 2][:, 0:tw], in0=pd[:, 0:tw],
                                               scalar=0.5, in1=ub[d % 2][:, 0:tw],
                                               op0=ALU.mult, op1=ALU.add), VE, 1)

        @blk.scalar
        def _(sc):
            wb(sc, ctx)
            for ti, (t0, tw) in enumerate(tsubs):
                for d in range(NJ):
                    it = ti * NJ + d
                    sc.wait_ge(VE.h, b_VE + it + 1)
                    sp_inc(sc.dma_start(out=dst[d * 128:(d + 1) * 128, t0:t0 + tw],
                                        in_=xo[it % 2][:, 0:tw]), P_ST, it)

        endbar(ctx, blk)


def phase_inproj(ctx):
    nc = ctx.nc
    tsubs = TSUBS_A
    NT = len(tsubs)
    P_W, P_ST, PE, SE = ctx.P_W, ctx.P_ST, ctx.PE, ctx.SE
    b_W, b_ST, b_PE, b_SE = P_W.base(), P_ST.base(), PE.v, SE.v

    with contextlib.ExitStack() as psx:
        wbuf = [psx.enter_context(nc.sbuf_tensor(f"ip_wb{i}", [128, NJ * 128], BF16)) for i in range(2)]
        ob = [psx.enter_context(nc.sbuf_tensor(f"ip_ob{i}", [128, W_A], BF16)) for i in range(2)]
        blk = psx.enter_context(nc.Block("inproj"))

        @blk.gpsimd
        def _(g):
            wb(g, ctx)
            for k in range(NO_IN):
                if k >= 2:
                    g.wait_ge(PE.h, b_PE + (k - 1) * NT)
                sp_inc(g.dma_start(out=wbuf[k % 2][:],
                                   in_=ctx.win[k * 128:(k + 1) * 128, :]), P_W, k)

        @blk.tensor
        def _(t):
            wb(t, ctx)
            for k in range(NO_IN):
                sp_wait(t, P_W, b_W, k)
                for ti, (t0, tw) in enumerate(tsubs):
                    it = k * NT + ti
                    pp = ctx.ps[4 + it % 2]
                    if it >= 2:
                        t.wait_ge(SE.h, b_SE + it - 1)
                    mm = None
                    for j in range(NJ):
                        mm = t.matmul(pp[:, 0:tw], wbuf[k % 2][:, j * 128:(j + 1) * 128],
                                      ctx.xln[:, j * W_A + t0:j * W_A + t0 + tw],
                                      start=(j == 0), stop=(j == NJ - 1))
                    inc(mm, PE, 1)

        @blk.scalar
        def _(sc):
            wb(sc, ctx)
            for k in range(NO_IN):
                for ti, (t0, tw) in enumerate(tsubs):
                    it = k * NT + ti
                    pp = ctx.ps[4 + it % 2]
                    sc.wait_ge(PE.h, b_PE + it + 1)
                    if k >= 2 and ti == 0 and k < 66:
                        sp_wait(sc, P_ST, b_ST, k - 2)
                    if k == 66:
                        inc(sc.activation(out=ctx.dtraw[:, t0:t0 + tw],
                                          in_=pp[0:64, 0:tw], func=AF.Copy), SE, 1)
                    else:
                        inc(sc.activation(out=ob[k % 2][:, t0:t0 + tw], in_=pp[:, 0:tw],
                                          func=AF.Copy), SE, 1)

        @blk.sync
        def _(s):
            wb(s, ctx)
            for k in range(66):
                s.wait_ge(SE.h, b_SE + (k + 1) * NT)
                if k < 32:
                    sp_inc(s.dma_start(out=ctx.z_scr[k * 128:(k + 1) * 128, :],
                                       in_=ob[k % 2][:, HALO:W_A]), P_ST, k)
                else:
                    kk = k - 32
                    sp_inc(s.dma_start(out=ctx.xbcp_scr[kk * 128:(kk + 1) * 128, :],
                                       in_=ob[k % 2][:]), P_ST, k)

        endbar(ctx, blk)
    ctx.xln_scope.close()
    ctx.xln = None


def phase_dt(ctx):
    nc = ctx.nc
    dC, PE, VE, SE = ctx.dC, ctx.PE, ctx.VE, ctx.SE
    b_dC, b_PE, b_VE, b_SE = dC.v, PE.v, VE.v, SE.v

    with contextlib.ExitStack() as psx:
        tmp = psx.enter_context(nc.sbuf_tensor("dt_tmp", [NHEADS, TOK], F32))
        segmask = psx.enter_context(nc.sbuf_tensor("dt_segmask", [NHEADS, TOK], F32))
        tmp2 = psx.enter_context(nc.sbuf_tensor("dt_tmp2", [NHEADS, TOK], F32))
        trow = psx.enter_context(nc.sbuf_tensor("dt_trow", [16, NHEADS], F32))
        trow2 = psx.enter_context(nc.sbuf_tensor("dt_trow2", [16, NHEADS], F32))
        trow1 = psx.enter_context(nc.sbuf_tensor("dt_trow1", [1, NHEADS], F32))
        blk = psx.enter_context(nc.Block("dt"))

        @blk.vector
        def _(v):
            wb(v, ctx)
            inc(v.tensor_scalar_add(out=tmp[:], in0=ctx.dtraw[:, HALO:W_A],
                                    scalar1=ctx.cols[:, 0:1]), VE, 1)
            v.wait_ge(SE.h, b_SE + 3)
            inc(v.scalar_tensor_tensor(out=tmp[:], in0=tmp[:], scalar=0.0,
                                       in1=tmp2[:], op0=ALU.max, op1=ALU.add), VE, 1)
            v.wait_ge(VE.h, b_VE + 2)
            inc(v.tensor_scalar_mul(out=ctx.Ab[:], in0=tmp[:],
                                    scalar1=ctx.cols[:, 1:2]), VE, 1)
            v.wait_ge(VE.h, b_VE + 3)
            v.wait_ge(dC.h, b_dC + 16)
            inc(v.tensor_tensor_scan(out=ctx.cumA[:], data0=segmask[:],
                                     data1=ctx.Ab[:], initial=0.0,
                                     op0=ALU.mult, op1=ALU.add), VE, 1)
            v.wait_ge(VE.h, b_VE + 4)
            inc(v.tensor_copy(out=ctx.tots[:].rearrange("h (c u) -> h c u", u=1),
                              in_=ctx.cumA[:].rearrange("h (c l) -> h c l", l=CHUNK)
                              [:, :, CHUNK - 1:CHUNK]), VE, 1)
            v.wait_ge(VE.h, b_VE + 5)
            v.tensor_tensor_scan(out=ctx.ptot[:, 1:NC + 1], data0=segmask[:, 0:NC],
                                 data1=ctx.tots[:], initial=0.0,
                                 op0=ALU.mult, op1=ALU.add)
            inc(v.tensor_copy(out=ctx.ptot[:, 0:1], in_=ctx.cols[:, 3:4]), VE, 1)
            v.wait_ge(PE.h, b_PE + 1)
            v.tensor_copy(out=trow[:], in_=ctx.ps[0][0:16, 0:NHEADS])
            v.tensor_copy(out=trow2[:], in_=ctx.ps[1][0:16, 0:NHEADS])
            inc(v.tensor_copy(out=trow1[:], in_=ctx.ps[2][0:1, 0:NHEADS]), VE, 1)

        @blk.scalar
        def _(sc):
            wb(sc, ctx)
            sc.wait_ge(VE.h, b_VE + 1)
            inc(sc.activation(out=tmp2[:], in_=tmp[:], func=AF.Abs), SE, 1)
            sc.wait_ge(SE.h, b_SE + 1)
            inc(sc.activation(out=tmp2[:], in_=tmp2[:], func=AF.Exp, scale=-1.0), SE, 1)
            sc.wait_ge(SE.h, b_SE + 2)
            inc(sc.activation(out=tmp2[:], in_=tmp2[:], func=AF.Ln, bias=1.0), SE, 1)

        @blk.tensor
        def _(t):
            wb(t, ctx)
            t.wait_ge(VE.h, b_VE + 6)
            t.transpose(ctx.ps[0][0:16, 0:NHEADS], ctx.tots[:], ctx.idf32[0:NHEADS, 0:NHEADS])
            t.transpose(ctx.ps[1][0:16, 0:NHEADS], ctx.ptot[:, 0:NC],
                        ctx.idf32[0:NHEADS, 0:NHEADS])
            inc(t.transpose(ctx.ps[2][0:1, 0:NHEADS], ctx.ptot[:, NC:NC + 1],
                            ctx.idf32[0:NHEADS, 0:NHEADS]), PE, 1)

        @blk.sync
        def _(s):
            wb(s, ctx)
            inc(s.dma_start(out=segmask[:], in_=ctx.segmask_c), dC, 16)
            s.wait_ge(VE.h, b_VE + 7)
            inc(s.dma_start(out=ctx.cums_scr, in_=ctx.cumA[:]), dC, 16)
            inc(s.dma_start(out=ctx.cums2_scr.rearrange("c (h l) -> h c l", l=CHUNK),
                            in_=ctx.cumA[:].rearrange("h (c l) -> h c l", l=CHUNK)),
                dC, 16)
            inc(s.dma_start(out=ctx.rows_scr[0:16, :], in_=trow[:]), dC, 16)
            inc(s.dma_start(out=ctx.rows_scr[16:32, :], in_=trow2[:]), dC, 16)
            inc(s.dma_start(out=ctx.pub_scr[0:1, 128 * D_INNER:PUBN], in_=trow1[:]),
                dC, 16)

        endbar(ctx, blk)


def phase_conv(ctx):
    nc = ctx.nc
    dC, P_LD, P_PS, VE, SE = ctx.dC, ctx.P_LD, ctx.P_PS, ctx.VE, ctx.SE
    b_dC, b_LD, b_PS, b_VE, b_SE = dC.v, P_LD.base(), P_PS.base(), VE.v, SE.v

    with contextlib.ExitStack() as psx:
        xin = [psx.enter_context(nc.sbuf_tensor(f"cv_xin{i}", [128, W_A], BF16)) for i in range(2)]
        acc = [psx.enter_context(nc.sbuf_tensor(f"cv_acc{i}", [128, TOK], F32)) for i in range(2)]
        outb = [psx.enter_context(nc.sbuf_tensor(f"cv_out{i}", [128, TOK], BF16)) for i in range(2)]
        wcv = psx.enter_context(nc.sbuf_tensor("cv_w", [128, NDC * 4], F32))
        bcv = psx.enter_context(nc.sbuf_tensor("cv_b", [128, NDC], F32))
        blk = psx.enter_context(nc.Block("conv"))

        @blk.sync
        def _(s):
            wb(s, ctx)
            inc(s.dma_start(out=wcv[:], in_=ctx.wconv), dC, 16)
            inc(s.dma_start(out=bcv[:], in_=ctx.bconv), dC, 16)
            for dc in range(NDC):
                if dc >= 2:
                    s.wait_ge(VE.h, b_VE + 4 * (dc - 1))
                sp_inc(s.dma_start(out=xin[dc % 2][:],
                                   in_=ctx.xbcp_scr[dc * 128:(dc + 1) * 128, :]), P_LD, dc)

        @blk.vector
        def _(v):
            wb(v, ctx)
            v.wait_ge(dC.h, b_dC + 32)
            for dc in range(NDC):
                bb = dc % 2
                sp_wait(v, P_LD, b_LD, dc)
                if dc >= 2:
                    v.wait_ge(SE.h, b_SE + dc - 1)
                inc(v.tensor_scalar_mul(out=acc[bb][:], in0=xin[bb][:, 0:TOK],
                                        scalar1=wcv[:, 4 * dc:4 * dc + 1]), VE, 1)
                for kk in range(1, 4):
                    v.wait_ge(VE.h, b_VE + 4 * dc + kk)
                    inc(v.scalar_tensor_tensor(out=acc[bb][:], in0=xin[bb][:, kk:kk + TOK],
                                               scalar=wcv[:, 4 * dc + kk:4 * dc + kk + 1],
                                               in1=acc[bb][:], op0=ALU.mult, op1=ALU.add),
                        VE, 1)

        @blk.scalar
        def _(sc):
            wb(sc, ctx)
            for dc in range(NDC):
                bb = dc % 2
                sc.wait_ge(VE.h, b_VE + 4 * (dc + 1))
                if dc >= 2:
                    sp_wait(sc, P_PS, b_PS, dc - 2)
                inc(sc.activation(out=outb[bb][:], in_=acc[bb][:], func=AF.Silu,
                                  bias=bcv[:, dc:dc + 1]), SE, 1)

        @blk.gpsimd
        def _(g):
            wb(g, ctx)
            for dc in range(NDC):
                g.wait_ge(SE.h, b_SE + dc + 1)
                sp_inc(g.dma_start(out=ctx.xbc_scr[dc * 128:(dc + 1) * 128, :],
                                   in_=outb[dc % 2][:]), P_PS, dc)

        endbar(ctx, blk)


def phase_d1(ctx):
    nc = ctx.nc
    P_LD, P_W, P_ST, P_PS, PE, VE, SE = (ctx.P_LD, ctx.P_W, ctx.P_ST, ctx.P_PS,
                                         ctx.PE, ctx.VE, ctx.SE)
    b_LD, b_W, b_ST, b_PS = P_LD.base(), P_W.base(), P_ST.base(), P_PS.base()
    b_PE, b_VE, b_SE = PE.v, VE.v, SE.v
    PE_PER = 2 + NHEADS
    VE_PER = 2 + NHEADS

    def ve_nrev(c):
        return b_VE + c * VE_PER + 1

    def ve_xsc(c):
        return b_VE + c * VE_PER + 2

    def ve_scan(c, h):
        return b_VE + c * VE_PER + 2 + (h + 1)

    def pe_tr(c):
        return b_PE + c * PE_PER + 1

    def pe_repl(c):
        return b_PE + c * PE_PER + 2

    def pe_mmc(c, h):
        return b_PE + c * PE_PER + 2 + (h + 1)

    with contextlib.ExitStack() as psx:
        xtm = [psx.enter_context(nc.sbuf_tensor(f"d1_xtm{i}", [128, D_INNER], BF16)) for i in range(2)]
        btm = [psx.enter_context(nc.sbuf_tensor(f"d1_btm{i}", [128, 128], BF16)) for i in range(2)]
        xsc = [psx.enter_context(nc.sbuf_tensor(f"d1_xsc{i}", [128, D_INNER], BF16)) for i in range(2)]
        sel = psx.enter_context(nc.sbuf_tensor("d1_sel", [128, D_INNER], F32))
        dect = [psx.enter_context(nc.sbuf_tensor(f"d1_dect{i}", [128, NHEADS], F32)) for i in range(2)]
        decc = [psx.enter_context(nc.sbuf_tensor(f"d1_decc{i}", [128, NHEADS], F32)) for i in range(2)]
        nrev = [psx.enter_context(nc.sbuf_tensor(f"d1_nrev{i}", [NHEADS, CHUNK], F32)) for i in range(2)]
        rowb = [psx.enter_context(nc.sbuf_tensor(f"d1_rowb{i}", [1, NHEADS], F32)) for i in range(2)]
        blk = psx.enter_context(nc.Block("d1"))

        @blk.sync
        def _(s):
            wb(s, ctx)
            for c in range(NC):
                if c >= 2:
                    s.wait_ge(PE.h, pe_mmc(c - 1, NHEADS - 1))
                    s.wait_ge(VE.h, ve_xsc(c - 1))
                sp_inc(s.dma_start_transpose(out=xtm[c % 2][:],
                                             in_=ctx.xbc_scr[0:D_INNER,
                                                             c * CHUNK:(c + 1) * CHUNK]),
                       P_LD, c)
                sp_inc(s.dma_start_transpose(out=btm[c % 2][:],
                                             in_=ctx.xbc_scr[D_INNER:D_INNER + 128,
                                                             c * CHUNK:(c + 1) * CHUNK]),
                       P_LD, c)

        @blk.gpsimd
        def _(g):
            wb(g, ctx)
            for c in range(NC):
                if c >= 2:
                    g.wait_ge(PE.h, pe_repl(c - 1))
                sp_inc(g.dma_start(out=rowb[c % 2][:], in_=ctx.rows_scr[c:c + 1, :]),
                       P_W, c)
                g.wait_ge(VE.h, ve_scan(c, NHEADS - 1))
                if c < NC - 1:
                    sp_inc(g.dma_start(out=ctx.sel_scr[(c + 1) * 128:(c + 2) * 128, :],
                                       in_=sel[:]), P_PS, c)

        @blk.vector
        def _(v):
            wb(v, ctx)
            for c in range(NC):
                cs = ctx.cumA[:, c * CHUNK:(c + 1) * CHUNK]
                if c >= 2:
                    v.wait_ge(PE.h, pe_tr(c - 1))
                inc(v.tensor_scalar(out=nrev[c % 2][:], in0=cs,
                                    scalar1=ctx.cumA[:, (c + 1) * CHUNK - 1:(c + 1) * CHUNK],
                                    scalar2=None, op0=ALU.subtract), VE, 1)
                v.wait_ge(SE.h, b_SE + 2 * c + 1)
                sp_wait(v, P_LD, b_LD, c, 32)
                if c >= 2:
                    v.wait_ge(PE.h, pe_mmc(c - 2, NHEADS - 1))
                inc(v.tensor_mul(out=xsc[c % 2][:].rearrange("p (h w) -> p h w", h=NHEADS),
                                 in0=xtm[c % 2][:].rearrange("p (h w) -> p h w", h=NHEADS),
                                 in1=dect[c % 2][:].rearrange("p (h w) -> p h w", w=1)
                                 .broadcast_to([128, NHEADS, HEADDIM])), VE, 1)
                v.wait_ge(SE.h, b_SE + 2 * c + 2)
                if c >= 1:
                    sp_wait(v, P_PS, b_PS, c - 1)
                for h in range(NHEADS):
                    pb = ctx.ps[4 + h % 2]
                    v.wait_ge(PE.h, pe_mmc(c, h))
                    if c == 0:
                        inc(v.tensor_copy(out=sel[:, h * 64:(h + 1) * 64],
                                          in_=pb[:, 0:64]), VE, 1)
                    else:
                        inc(v.scalar_tensor_tensor(out=sel[:, h * 64:(h + 1) * 64],
                                                   in0=sel[:, h * 64:(h + 1) * 64],
                                                   scalar=decc[c % 2][:, h:h + 1],
                                                   in1=pb[:, 0:64],
                                                   op0=ALU.mult, op1=ALU.add), VE, 1)

        @blk.scalar
        def _(sc):
            wb(sc, ctx)
            for c in range(NC):
                sc.wait_ge(PE.h, pe_tr(c))
                if c >= 2:
                    sc.wait_ge(VE.h, ve_xsc(c - 1))
                inc(sc.activation(out=dect[c % 2][:], in_=ctx.ps[2][:, 0:NHEADS],
                                  func=AF.Exp, scale=-1.0), SE, 1)
                sc.wait_ge(PE.h, pe_repl(c))
                if c >= 2:
                    sc.wait_ge(VE.h, ve_scan(c - 1, NHEADS - 1))
                inc(sc.activation(out=decc[c % 2][:], in_=ctx.ps[3][:, 0:NHEADS],
                                  func=AF.Exp), SE, 1)
                if c == NC - 1:
                    sc.wait_ge(VE.h, ve_scan(c, NHEADS - 1))
                    sp_inc(sc.dma_start(out=ctx.pub_scr[0:1, 0:128 * D_INNER]
                                        .rearrange("u (p f) -> (u p) f", p=128),
                                        in_=sel[:]), P_ST, c)

        @blk.tensor
        def _(t):
            wb(t, ctx)
            for c in range(NC):
                t.wait_ge(VE.h, ve_nrev(c))
                if c >= 1:
                    t.wait_ge(SE.h, b_SE + 2 * (c - 1) + 1)
                inc(t.transpose(ctx.ps[2][:, 0:NHEADS], nrev[c % 2][:],
                                ctx.idf32[0:NHEADS, 0:NHEADS]), PE, 1)
                sp_wait(t, P_W, b_W, c)
                if c >= 1:
                    t.wait_ge(SE.h, b_SE + 2 * (c - 1) + 2)
                inc(t.matmul(ctx.ps[3][:, 0:NHEADS], ctx.onesrow[:], rowb[c % 2][:],
                             start=True, stop=True), PE, 1)
                t.wait_ge(VE.h, ve_xsc(c))
                for h in range(NHEADS):
                    pb = ctx.ps[4 + h % 2]
                    if h >= 2:
                        t.wait_ge(VE.h, ve_scan(c, h - 2))
                    elif c >= 1:
                        t.wait_ge(VE.h, ve_scan(c - 1, NHEADS - 2 + h))
                    inc(t.matmul(pb[:, 0:64], btm[c % 2][:],
                                 xsc[c % 2][:, h * 64:(h + 1) * 64],
                                 start=True, stop=True), PE, 1)

        endbar(ctx, blk)


def phase_exchange(ctx):
    nc = ctx.nc
    P_W, P_X, P_ST, PE, VE, SE, CC = (ctx.P_W, ctx.P_X, ctx.P_ST,
                                      ctx.PE, ctx.VE, ctx.SE, ctx.CC)
    b_W, b_X, b_ST = P_W.base(), P_X.base(), P_ST.base()
    b_PE, b_VE, b_SE, b_CC = PE.v, VE.v, SE.v, CC.v

    ctx.S0_scope = contextlib.ExitStack()
    ctx.S0 = ctx.S0_scope.enter_context(nc.sbuf_tensor("S0", [128, D_INNER], F32))

    with contextlib.ExitStack() as psx:
        logd = psx.enter_context(nc.sbuf_tensor("ex_logd", [8, NHEADS], F32))
        btw = psx.enter_context(nc.sbuf_tensor("ex_btw", [8, 8], F32))
        prd = psx.enter_context(nc.sbuf_tensor("ex_prd", [8, 1], F32))
        coef = psx.enter_context(nc.sbuf_tensor("ex_coef", [8, NHEADS], F32))
        crow = [psx.enter_context(nc.sbuf_tensor(f"ex_crow{i}", [1, NHEADS], F32)) for i in range(2)]
        lbuf = [psx.enter_context(nc.sbuf_tensor(f"ex_lb{i}", [128, D_INNER], F32)) for i in range(2)]
        tmp = psx.enter_context(nc.sbuf_tensor("ex_tmp", [128, D_INNER], F32))
        blk = psx.enter_context(nc.Block("exchange"))

        @blk.gpsimd
        def _(g):
            wb(g, ctx)
            cc_i = g.collective_compute("AllGather", ALU.bypass,
                                        replica_groups=[list(range(8))],
                                        ins=[ctx.pub_scr.opt()],
                                        outs=[ctx.gat_scr.opt()])
            inc(cc_i, CC, 1)
            g.wait_ge(CC.h, b_CC + 1)
            inc(g.dma_start(out=logd[:],
                            in_=ctx.gat_scr[:, 128 * D_INNER:PUBN]), P_W.s[0], 16)
            inc(g.dma_start(out=btw[:], in_=ctx.btwT), P_W.s[0], 16)
            inc(g.dma_start(out=prd[:], in_=ctx.predc), P_W.s[0], 16)
            for j in range(8):
                if j == 0:
                    sp_wait(g, P_ST, b_ST, 0)
                if j >= 1:
                    g.wait_ge(PE.h, b_PE + 1 + j)
                inc(g.dma_start(out=crow[j % 2][:],
                                in_=ctx.rows_scr[32 + j:33 + j, :]), P_X.s[0], 16)
                if j >= 1:
                    g.wait_ge(VE.h, b_VE + 1 + 2 * j)
                inc(g.dma_start(out=lbuf[j % 2][:],
                                in_=ctx.gat_scr[j:j + 1, 0:128 * D_INNER]
                                .rearrange("u (p f) -> (u p) f", p=128)), P_X.s[1], 16)

        @blk.tensor
        def _(t):
            wb(t, ctx)
            t.wait_ge(P_W.s[0].h, b_W[0] + 48)
            inc(t.matmul(ctx.ps[0][0:8, 0:NHEADS], btw[:], logd[:], start=True,
                         stop=True), PE, 1)
            for j in range(8):
                t.wait_ge(P_X.s[0].h, b_X[0] + 16 * (j + 1))
                if j >= 1:
                    t.wait_ge(VE.h, b_VE + 1 + 2 * j)
                inc(t.matmul(ctx.ps[1][:, 0:NHEADS], ctx.onesrow[:], crow[j % 2][:],
                             start=True, stop=True), PE, 1)

        @blk.scalar
        def _(sc):
            wb(sc, ctx)
            sc.wait_ge(PE.h, b_PE + 1)
            inc(sc.activation(out=coef[:], in_=ctx.ps[0][0:8, 0:NHEADS], func=AF.Exp),
                SE, 1)

        @blk.vector
        def _(v):
            wb(v, ctx)
            v.wait_ge(SE.h, b_SE + 1)
            inc(v.tensor_scalar_mul(out=coef[:], in0=coef[:], scalar1=prd[:]), VE, 1)
            for j in range(8):
                v.wait_ge(PE.h, b_PE + 1 + (j + 1))
                v.wait_ge(P_X.s[1].h, b_X[1] + 16 * (j + 1))
                inc(v.tensor_mul(out=tmp[:].rearrange("p (h w) -> p h w", h=NHEADS),
                                 in0=lbuf[j % 2][:].rearrange("p (h w) -> p h w", h=NHEADS),
                                 in1=ctx.ps[1][:, 0:NHEADS]
                                 .rearrange("p (h w) -> p h w", w=1)
                                 .broadcast_to([128, NHEADS, HEADDIM])), VE, 1)
                v.wait_ge(VE.h, b_VE + 2 + 2 * j)
                if j == 0:
                    inc(v.tensor_copy(out=ctx.S0[:], in_=tmp[:]), VE, 1)
                else:
                    inc(v.tensor_add(out=ctx.S0[:], in0=ctx.S0[:], in1=tmp[:]), VE, 1)

        @blk.sync
        def _(s):
            wb(s, ctx)
            s.wait_ge(VE.h, b_VE + 1)
            sp_inc(s.dma_start(out=ctx.rows_scr[32:40, :], in_=coef[:]), P_ST, 0)

        endbar(ctx, blk)


def phase_d2(ctx):
    nc = ctx.nc
    P_LD, P_W, P_PS, PE, VE, SE, PO = (ctx.P_LD, ctx.P_W, ctx.P_PS,
                                       ctx.PE, ctx.VE, ctx.SE, ctx.PO)
    b_LD, b_W, b_PS = P_LD.base(), P_W.base(), P_PS.base()
    b_PE, b_VE, b_SE, b_PO = PE.v, VE.v, SE.v, PO.v
    NPAIRH = NHEADS // 2

    PE_PER = 3 + NHEADS + 2
    SE_PER = 3 + NHEADS + 3
    VE_PER = 4 + 8 + NHEADS + NPAIRH + 3 + NZ
    PO_PER = 1

    def pe_cbt(c):
        return b_PE + c * PE_PER + 1

    def pe_prec(c):
        return b_PE + c * PE_PER + 2

    def pe_cam(c):
        return b_PE + c * PE_PER + 3

    def pe_ymm(c, h):
        return b_PE + c * PE_PER + 3 + (h + 1)

    def pe_sq(c):
        return b_PE + c * PE_PER + 3 + NHEADS + 1

    def pe_rr(c):
        return b_PE + c * PE_PER + 3 + NHEADS + 2

    def se_sz(c):
        return b_SE + c * SE_PER + 1

    def se_ebc(c):
        return b_SE + c * SE_PER + 2

    def se_prec(c):
        return b_SE + c * SE_PER + 3

    def se_e0(c, h):
        return b_SE + c * SE_PER + 3 + (h + 1)

    def se_sq(c):
        return b_SE + c * SE_PER + 3 + NHEADS + 1

    def se_rstd(c):
        return b_SE + c * SE_PER + 3 + NHEADS + 3

    def ve_cam(c):
        return b_VE + c * VE_PER + 1

    def ve_cbt(c):
        return b_VE + c * VE_PER + 2

    def ve_mask(c):
        return b_VE + c * VE_PER + 3

    def ve_sv(c):
        return b_VE + c * VE_PER + 4

    def ve_stg(c, g):
        return b_VE + c * VE_PER + 4 + 9 * g + 1

    def ve_seg(c, h):
        g, j = divmod(h, 8)
        return b_VE + c * VE_PER + 4 + 9 * g + 2 + j

    def ve_yev(c, m):
        return b_VE + c * VE_PER + 76 + (m + 1)

    def ve_ygt(c):
        return b_VE + c * VE_PER + 109

    def ve_rr(c):
        return b_VE + c * VE_PER + 110

    def ve_rrs(c):
        return b_VE + c * VE_PER + 111

    def ve_end(c):
        return b_VE + (c + 1) * VE_PER

    def po_mask(c):
        return b_PO + c * PO_PER + 1

    with contextlib.ExitStack() as psx:
        bc = psx.enter_context(nc.sbuf_tensor("d2_bc", [128, NHEADS * CHUNK], F32))
        ebc = psx.enter_context(nc.sbuf_tensor("d2_ebc", [128, NHEADS * CHUNK], BF16))
        sva = psx.enter_context(nc.sbuf_tensor("d2_sva", [128, NHEADS * CHUNK], BF16))
        e0a = psx.enter_context(nc.sbuf_tensor("d2_e0a", [128, NHEADS * CHUNK], BF16))
        xtm = [psx.enter_context(nc.sbuf_tensor(f"d2_xtm{i}", [128, D_INNER], BF16)) for i in range(2)]
        xfm = [psx.enter_context(nc.sbuf_tensor(f"d2_xfm{i}", [128, NZ * CHUNK], BF16)) for i in range(2)]
        cfm = [psx.enter_context(nc.sbuf_tensor(f"d2_cfm{i}", [128, CHUNK], BF16)) for i in range(2)]
        bfm = [psx.enter_context(nc.sbuf_tensor(f"d2_bfm{i}", [128, CHUNK], BF16)) for i in range(2)]
        cbt = psx.enter_context(nc.sbuf_tensor("d2_cbt", [128, CHUNK], BF16))
        zt = psx.enter_context(nc.sbuf_tensor("d2_zt", [128, NZ * CHUNK], BF16))
        selb = [psx.enter_context(nc.sbuf_tensor(f"d2_selb{i}", [128, D_INNER], BF16)) for i in range(2)]
        seg = psx.enter_context(nc.sbuf_tensor("d2_seg", [128, D_INNER], BF16))
        prec = psx.enter_context(nc.sbuf_tensor("d2_prec", [128, NHEADS], F32))
        camt = psx.enter_context(nc.sbuf_tensor("d2_camt", [128, NHEADS], F32))
        yfm = psx.enter_context(nc.sbuf_tensor("d2_yfm", [128, NZ * CHUNK], BF16))
        ynb = psx.enter_context(nc.sbuf_tensor("d2_ynb", [128, NZ * CHUNK], BF16))
        rr = psx.enter_context(nc.sbuf_tensor("d2_rr", [1, CHUNK], F32))
        rrs = psx.enter_context(nc.sbuf_tensor("d2_rrs", [128, CHUNK], F32))
        rowp = psx.enter_context(nc.sbuf_tensor("d2_rowp", [1, NHEADS], F32))
        blk = psx.enter_context(nc.Block("d2"))

        def hb3(t2d, nh=NHEADS):
            return t2d[:].rearrange("p (h l) -> p h l", h=nh)

        @blk.sync
        def _(s):
            wb(s, ctx)
            for c in range(NC):
                if c >= 2:
                    s.wait_ge(PE.h, pe_rr(c - 2))
                    s.wait_ge(VE.h, ve_end(c - 2))
                sp_inc(s.dma_start_transpose(out=xtm[c % 2][:],
                                             in_=ctx.xbc_scr[0:D_INNER,
                                                             c * CHUNK:(c + 1) * CHUNK]),
                       P_LD, c)
                sp_inc(s.dma_start(out=cfm[c % 2][:],
                                   in_=ctx.xbc_scr[D_INNER + 128:D_INNER + 256,
                                                   c * CHUNK:(c + 1) * CHUNK]), P_LD, c)
                sp_inc(s.dma_start(out=bfm[c % 2][:],
                                   in_=ctx.xbc_scr[D_INNER:D_INNER + 128,
                                                   c * CHUNK:(c + 1) * CHUNK]), P_LD, c)
                sp_inc(s.dma_start(out=xfm[c % 2][:].rearrange("p (i t) -> p i t", i=NZ),
                                   in_=ctx.xbc_scr[0:D_INNER, c * CHUNK:(c + 1) * CHUNK]
                                   .rearrange("(i p) t -> p i t", p=128)), P_LD, c)
                cc_ld = max(c, 1)
                sp_inc(s.dma_start(out=selb[c % 2][:],
                                   in_=ctx.sel_scr[cc_ld * 128:(cc_ld + 1) * 128, :]),
                       P_LD, c)
                if c >= 1:
                    s.wait_ge(VE.h, ve_ygt(c - 1))
                sp_inc(s.dma_start(out=zt[:].rearrange("p (i t) -> p i t", i=NZ),
                                   in_=ctx.z_scr[:, c * CHUNK:(c + 1) * CHUNK]
                                   .rearrange("(i p) t -> p i t", p=128)), P_LD, c)

        @blk.gpsimd
        def _(g):
            wb(g, ctx)
            for c in range(NC):
                if c >= 1:
                    g.wait_ge(SE.h, se_e0(c - 1, NHEADS - 1))
                sp_inc(g.dma_start(out=bc[:],
                                   in_=ctx.cums2_scr[c:c + 1, :]
                                   .broadcast_to([128, NHEADS * CHUNK])), P_W, c)
                if c >= 1:
                    g.wait_ge(PE.h, pe_prec(c - 1))
                sp_inc(g.dma_start(out=rowp[:], in_=ctx.rows_scr[16 + c:17 + c, :]),
                       P_W, c)
                g.wait_ge(SE.h, se_ebc(c))
                inc(g.tensor_tensor(out=hb3(bc)[:, 32:64, :], in0=hb3(bc)[:, 32:64, :],
                                    in1=ctx.maskb[:].rearrange("s (u l) -> s u l", u=1)
                                    .broadcast_to([128, 32, CHUNK]), op=ALU.add), PO, 1)
                g.wait_ge(VE.h, ve_end(c))
                if c >= 1:
                    sp_wait(g, P_PS, b_PS, c - 1)
                sp_inc(g.dma_start(out=ctx.y_scr[:, c * CHUNK:(c + 1) * CHUNK]
                                   .rearrange("(i p) t -> p i t", p=128),
                                   in_=ynb[:].rearrange("p (i t) -> p i t", i=NZ)),
                       P_PS, c)

        @blk.scalar
        def _(sc):
            wb(sc, ctx)
            for c in range(NC):
                sp_wait(sc, P_LD, b_LD, c, 96)
                if c >= 1:
                    sc.wait_ge(VE.h, ve_ygt(c - 1))
                inc(sc.activation(out=zt[:], in_=zt[:], func=AF.Silu), SE, 1)
                sp_wait(sc, P_W, b_W, c, 32)
                if c >= 1:
                    sc.wait_ge(VE.h, ve_sv(c - 1))
                inc(sc.activation(out=ebc[:], in_=bc[:], func=AF.Exp), SE, 1)
                sc.wait_ge(PE.h, pe_prec(c))
                if c >= 1:
                    sc.wait_ge(VE.h, ve_seg(c - 1, NHEADS - 1))
                inc(sc.activation(out=prec[:], in_=ctx.ps[2][:, 0:NHEADS], func=AF.Exp),
                    SE, 1)
                sc.wait_ge(VE.h, ve_mask(c))
                sc.wait_ge(PO.h, po_mask(c))
                sc.wait_ge(VE.h, ve_cam(c))
                if c >= 1:
                    sc.wait_ge(PE.h, pe_ymm(c - 1, NHEADS - 1))
                for h in range(NHEADS):
                    inc(sc.activation(out=e0a[:, h * CHUNK:(h + 1) * CHUNK],
                                      in_=bc[:, h * CHUNK:(h + 1) * CHUNK],
                                      func=AF.Exp, bias=camt[:, h:h + 1]), SE, 1)
                sc.wait_ge(VE.h, ve_ygt(c))
                if c >= 1:
                    sp_wait(sc, P_PS, b_PS, c - 1)
                inc(sc.activation(out=ynb[:], in_=yfm[:], func=AF.Square), SE, 1)
                sc.wait_ge(VE.h, ve_rr(c))
                inc(sc.activation(out=rr[:], in_=rr[:], func=AF.Ln), SE, 1)
                sc.wait_ge(SE.h, b_SE + c * SE_PER + 3 + NHEADS + 2)
                inc(sc.activation(out=rr[:], in_=rr[:], func=AF.Exp, scale=-0.5), SE, 1)

        @blk.tensor
        def _(t):
            wb(t, ctx)
            for c in range(NC):
                sp_wait(t, P_LD, b_LD, c, 96)
                if c >= 1:
                    t.wait_ge(VE.h, ve_cbt(c - 1))
                inc(t.matmul(ctx.ps[0][:, 0:CHUNK], bfm[c % 2][:], cfm[c % 2][:],
                             start=True, stop=True), PE, 1)
                sp_wait(t, P_W, b_W, c, 32)
                if c >= 1:
                    t.wait_ge(SE.h, se_prec(c - 1))
                inc(t.matmul(ctx.ps[2][:, 0:NHEADS], ctx.onesrow[:], rowp[:],
                             start=True, stop=True), PE, 1)
                if c >= 1:
                    t.wait_ge(VE.h, ve_cam(c - 1))
                inc(t.transpose(ctx.ps[3][:, 0:NHEADS],
                                ctx.cumA[:, c * CHUNK:(c + 1) * CHUNK],
                                ctx.idf32[0:NHEADS, 0:NHEADS]), PE, 1)
                t.wait_ge(VE.h, ve_sv(c))
                for h in range(NHEADS):
                    m = h // 2
                    odd = h % 2
                    pb = ctx.ps[4 + m % 2]
                    po_r = pb[64:128, 0:CHUNK] if odd else pb[0:64, 0:CHUNK]
                    tp = (0, 64) if odd else None
                    t.wait_ge(VE.h, ve_stg(c, h // 8))
                    t.wait_ge(VE.h, ve_seg(c, h))
                    if not odd:
                        if m >= 2:
                            t.wait_ge(VE.h, ve_yev(c, m - 2))
                        elif c >= 1:
                            t.wait_ge(VE.h, ve_yev(c - 1, NPAIRH - 2 + m))
                    t.matmul(po_r, xtm[c % 2][:, h * 64:(h + 1) * 64],
                             e0a[:, h * CHUNK:(h + 1) * CHUNK],
                             start=True, stop=False, tile_position=tp)
                    inc(t.matmul(po_r, seg[:, h * 64:(h + 1) * 64],
                                 sva[:, h * CHUNK:(h + 1) * CHUNK],
                                 start=False, stop=True, tile_position=tp), PE, 1)
                t.wait_ge(SE.h, se_sq(c))
                if c >= 1:
                    t.wait_ge(VE.h, ve_rr(c - 1))
                mm = None
                for j in range(NZ):
                    mm = t.matmul(ctx.ps[6][0:1, 0:CHUNK], ctx.onescolbf[:],
                                  ynb[:, j * CHUNK:(j + 1) * CHUNK],
                                  start=(j == 0), stop=(j == NZ - 1))
                inc(mm, PE, 1)
                t.wait_ge(SE.h, se_rstd(c))
                if c >= 1:
                    t.wait_ge(VE.h, ve_rrs(c - 1))
                inc(t.matmul(ctx.ps[7][:, 0:CHUNK], ctx.onesrow[:], rr[:],
                             start=True, stop=True), PE, 1)

        @blk.vector
        def _(v):
            wb(v, ctx)
            for c in range(NC):
                v.wait_ge(PE.h, pe_cam(c))
                if c >= 1:
                    v.wait_ge(SE.h, se_e0(c - 1, NHEADS - 1))
                inc(v.tensor_scalar_mul(out=camt[:], in0=ctx.ps[3][:, 0:NHEADS],
                                        scalar1=-1.0), VE, 1)
                inc(v.tensor_copy(out=cbt[:], in_=ctx.ps[0][:, 0:CHUNK]), VE, 1)
                v.wait_ge(SE.h, se_ebc(c))
                inc(v.tensor_tensor(out=hb3(bc)[:, 0:32, :], in0=hb3(bc)[:, 0:32, :],
                                    in1=ctx.maskb[:].rearrange("s (u l) -> s u l", u=1)
                                    .broadcast_to([128, 32, CHUNK]), op=ALU.add), VE, 1)
                if c >= 1:
                    v.wait_ge(PE.h, pe_ymm(c - 1, NHEADS - 1))
                inc(v.tensor_mul(out=hb3(sva), in0=hb3(ebc),
                                 in1=cfm[c % 2][:].rearrange("p (u l) -> p u l", u=1)
                                 .broadcast_to([128, NHEADS, CHUNK])), VE, 1)
                v.wait_ge(SE.h, se_prec(c))
                sp_wait(v, P_LD, b_LD, c, 96)
                if c >= 1:
                    v.wait_ge(PO.h, b_PO + c * PO_PER)
                for g in range(8):
                    v.wait_ge(SE.h, se_e0(c, 8 * g + 7))
                    inc(v.tensor_mul(out=hb3(e0a)[:, 8 * g:8 * g + 8, :],
                                     in0=hb3(e0a)[:, 8 * g:8 * g + 8, :],
                                     in1=cbt[:].rearrange("p (u l) -> p u l", u=1)
                                     .broadcast_to([128, 8, CHUNK])), VE, 1)
                    for j in range(8):
                        h = 8 * g + j
                        if c == 0:
                            inc(v.tensor_scalar_mul(out=seg[:, h * 64:(h + 1) * 64],
                                                    in0=ctx.S0[:, h * 64:(h + 1) * 64],
                                                    scalar1=prec[:, h:h + 1]), VE, 1)
                        else:
                            inc(v.scalar_tensor_tensor(out=seg[:, h * 64:(h + 1) * 64],
                                                       in0=ctx.S0[:, h * 64:(h + 1) * 64],
                                                       scalar=prec[:, h:h + 1],
                                                       in1=selb[c % 2][:, h * 64:(h + 1) * 64],
                                                       op0=ALU.mult, op1=ALU.add), VE, 1)
                if c >= 1:
                    v.wait_ge(SE.h, se_sq(c - 1))
                for m in range(NPAIRH):
                    pb = ctx.ps[4 + m % 2]
                    v.wait_ge(PE.h, pe_ymm(c, 2 * m + 1))
                    inc(v.scalar_tensor_tensor(
                        out=yfm[:, m * CHUNK:(m + 1) * CHUNK],
                        in0=xfm[c % 2][:, m * CHUNK:(m + 1) * CHUNK],
                        scalar=ctx.dp_pair[:, m:m + 1], in1=pb[:, 0:CHUNK],
                        op0=ALU.mult, op1=ALU.add), VE, 1)
                v.wait_ge(SE.h, se_sz(c))
                v.wait_ge(VE.h, ve_yev(c, NPAIRH - 1))
                inc(v.tensor_mul(out=yfm[:], in0=yfm[:], in1=zt[:]), VE, 1)
                v.wait_ge(PE.h, pe_sq(c))
                inc(v.tensor_scalar(out=rr[:], in0=ctx.ps[6][0:1, 0:CHUNK],
                                    scalar1=1.0 / D_INNER, scalar2=1e-5,
                                    op0=ALU.mult, op1=ALU.add), VE, 1)
                v.wait_ge(PE.h, pe_rr(c))
                inc(v.tensor_copy(out=rrs[:], in_=ctx.ps[7][:, 0:CHUNK]), VE, 1)
                v.wait_ge(VE.h, ve_rrs(c))
                for j in range(NZ):
                    inc(v.scalar_tensor_tensor(out=ynb[:, j * CHUNK:(j + 1) * CHUNK],
                                               in0=yfm[:, j * CHUNK:(j + 1) * CHUNK],
                                               scalar=ctx.nw[:, j:j + 1], in1=rrs[:],
                                               op0=ALU.mult, op1=ALU.mult), VE, 1)

        endbar(ctx, blk)
    ctx.S0_scope.close()


def phase_outproj(ctx):
    nc = ctx.nc
    dH, P_LD, P_W, P_ST, PE, VE = ctx.dH, ctx.P_LD, ctx.P_W, ctx.P_ST, ctx.PE, ctx.VE
    b_dH, b_LD, b_W, b_ST, b_PE, b_VE = (dH.v, P_LD.base(), P_W.base(), P_ST.base(),
                                         PE.v, VE.v)
    NSEG = 2
    SW = 1024

    with contextlib.ExitStack() as psx:
        yseg = psx.enter_context(nc.sbuf_tensor("op_yseg", [128, NZ * SW], BF16))
        wo = [psx.enter_context(nc.sbuf_tensor(f"op_wo{i}", [128, NZ * 128], BF16)) for i in range(2)]
        x1b = [psx.enter_context(nc.sbuf_tensor(f"op_x1{i}", [128, 512], F32)) for i in range(2)]
        xob = [psx.enter_context(nc.sbuf_tensor(f"op_xo{i}", [128, 512], F32)) for i in range(2)]
        blk = psx.enter_context(nc.Block("outproj"))

        @blk.sync
        def _(s):
            wb(s, ctx)
            for sg in range(NSEG):
                if sg >= 1:
                    s.wait_ge(PE.h, b_PE + sg * 32)
                inc(s.dma_start(out=yseg[:].rearrange("p (i t) -> p i t", i=NZ),
                                in_=ctx.y_scr[:, sg * SW:(sg + 1) * SW]
                                .rearrange("(i p) t -> p i t", p=128)), dH, 16)
                for d in range(NJ):
                    for ts2 in range(2):
                        it = sg * 32 + d * 2 + ts2
                        t0 = sg * SW + ts2 * 512
                        if it >= 2:
                            s.wait_ge(VE.h, b_VE + it - 1)
                        sp_inc(s.dma_start(out=x1b[it % 2][:],
                                           in_=ctx.x1_scr[d * 128:(d + 1) * 128,
                                                          HALO + t0:HALO + t0 + 512]),
                               P_LD, it)

        @blk.gpsimd
        def _(g):
            wb(g, ctx)
            for sg in range(NSEG):
                for d in range(NJ):
                    kd = sg * NJ + d
                    if kd >= 2:
                        g.wait_ge(PE.h, b_PE + (kd - 1) * 2)
                    sp_inc(g.dma_start(out=wo[d % 2][:],
                                       in_=ctx.wout[d * 128:(d + 1) * 128, :]), P_W, kd)

        @blk.tensor
        def _(t):
            wb(t, ctx)
            for sg in range(NSEG):
                t.wait_ge(dH.h, b_dH + 16 * (sg + 1))
                for d in range(NJ):
                    kd = sg * NJ + d
                    sp_wait(t, P_W, b_W, kd)
                    for ts2 in range(2):
                        it = sg * 32 + d * 2 + ts2
                        pp = ctx.ps[4 + it % 2]
                        if it >= 2:
                            t.wait_ge(VE.h, b_VE + it - 1)
                        mm = None
                        for i in range(NZ):
                            mm = t.matmul(pp[:], wo[d % 2][:, i * 128:(i + 1) * 128],
                                          yseg[:, i * SW + ts2 * 512:i * SW + ts2 * 512 + 512],
                                          start=(i == 0), stop=(i == NZ - 1))
                        inc(mm, PE, 1)

        @blk.vector
        def _(v):
            wb(v, ctx)
            for sg in range(NSEG):
                for d in range(NJ):
                    for ts2 in range(2):
                        it = sg * 32 + d * 2 + ts2
                        pp = ctx.ps[4 + it % 2]
                        v.wait_ge(PE.h, b_PE + it + 1)
                        sp_wait(v, P_LD, b_LD, it)
                        if it >= 2:
                            sp_wait(v, P_ST, b_ST, it - 2)
                        inc(v.tensor_add(out=xob[it % 2][:], in0=pp[:],
                                         in1=x1b[it % 2][:]), VE, 1)

        @blk.scalar
        def _(sc):
            wb(sc, ctx)
            for sg in range(NSEG):
                for d in range(NJ):
                    for ts2 in range(2):
                        it = sg * 32 + d * 2 + ts2
                        t0 = sg * SW + ts2 * 512
                        sc.wait_ge(VE.h, b_VE + it + 1)
                        sp_inc(sc.dma_start(out=ctx.x2_scr[d * 128:(d + 1) * 128,
                                                           t0:t0 + 512],
                                            in_=xob[it % 2][:]), P_ST, it)

        endbar(ctx, blk)


def phase_dbg(ctx):
    nc = ctx.nc
    dL = ctx.dC
    b_dL = dL.v
    outs = []
    for nm in [d for d in DBG if d != "xln"]:
        src = getattr(ctx, nm)
        outs.append((nc.dram_tensor("dbg_" + nm, list(src.shape), src.dtype,
                                    kind="ExternalOutput").ap(), src))
    with nc.Block("dbg") as blk:
        @blk.sync
        def _(s):
            wb(s, ctx)
            n = 0
            for dst, src in outs:
                inc(s.dma_start(out=dst, in_=src), dL, 16)
                n += 1
            s.wait_ge(dL.h, b_dL + 16 * n)


_NC_CACHE = {}


def _get_nc():
    key = (tuple(DBG), STOP_AFTER)
    if key not in _NC_CACHE:
        _NC_CACHE[key] = build_program()
    return _NC_CACHE[key]


def kernel(L0_attn_u, L0_attn_W_in, L0_attn_W_conv, L0_attn_b_conv,
           L0_attn_dt_bias, L0_attn_A_log, L0_attn_Dp, L0_attn_norm_w,
           L0_attn_W_out, L0_ffn_w_gate_up, L0_ffn_w_down):
    f32 = np.float32
    u = np.asarray(L0_attn_u, f32)
    ut = np.ascontiguousarray(u.transpose(0, 2, 1))
    wgu_t = np.asarray(L0_ffn_w_gate_up, f32).T
    wgu = np.ascontiguousarray(
        wgu_t.reshape(16, 128, 128, 128).transpose(2, 1, 0, 3).reshape(16384, 2048))
    wdn_t = np.asarray(L0_ffn_w_down, f32).T
    wdn = np.ascontiguousarray(
        wdn_t.reshape(64, 128, 16, 128).transpose(2, 1, 0, 3).reshape(2048, 8192))
    win_t = np.asarray(L0_attn_W_in, f32).T
    win_p = np.zeros((2048, NO_IN * 128), f32)
    win_p[:, 0:8512] = win_t
    win = np.ascontiguousarray(
        win_p.reshape(16, 128, NO_IN, 128).transpose(2, 1, 0, 3)
        .reshape(NO_IN * 128, 2048))
    wout_t = np.asarray(L0_attn_W_out, f32).T
    wout = np.ascontiguousarray(
        wout_t.reshape(32, 128, 16, 128).transpose(2, 1, 0, 3).reshape(2048, 4096))
    wconv = np.ascontiguousarray(
        np.asarray(L0_attn_W_conv, f32).reshape(NDC, 128, 4).transpose(1, 0, 2)
        .reshape(128, NDC * 4))
    bconv = np.ascontiguousarray(np.asarray(L0_attn_b_conv, f32).reshape(NDC, 128).T)
    dtb = np.asarray(L0_attn_dt_bias, f32).reshape(NHEADS, 1)
    alog = np.asarray(L0_attn_A_log, f32).reshape(NHEADS, 1)
    dprow = np.asarray(L0_attn_Dp, f32).reshape(1, NHEADS)
    nwin = np.ascontiguousarray(np.asarray(L0_attn_norm_w, f32).reshape(NZ, 128).T)

    in_maps = []
    for core in range(8):
        b, q = divmod(core, 4)
        u_pad = np.zeros((D_MODEL, W_A), f32)
        t0 = q * TOK
        lo = max(0, t0 - HALO)
        u_pad[:, HALO - (t0 - lo):] = ut[b][:, lo:t0 + TOK]
        btwT = np.zeros((8, 8), f32)
        predc = np.zeros((8, 1), f32)
        for j in range(8):
            if j // 4 == b and j % 4 < q:
                predc[j, 0] = 1.0
                for i in range(8):
                    if i // 4 == b and (j % 4) < (i % 4) < q:
                        btwT[i, j] = 1.0
        in_maps.append({
            "u": np.ascontiguousarray(u_pad), "wgu": wgu, "wdn": wdn, "win": win,
            "wout": wout, "wconv": wconv, "bconv": bconv, "dtb": dtb, "alog": alog,
            "dprow": dprow, "nwin": nwin, "btwT": btwT, "predc": predc,
        })

    nc = _get_nc()
    res = run_bass_kernel_spmd(nc, in_maps, core_ids=list(range(8)))
    out = np.empty((2, 8192, 2048), f32)
    for core in range(8):
        b, q = divmod(core, 4)
        out[b, q * TOK:(q + 1) * TOK, :] = np.asarray(res.results[core]["o"]).T
    if DBG:
        kernel._dbg = [{("dbg_" + nm): np.asarray(res.results[c]["dbg_" + nm])
                        for nm in DBG} for c in range(8)]
    return out
